# version 01
# baseline (speedup 1.0000x reference)
import numpy as np
import ml_dtypes
import concourse.bass as bass
import concourse.mybir as mybir
from concourse.bass_utils import run_bass_kernel_spmd

F32 = mybir.dt.float32
BF16 = mybir.dt.bfloat16
U8 = mybir.dt.uint8
AF = mybir.ActivationFunctionType
ALU = mybir.AluOpType
BFdt = ml_dtypes.bfloat16

D = 1024
NCH = 8
NT = 2048
SL = 128
NSL = NT // SL
ZSL = 512
NZS = NT // ZSL
T = 4096
HL = 8
CPT = 12.0
CBIG = 24.0
CG = 8.0
SQB = 20.0


class Sem:
    def __init__(self, nc, name):
        self.h = nc.alloc_semaphore(name=name)
        self.n = 0

    def bump(self, k=1):
        self.n += k

    def now(self):
        return (self, self.n)


class Sched:
    def __init__(self):
        self.ops = []

    def op(self, eng, fn, waits=(), inc=None, inck=1):
        w = []
        for x in waits:
            if x is None:
                continue
            if isinstance(x, list):
                w.extend(y for y in x if y is not None)
            else:
                w.append(x)
        self.ops.append((eng, fn, w, (inc, inck) if inc is not None else None))
        if inc is not None:
            inc.bump(inck)
            return inc.now()
        return None

    def replay(self, eng_name, handle):
        for eng, fn, waits, post in self.ops:
            if eng != eng_name:
                continue
            seen = {}
            for (sem, val) in waits:
                if val > 0 and seen.get(sem, -1) < val:
                    seen[sem] = val
            for sem, val in seen.items():
                handle.wait_ge(sem.h, val)
            inst = fn(handle)
            if post is not None:
                sem, k = post
                if inst is None:
                    inst = handle.nop()
                inst.then_inc(sem.h, k)


def build(dbg=(), parts=3, nocrow=False):
    nc = bass.Bass("TRN2", target_bir_lowering=False)

    def din(name, shape, dtype=F32):
        return nc.dram_tensor(name, list(shape), dtype, kind="ExternalInput").ap()

    xt_in = din("xt", (D, NT))
    wqd_in = din("wqd", (D, 1088))
    wup_in = din("wup", (64, 2048))
    wg_in = din("wg", (D, 8))
    wo_in = din("wo", (D, D))
    bo_in = din("bo", (1, NCH, 128))
    we_in = din("we", (8, D, D))
    wint_in = din("wint", (D, 1536))
    wout_in = din("wout", (D, D))
    g1_in = din("g1c", (128, 8))
    g2_in = din("g2c", (128, 8))
    bq_in = din("bqc", (128, 8))
    bd_in = din("bdc", (64, 1))
    bup_in = din("bupc", (128, 16))
    bg_in = din("bgc", (8, 1))
    ridx_in = din("ridx", (1, 1), mybir.dt.int32)
    mrq_in = din("mrq", (9, 16 * SL))
    mrk_in = din("mrk", (9, 16 * SL))

    out_d = nc.dram_tensor("out", [D, NT], F32, kind="ExternalOutput").ap()
    dbg_d = {}
    for name, shape in dbg:
        dbg_d[name] = nc.dram_tensor("dbg_" + name, list(shape), F32,
                                     kind="ExternalOutput").ap()

    x_page = nc.dram_tensor("x_page", [128, NCH, NT], F32).ap()
    k_dram = nc.dram_tensor("k_dram", [512, T], BF16).ap()
    v_dram = nc.dram_tensor("v_dram", [HL, T, 128], BF16).ap()
    ln2own_d = nc.dram_tensor("ln2own", [D, NT], BF16).ap()
    ln2all_d = nc.dram_tensor("ln2all", [2 * D, NT], BF16).ap()
    attnown_d = nc.dram_tensor("attnown", [2, 512, NT], BF16).ap()
    attnall_d = nc.dram_tensor("attnall", [2, 2, 512, NT], BF16).ap()

    xar = nc.alloc_sbuf_tensor("xar", [128, NCH, NT], F32)
    h2ar = nc.alloc_sbuf_tensor("h2ar", [128, NCH, NT], BF16)
    wqdb = nc.alloc_sbuf_tensor("wqdb", [128, NCH, 1088], BF16)
    wupb = nc.alloc_sbuf_tensor("wupb", [64, 2048], BF16)
    wgb = nc.alloc_sbuf_tensor("wgb", [128, NCH, 8], BF16)
    woextb = nc.alloc_sbuf_tensor("woextb", [2, NCH, 128], BF16)
    rowsA = nc.alloc_sbuf_tensor("rowsA", [1, 2, NT], F32)
    mrlnb = nc.alloc_sbuf_tensor("mrlnb", [1, NT], BF16)
    g1t = nc.alloc_sbuf_tensor("g1t", [128, 8], F32)
    g2t = nc.alloc_sbuf_tensor("g2t", [128, 8], F32)
    bqt = nc.alloc_sbuf_tensor("bqt", [128, 8], F32)
    bdt = nc.alloc_sbuf_tensor("bdt", [64, 1], F32)
    bupt = nc.alloc_sbuf_tensor("bupt", [128, 16], F32)
    bgst = nc.alloc_sbuf_tensor("bgst", [8, 1], F32)
    cgt = nc.alloc_sbuf_tensor("cgt", [8, 1], F32)
    ridxt = nc.alloc_sbuf_tensor("ridxt", [1, 1], mybir.dt.int32)
    onesF = nc.alloc_sbuf_tensor("onesF", [128, 1], F32)
    onesB = nc.alloc_sbuf_tensor("onesB", [128, 1], BF16)
    ones1B = nc.alloc_sbuf_tensor("ones1B", [1, 128], BF16)
    ones1F = nc.alloc_sbuf_tensor("ones1F", [1, 128], F32)
    ones8F = nc.alloc_sbuf_tensor("ones8F", [8, 64], F32)
    consts = nc.alloc_sbuf_tensor("consts", [128, 8], F32)
    crow2 = nc.alloc_sbuf_tensor("crow2", [1, 4], F32)
    identB = nc.alloc_sbuf_tensor("identB", [64, 64], BF16)
    selB = nc.alloc_sbuf_tensor("selB", [8, 8, 128], BF16)
    scratch = nc.alloc_sbuf_tensor("scratch", [128, 65 * 1024], U8)

    PS = nc.alloc_psum_tensor("PS", [128, 8, 512], F32)

    def carve(off, shape, dtype):
        esz = 2 if dtype == BF16 else 4
        n = int(np.prod(shape[1:])) * esz
        ap = scratch[:, off:off + n].bitcast(dtype)
        if len(shape) == 3:
            ap = ap.rearrange("p (a b) -> p a b", a=shape[1])
        elif len(shape) == 4:
            ap = ap.rearrange("p (a b c) -> p a b c", a=shape[1], b=shape[2])
        elif len(shape) == 5:
            ap = ap.rearrange("p (a b c d) -> p a b c d", a=shape[1], b=shape[2],
                              c=shape[3])
        if shape[0] != 128:
            ap = ap[0:shape[0]]
        return ap

    A_xm = carve(0, (128, NCH, SL), F32)
    A_ht = carve(4096, (128, NCH, SL), BF16)
    A_q = carve(6144, (128, NCH, SL), BF16)
    A_dv = carve(8192, (64, SL), BF16)
    A_kt = carve(8448, (128, NCH, SL), BF16)
    A_vt = carve(10496, (128, NCH, SL), BF16)
    A_at = carve(12544, (128, NCH, SL), BF16)
    A_cr = carve(14592, (2, SL), BF16)
    A_tmp = carve(14848, (128, SL), F32)
    A_wo = carve(15360, (128, NCH, D), BF16)
    Qp = carve(31744, (73, 16 * SL), BF16)
    Kp = carve(35840, (73, 16 * SL), BF16)
    Vp = carve(39936, (64, 16 * SL), BF16)
    TEMP = carve(44032, (64, 16 * SL), BF16)
    A_sq = carve(48128, (128, 2, 1024), F32)
    PtT = carve(56320, (128, 2, 512), BF16)
    Vg = carve(58368, (128, 4, 128), BF16)
    Rt = carve(59392, (64, 512), F32)
    r2row = carve(61440, (1, SL), F32)
    nmb = carve(61952, (128, 1088), BF16)
    nmrQ = carve(64128, (1, 1088), BF16)
    rmrow = carve(0, (1, NT), F32)
    rtrow = carve(8192, (1, NT), F32)
    B_we = carve(0, (128, 2, NCH, D), BF16)
    B_z = carve(32768, (128, 2, NCH, ZSL), BF16)
    B_gall = carve(49152, (128, 8, ZSL), BF16)
    B_ge = carve(57344, (8, ZSL), F32)
    B_gt = carve(59392, (8, ZSL), BF16)
    RtB = carve(60416, (64, ZSL), F32)
    AT_lnp = carve(0, (128, 2, NCH, 512), BF16)
    AT_q = carve(16384, (128, 4, T), BF16)
    AT_kb = carve(49152, (128, 2, T), BF16)
    AT_vb = carve(0, (128, 2, 2, HL, 128), BF16)
    AT_ws = carve(8192, (128, 2, NCH, 128), BF16)
    AT_wm = carve(12288, (128, 128), BF16)
    AT_pr = carve(12544, (1, 512), BF16)
    VW = carve(49152, (128, NCH, 512), BF16)
    AT_vs = carve(0, (128, 2, 32, 128), BF16)
    KS = carve(49152, (128, T), BF16)
    BPt = carve(57344, (128, 2, 1024), BF16)
    RtA = carve(61440, (64, 512), F32)
    AT_ah = carve(63488, (64, 2, 512), BF16)
    AT_wout = carve(16384, (128, NCH, D), BF16)

    ln2full = xar[:, :, :].bitcast(BF16)
    at_slab = h2ar

    SM = Sem(nc, "SM")
    SV = Sem(nc, "SV")
    SS = Sem(nc, "SS")
    SG = Sem(nc, "SG")
    DL = Sem(nc, "DL")
    DS = Sem(nc, "DS")
    DW = Sem(nc, "DW")
    DK = Sem(nc, "DK")
    CC = Sem(nc, "CC")
    ESEM = {"v": SV, "s": SS, "g": SG}
    CPOP = {"v": "tensor_copy", "s": "copy", "g": "tensor_copy"}

    sch = Sched()

    def mm(out, lhsT, rhs, start, stop, waits=(), inc=None, tp=None):
        return sch.op("t", lambda e, o=out, l=lhsT, r=rhs, a=start, b=stop, t_=tp:
                      e.matmul(o, l, r, start=a, stop=b, tile_position=t_),
                      waits, inc)

    def transp(out, in_, ident, waits=(), inc=None):
        return sch.op("t", lambda e, o=out, i=in_, d=ident: e.transpose(o, i, d),
                      waits, inc)

    def act(out, in_, func, bias=0.0, scale=1.0, accum=None, waits=(), inc=None):
        return sch.op("s", lambda e, o=out, i=in_, f=func, b=bias, s=scale, a=accum:
                      e.activation(o, i, f, bias=b, scale=s, accum_out=a),
                      waits, inc)

    def vop(eng, name, *args, waits=(), inc=None, **kw):
        def fn(e, n=name, a=args, k=kw):
            return getattr(e, n)(*a, **k)
        return sch.op(eng, fn, waits, inc)

    def dma(eng, out, in_, waits=(), inc=None):
        return sch.op(eng, lambda e, o=out, i=in_: e.dma_start(out=o, in_=i),
                      waits, inc, inck=16)

    def barrier():
        snaps = [SM.now(), SV.now(), SS.now(), SG.now(), DL.now(), DS.now(),
                 DW.now(), DK.now()]
        outs = []
        outs.append(sch.op("t", lambda e: e.nop(), snaps, SM))
        outs.append(sch.op("v", lambda e: e.nop(), snaps, SV))
        outs.append(sch.op("s", lambda e: e.nop(), snaps, SS))
        outs.append(sch.op("g", lambda e: e.nop(), snaps, SG))
        return outs

    def dump(name, ap, waits):
        if name in dbg_d:
            return dma("g", dbg_d[name], ap, waits=waits, inc=DS)
        return None

    x_ld = dma("y", xar[:, :, :], xt_in.rearrange("(c p) n -> p c n", p=128), inc=DL)
    wqd_ld = dma("g", wqdb[:, :, :], wqd_in.rearrange("(c p) n -> p c n", p=128), inc=DW)
    wup_ld = dma("g", wupb[0:64, :], wup_in, inc=DW)
    wg_ld = dma("g", wgb[:, :, :], wg_in.rearrange("(c p) n -> p c n", p=128), inc=DW)

    for dst, src, p in ((g1t, g1_in, 128), (g2t, g2_in, 128), (bqt, bq_in, 128),
                        (bdt, bd_in, 64), (bupt, bup_in, 128), (bgst, bg_in, 8)):
        dma("y", dst[0:p, :], src, inc=DL)
    ridx_ld = dma("y", ridxt[0:1, :], ridx_in, inc=DL)
    w_small = DL.now()

    def ms(ap, val, waits=(), inc=None):
        return vop("g", "memset", ap, val, waits=waits, inc=inc)

    ms(onesF[:, :], 1.0)
    ms(onesB[:, :], 1.0)
    ms(ones1B[0:1, :], 1.0)
    ms(ones1F[0:1, :], 1.0)
    ms(ones8F[0:8, :], 1.0)
    for col, val in ((0, 0.125), (1, -CPT), (2, -CBIG), (3, 0.5), (4, 1.0 / D),
                     (5, 1e-5), (6, 1e-6), (7, 0.0)):
        ms(consts[:, col:col + 1], val)
    ms(crow2[0:1, 0:1], -1e-5)
    ms(crow2[0:1, 1:2], 1.0 + 1e-6)
    ms(crow2[0:1, 2:3], 0.0)
    ms(crow2[0:1, 3:4], -0.5)
    ms(cgt[0:8, :], -CG)
    wx_ms = ms(woextb[0:2, :, :], -1.0)
    bo_ld = dma("g", woextb[1:2, :, :], bo_in, waits=[wx_ms], inc=DW)
    ms(identB[0:64, :], 0.0)
    vop("g", "affine_select", identB[0:64, :], identB[0:64, :],
        pattern=[[-1, 64]], compare_op=ALU.not_equal, fill=1.0, base=0,
        channel_multiplier=1)
    mr_ld1 = dma("g", Qp[64:73, :], mrq_in, inc=DW)
    mr_ld2 = dma("g", Kp[64:73, :], mrk_in, inc=DW)
    ms(Vg[:, :, :], 1.0)
    ms(selB[0:8, :, :], 0.0)
    vop("g", "affine_select", selB[0:8, :, :].rearrange("p a b -> p (a b)"),
        selB[0:8, :, :].rearrange("p a b -> p (a b)"),
        pattern=[[-1, 8], [0, 128]], compare_op=ALU.not_equal, fill=1.0, base=0,
        channel_multiplier=1)
    w_consts = sch.op("g", lambda e: e.nop(), [], SG)

    bg_sh = vop("v", "tensor_tensor", bgst[0:8, :], bgst[0:8, :], cgt[0:8, :],
                ALU.add, waits=[w_small, w_consts], inc=SV)

    wv = None
    for c in range(NCH):
        wv = vop("v", "tensor_scalar", wqdb[:, c, :], wqdb[:, c, :],
                 g1t[:, c:c + 1], None, ALU.mult,
                 waits=[wqd_ld, w_small] if c == 0 else (),
                 inc=SV if c == NCH - 1 else None)
    cs_mm = []
    for fo in range(3):
        lo, hi = fo * 512, min(fo * 512 + 512, 1088)
        last = None
        for c in range(NCH):
            last = mm(PS[0:1, fo, 0:hi - lo], onesB[:, :], wqdb[:, c, lo:hi],
                      start=(c == 0), stop=(c == NCH - 1),
                      waits=[wv] if c == 0 else (), inc=SM if c == NCH - 1 else None)
        cs_mm.append(last)
    nmr = nmrQ[0:1, :]
    nm_ops = []
    for fo in range(3):
        lo, hi = fo * 512, min(fo * 512 + 512, 1088)
        nm_ops.append(vop("v", "tensor_scalar", nmr[:, lo:hi], PS[0:1, fo, 0:hi - lo],
                          consts[0:1, 4:5], None, ALU.mult, waits=[cs_mm[fo]], inc=SV))
    wbc = []
    for fo in range(3):
        lo, hi = fo * 512, min(fo * 512 + 512, 1088)
        g = mm(PS[:, 3, 0:hi - lo], ones1B[0:1, :], nmr[:, lo:hi],
               start=True, stop=True, waits=[nm_ops[fo], wbc[-1] if wbc else None],
               inc=SM)
        wbc.append(vop("v", "tensor_copy", nmb[:, lo:hi], PS[:, 3, 0:hi - lo],
                       waits=[g], inc=SV))
    wprep = None
    for c in range(NCH):
        wprep = vop("v", "tensor_tensor", wqdb[:, c, :], wqdb[:, c, :],
                    nmb[:, :], ALU.subtract, waits=[wbc[-1]] if c == 0 else (),
                    inc=SV if c == NCH - 1 else None)

    def ln_stats(x_ready, r_rln, r_R1, guards=()):
        sqd = {}
        s1_mm = s2_mm = None
        for c in range(NCH):
            for hh in range(2):
                pp = (2 * c + hh) % 2
                sq = act(A_sq[:, pp, :], xar[:, c, hh * 1024:(hh + 1) * 1024],
                         AF.Square, waits=[x_ready, sqd.get(pp)] + list(guards), inc=SS)
                for q in range(2):
                    qq = hh * 2 + q
                    s2_mm = mm(PS[0:1, 4 + qq, :], onesF[:, :],
                               A_sq[:, pp, q * 512:(q + 1) * 512],
                               start=(c == 0), stop=(c == NCH - 1),
                               waits=[sq] if q == 0 else (),
                               inc=SM if q == 1 else None)
                sqd[pp] = s2_mm
            for q in range(4):
                s1_mm = mm(PS[0:1, q, :], onesF[:, :], xar[:, c, q * 512:(q + 1) * 512],
                           waits=([x_ready] + list(guards)) if c == 0 and q == 0 else (),
                           start=(c == 0), stop=(c == NCH - 1),
                           inc=SM if c == NCH - 1 and q == 3 else None)
        S1 = PS[0:1, 0:4, :].rearrange("p a b -> p (a b)")
        S2 = PS[0:1, 4:8, :].rearrange("p a b -> p (a b)")
        r_m = rmrow[0:1, :]
        r_t = rtrow[0:1, :]
        a1 = vop("v", "tensor_scalar", r_m, S1, consts[0:1, 4:5], None, ALU.mult,
                 waits=[s1_mm], inc=SV)
        a2 = vop("v", "tensor_tensor", r_t, r_m, r_m, ALU.mult, waits=[a1], inc=SV)
        a3 = vop("v", "scalar_tensor_tensor", r_rln, S2, consts[0:1, 4:5], r_t,
                 ALU.mult, ALU.subtract, waits=[s2_mm, a2], inc=SV)
        a4 = act(r_t, r_rln, AF.Ln, bias=consts[0:1, 5:6], waits=[a3], inc=SS)
        a5 = act(r_rln, r_t, AF.Exp, bias=crow2[0:1, 2:3], scale=crow2[0:1, 3:4],
                 waits=[a4], inc=SS)
        done = a5
        if r_R1 is not None:
            b1 = vop("v", "tensor_tensor", r_t, r_rln, r_rln, ALU.mult, waits=[a5], inc=SV)
            b2 = vop("v", "tensor_scalar", r_t, r_t, crow2[0:1, 0:1], crow2[0:1, 1:2],
                     ALU.mult, ALU.add, waits=[b1], inc=SV)
            b3 = act(r_t, r_t, AF.Ln, bias=crow2[0:1, 2:3], waits=[b2], inc=SS)
            b4 = act(r_t, r_t, AF.Exp, bias=crow2[0:1, 2:3], scale=crow2[0:1, 3:4],
                     waits=[b3], inc=SS)
            b5 = vop("v", "tensor_tensor", r_R1, r_rln, r_t, ALU.mult, waits=[b4], inc=SV)
            b6 = vop("v", "tensor_tensor", mrlnb[0:1, :], r_m, r_rln, ALU.mult,
                     waits=[b5], inc=SV)
            done = b6
        return done, a3

    def emit_ffn(call, x_ready):
        r_R1 = rowsA[0:1, 0, :]
        r_rln = rowsA[0:1, 1, :]
        mr1 = dma("g", Qp[64:73, :], mrq_in, waits=[x_ready], inc=DW)
        mr2 = dma("g", Kp[64:73, :], mrk_in, waits=[x_ready], inc=DW)
        mrows = [mr1, mr2]
        vg_ms = vop("g", "memset", Vg[:, :, :], 1.0, waits=[x_ready], inc=SG)
        rows_done, stats_rel = ln_stats(x_ready, r_rln, r_R1, guards=[wprep])
        wo_ld = dma("g", A_wo[:, :, :], wo_in.rearrange("(c p) n -> p c n", p=128),
                    waits=[x_ready, rows_done], inc=DW)
        dump(f"wqdb_{call}", wqdb[:, :, :].rearrange("p a b -> p (a b)"), [rows_done])
        dump(f"wupb_{call}", wupb[0:64, :], [rows_done])
        dump(f"awo_{call}", A_wo[:, :, :].rearrange("p a b -> p (a b)"), [wo_ld])
        dump(f"qkp_{call}", Kp[0:73, :], [mrows, SG.now(), SV.now(), SS.now()])
        dump(f"rows_{call}", rowsA[0:1, :, :].rearrange("p a b -> p (a b)"),
             [rows_done])

        bank4_free = [stats_rel]
        r2guard = [stats_rel]
        b56_free = [stats_rel, stats_rel]
        b7_free = [stats_rel]
        b03_free = [stats_rel]
        sq_guard = [stats_rel, stats_rel]
        prev_tmp = None
        exp_guard = [None, None]
        av_guard = [None, None]
        tr_guard = [None, None]
        dv_guard = [vg_ms, vg_ms]

        for s in range(NSL):
            t0 = s * SL
            tsl = slice(t0, t0 + SL)
            b_r1 = mm(PS[:, 4, 0:SL], ones1F[0:1, :], r_R1[:, t0:t0 + SL],
                      start=True, stop=True, waits=[rows_done] + bank4_free, inc=SM)
            hts = None
            for c in range(NCH):
                hts = vop("v", "scalar_tensor_tensor", A_ht[:, c, :], xar[:, c, tsl],
                          onesF[:, :], PS[:, 4, 0:SL], ALU.mult, ALU.mult,
                          waits=[b_r1] if c == 0 else (),
                          inc=SV if c == NCH - 1 else None)
            bank4_free = [hts]
            if s == 0:
                dump(f"ht_{call}", A_ht[:, :, :].rearrange("p a b -> p (a b)"), [hts])
            ev = {}
            for fo in range(9):
                bk = 5 + fo % 2
                fl = fo * 128
                fw = 64 if fo == 8 else 128
                mml = None
                for c in range(NCH):
                    mml = mm(PS[0:fw, bk, 0:SL], wqdb[:, c, fl:fl + fw], A_ht[:, c, :],
                             start=(c == 0), stop=(c == NCH - 1),
                             waits=[hts, wprep, b56_free[fo % 2]] if c == 0 else (),
                             inc=SM if c == NCH - 1 else None)
                if fo == 8:
                    e = act(A_dv[0:64, :], PS[0:64, bk, 0:SL], AF.Identity,
                            bias=bdt[0:64, :], waits=[mml], inc=SS)
                else:
                    e = act(A_q[:, fo, :], PS[:, bk, 0:SL], AF.Identity,
                            bias=bqt[:, fo:fo + 1], waits=[mml], inc=SS)
                ev[fo] = e
                b56_free[fo % 2] = e
            kev = {}
            for fo in range(16):
                bk = 5 + fo % 2
                mml = mm(PS[:, bk, 0:SL], wupb[0:64, fo * 128:(fo + 1) * 128],
                         A_dv[0:64, :], start=True, stop=True,
                         waits=[ev[8], b56_free[fo % 2]], inc=SM)
                dst = A_kt[:, fo, :] if fo < 8 else A_vt[:, fo - 8, :]
                e = act(dst, PS[:, bk, 0:SL], AF.Identity, bias=bupt[:, fo:fo + 1],
                        waits=[mml], inc=SS)
                kev[fo] = e
                b56_free[fo % 2] = e
            q_done = [ev[i] for i in range(8)]
            kv_done = [kev[i] for i in range(16)]
            if s == 0:
                dump(f"q_{call}", A_q[:, :, :].rearrange("p a b -> p (a b)"), q_done)
                dump(f"kt_{call}", A_kt[:, :, :].rearrange("p a b -> p (a b)"), kv_done)
            engs = ["v", "s", "g"]
            for h in range(16):
                eng = engs[h % 3]
                for (src, dstt) in (
                        (A_q[(h % 2) * 64:(h % 2) * 64 + 64, h // 2, :], Qp),
                        (A_kt[(h % 2) * 64:(h % 2) * 64 + 64, h // 2, :], Kp),
                        (A_vt[(h % 2) * 64:(h % 2) * 64 + 64, h // 2, :], Vp)):
                    dst = dstt[0:64, :].rearrange("p (a b) -> p a b", b=16)[:, :, h]
                    vop(eng, CPOP[eng], dst, src,
                        waits=q_done + kv_done + [w_consts] + b03_free,
                        inc=ESEM[eng])
            builds = [SV.now(), SS.now(), SG.now()]
            NR = 16 * SL // 128 // 4
            ptv_all = PS[:, 6:8, :].rearrange("p a b -> p (a b)").bitcast(BF16)
            last_div = None
            for r in range(NR):
                bS = r % 2
                bO = 2 + r % 2
                qkl = None
                for gi in range(4):
                    g_ = r * 4 + gi
                    col = slice(g_ * 128, (g_ + 1) * 128)
                    qkl = mm(PS[:, bS, gi * 128:(gi + 1) * 128], Kp[0:73, col],
                             Qp[0:73, col], start=(gi == 0), stop=(gi == 3),
                             waits=builds + mrows + [exp_guard[bS]] if gi == 0 else (),
                             inc=SM if gi == 3 else None)
                trl = None
                for gi in range(4):
                    g_ = r * 4 + gi
                    col = slice(g_ * 128, (g_ + 1) * 128)
                    ptv = ptv_all[:, (r % 2) * 1024 + gi * 64:(r % 2) * 1024 + gi * 64 + 64]
                    trl = transp(ptv, Vp[0:64, col], identB[0:64, :],
                                 waits=builds + [tr_guard[r % 2]] if gi == 0 else (),
                                 inc=SM if gi == 3 else None)
                ex = act(PtT[:, r % 2, :], PS[:, bS, :], AF.Exp,
                         bias=consts[:, 1:2], scale=consts[:, 0:1],
                         waits=[qkl, av_guard[r % 2]], inc=SS)
                exp_guard[bS] = ex
                evs = None
                for gi in range(4):
                    src = ptv_all[:, (r % 2) * 1024 + gi * 64:(r % 2) * 1024 + gi * 64 + 64]
                    evs = vop("v", "tensor_copy", Vg[:, gi, 0:64], src,
                              waits=[trl, av_guard[r % 2]] if gi == 0 else (),
                              inc=SV if gi == 3 else None)
                tr_guard[r % 2] = evs
                avl = None
                for gi in range(4):
                    avl = mm(PS[:, bO, gi * 128:(gi + 1) * 128], Vg[:, gi, :],
                             PtT[:, r % 2, gi * 128:(gi + 1) * 128],
                             start=(gi == 0), stop=(gi == 3),
                             waits=[evs, ex, dv_guard[r % 2]] if gi == 0 else (),
                             inc=SM if gi == 3 else None)
                av_guard[r % 2] = avl
                rc = vop("v", "reciprocal", Rt[0:64, :], PS[64:128, bO, :],
                         waits=[avl], inc=SV)
                last_div = vop("v", "tensor_tensor", TEMP[0:64, r * 512:(r + 1) * 512],
                               PS[0:64, bO, :], Rt[0:64, :], ALU.mult,
                               waits=[rc], inc=SV)
                dv_guard[r % 2] = last_div
            b03_free = [exp_guard[0], exp_guard[1], dv_guard[0], dv_guard[1]]
            if s == 0:
                dump(f"temp_{call}", TEMP[0:64, :], [last_div])
            ocs = []
            for h in range(16):
                eng = engs[h % 3]
                src = TEMP[0:64, :].rearrange("p (a b) -> p a b", b=16)[:, :, h]
                dst = A_at[(h % 2) * 64:(h % 2) * 64 + 64, h // 2, :]
                ocs.append(vop(eng, CPOP[eng], dst, src, waits=[last_div],
                               inc=ESEM[eng]))
            oc_done = [SV.now(), SS.now(), SG.now()]
            cr0 = vop("g", "memset", A_cr[0:2, :], 1.0,
                      waits=oc_done, inc=SG)
            cr1 = vop("v", "tensor_copy", A_cr[0:1, :], mrlnb[0:1, tsl],
                      waits=[rows_done, cr0], inc=SV)
            rlb = mm(PS[:, 4, 0:SL], ones1F[0:1, :], r_rln[:, t0:t0 + SL],
                     start=True, stop=True, waits=[rows_done] + bank4_free, inc=SM)
            xm = {}
            for fo in range(NCH):
                bk = 5 + fo % 2
                for c in range(NCH):
                    mml = mm(PS[:, bk, 0:SL], A_wo[:, c, fo * 128:(fo + 1) * 128],
                             A_at[:, c, :], start=(c == 0),
                             stop=(nocrow and c == NCH - 1),
                             waits=oc_done + [wo_ld, b56_free[fo % 2]] if c == 0 else (),
                             inc=SM if (nocrow and c == NCH - 1) else None)
                if not nocrow:
                    mml = mm(PS[:, bk, 0:SL], woextb[0:2, fo, :], A_cr[0:2, :],
                             start=False, stop=True, waits=[cr0, cr1], inc=SM)
                t1 = vop("v", "tensor_tensor", A_tmp[:, :], xar[:, fo, tsl],
                         PS[:, 4, 0:SL], ALU.mult, waits=[rlb] if fo == 0 else (),
                         inc=SV)
                xm[fo] = vop("v", "tensor_tensor", A_xm[:, fo, :], A_tmp[:, :],
                             PS[:, bk, 0:SL], ALU.add, waits=[mml, t1], inc=SV)
                b56_free[fo % 2] = xm[fo]
            bank4_free = [xm[NCH - 1]]
            if s == 0:
                dsnap = dump(f"xmid_{call}", A_xm[:, :, :].rearrange("p a b -> p (a b)"),
                             [xm[i] for i in range(NCH)])
                if dsnap is not None:
                    bank4_free = bank4_free + [dsnap]
                    for _k in range(2):
                        b56_free[_k] = [b56_free[_k], dsnap]
            s2m = None
            for c in range(NCH):
                sq = act(A_sq[:, c % 2, 0:SL], A_xm[:, c, :], AF.Square,
                         waits=[xm[c], sq_guard[c % 2]], inc=SS)
                s2m = mm(PS[0:1, 7, 0:SL], onesF[:, :], A_sq[:, c % 2, 0:SL],
                         start=(c == 0), stop=(c == NCH - 1),
                         waits=[sq] + (b7_free if c == 0 else []),
                         inc=SM)
                sq_guard[c % 2] = s2m
            r2s = act(r2row[0:1, :], PS[0:1, 7, 0:SL], AF.Ln,
                      bias=consts[0:1, 6:7], scale=consts[0:1, 4:5],
                      waits=[s2m, r2guard[0]], inc=SS)
            b7_free = [r2s]
            r2r = act(r2row[0:1, :], r2row[0:1, :], AF.Exp,
                      bias=crow2[0:1, 2:3], scale=crow2[0:1, 3:4],
                      waits=[r2s], inc=SS)
            r2b = mm(PS[:, 4, 0:SL], ones1F[0:1, :], r2row[0:1, :],
                     start=True, stop=True, waits=[r2r] + bank4_free, inc=SM)
            r2guard[0] = r2b
            h2l = None
            tmpl = None
            for c in range(NCH):
                h2l = vop("v", "scalar_tensor_tensor", h2ar[:, c, tsl], A_xm[:, c, :],
                          g2t[:, c:c + 1], PS[:, 4, 0:SL], ALU.mult, ALU.mult,
                          waits=[r2b] if c == 0 else (),
                          inc=SV if c == NCH - 1 else None)
            for c in range(NCH):
                tmpl = vop("v", "scalar_tensor_tensor", xar[:, c, tsl], A_xm[:, c, :],
                           consts[:, 3:4], xar[:, c, tsl], ALU.mult, ALU.add,
                           waits=[h2l] if c == 0 else (),
                           inc=SV if c == NCH - 1 else None)
            bank4_free = [h2l]
            prev_tmp = tmpl
        bar = barrier()
        dump(f"h2_{call}", h2ar[:, 0, :], bar)
        x1l = None
        we_ld = [None, None]
        first_zs_waits = bar
        for zs in range(NZS):
            z0 = zs * ZSL
            zsl = slice(z0, z0 + ZSL)
            gl = None
            for c in range(NCH):
                gl = mm(PS[0:8, 0, 0:ZSL], wgb[:, c, :], h2ar[:, c, zsl],
                        start=(c == 0), stop=(c == NCH - 1),
                        waits=first_zs_waits + [wg_ld] if c == 0 else (),
                        inc=SM if c == NCH - 1 else None)
            gex = act(B_ge[0:8, :], PS[0:8, 0, 0:ZSL], AF.Exp, bias=bgst[0:8, :],
                      waits=[gl, bg_sh], inc=SS)
            gsum = mm(PS[0:64, 1, 0:ZSL], ones8F[0:8, :], B_ge[0:8, :],
                      start=True, stop=True, waits=[gex], inc=SM)
            grc = vop("v", "reciprocal", RtB[0:64, :], PS[0:64, 1, 0:ZSL],
                      waits=[gsum], inc=SV)
            gT = vop("v", "tensor_tensor", B_gt[0:8, :], B_ge[0:8, :], RtB[0:8, :],
                     ALU.mult, waits=[grc], inc=SV)
            if zs == 0:
                dump(f"gate_{call}", B_gt[0:8, :], [gT])
            gevs = []
            gbl = [grc, grc]
            for e in range(8):
                bk = e % 2
                gb = mm(PS[:, bk, 0:ZSL], selB[0:8, e, :], B_gt[0:8, :],
                        start=True, stop=True, waits=[gT, gbl[bk], w_consts], inc=SM)
                gv = vop("v", "tensor_copy", B_gall[:, e, :], PS[:, bk, 0:ZSL],
                         waits=[gb], inc=SV)
                gbl[bk] = gv
                gevs.append(gv)
            mm_last = None
            ze_rel = [None, None]
            we_rel = getattr(emit_ffn, "_we_rel", [None, None])
            for e in range(8):
                wb = e % 2
                we_ld[wb] = dma("g", B_we[:, wb, :, :],
                                we_in[e].rearrange("(c p) n -> p c n", p=128),
                                waits=[we_ld[wb], we_rel[wb],
                                       bar[0] if zs == 0 and e < 2 else None],
                                inc=DW)
                zl = None
                for c in range(NCH):
                    zl = vop("v", "tensor_tensor", B_z[:, wb, c, :], h2ar[:, c, zsl],
                             B_gall[:, e, :], ALU.mult,
                             waits=[gevs[e], ze_rel[wb]] if c == 0 else (),
                             inc=SV if c == NCH - 1 else None)
                ze_rel[wb] = None
                for fo in range(NCH):
                    for c in range(NCH):
                        mm_last = mm(PS[:, fo, 0:ZSL],
                                     B_we[:, wb, c, fo * 128:(fo + 1) * 128],
                                     B_z[:, wb, c, :],
                                     start=(e == 0 and c == 0),
                                     stop=(e == 7 and c == NCH - 1),
                                     waits=[zl, we_ld[wb]] + ([gevs[-1]] if e == 0 else [])
                                     if c == 0 else (),
                                     inc=SM if c == NCH - 1 and fo == NCH - 1 else None)
                ze_rel[wb] = mm_last
                we_rel[wb] = mm_last
            emit_ffn._we_rel = we_rel
            x1c = []
            for fo in range(NCH):
                x1l = vop("v", "scalar_tensor_tensor", xar[:, fo, zsl],
                          PS[:, fo, 0:ZSL], consts[:, 3:4], xar[:, fo, zsl],
                          ALU.mult, ALU.add, waits=[mm_last, prev_tmp], inc=SV)
                x1c.append(x1l)
            first_zs_waits = x1c
        dump(f"x1_{call}", xar[:, 0, :], [x1l])
        return x1l

    def emit_attn(x1_done):
        bar0 = barrier()
        r_rln = rowsA[0:1, 0, :]
        rows_done, stats_rel = ln_stats(x1_done, r_rln, None)
        lnp_rel = [stats_rel, stats_rel]
        ln_dma = []
        bcast_rel = [stats_rel, stats_rel]
        for qd in range(4):
            pp = qd % 2
            rb = mm(PS[:, pp, :], ones1F[0:1, :], r_rln[:, qd * 512:(qd + 1) * 512],
                    start=True, stop=True, waits=[rows_done, bcast_rel[pp]], inc=SM)
            lnl = None
            for c in range(NCH):
                lnl = vop("v", "tensor_tensor", AT_lnp[:, pp, c, :],
                          xar[:, c, qd * 512:(qd + 1) * 512], PS[:, pp, :],
                          ALU.mult, waits=[rb, lnp_rel[pp]] if c == 0 else (),
                          inc=SV if c == NCH - 1 else None)
            bcast_rel[pp] = lnl
            d = dma("y", ln2own_d.rearrange("(c p) (a n) -> p a c n", p=128, a=4)[:, qd, :, :],
                    AT_lnp[:, pp, :, :], waits=[lnl], inc=DS)
            lnp_rel[pp] = d
            ln_dma.append(d)
        d2 = dma("y", x_page[:, :, :], xar[:, :, :], waits=[x1_done], inc=DS)
        ag1 = sch.op("g", lambda e: e.collective_compute(
            "AllGather", ALU.bypass,
            replica_groups=[[0, 1], [2, 3], [4, 5], [6, 7]],
            ins=[ln2own_d[:, :].opt()], outs=[ln2all_d[:, :].opt()]),
            ln_dma + [bar0[3]], CC)
        d3a = dma("y", ln2full[:, :, 0:NT],
                  ln2all_d[0:D, :].rearrange("(c p) n -> p c n", p=128),
                  waits=[ag1, d2], inc=DL)
        d3 = dma("y", ln2full[:, :, NT:2 * NT],
                 ln2all_d[D:2 * D, :].rearrange("(c p) n -> p c n", p=128),
                 waits=[ag1, d2], inc=DL)
        d3 = [d3a, d3]
        dump("ln2full", ln2full[:, 0, :], [d3])
        bar_lp = barrier()

        wm_rel = [stats_rel]
        wsl_rel = [None, None]
        b01_rel = [bcast_rel[0], bcast_rel[1]]
        kb_rel = [None, None]
        k_dma = []
        qk_last = None

        def prep_128slab(dst_ws, pp, fo):
            ld = dma("g", dst_ws,
                     wint_in[:, fo * 128:(fo + 1) * 128].rearrange("(c p) n -> p c n", p=128),
                     waits=[wsl_rel[pp]] + (ln_dma if fo < 2 else []), inc=DW)
            csl = None
            for c in range(NCH):
                csl = mm(PS[0:1, 6, 0:128], onesB[:, :], dst_ws[:, c, :],
                         start=(c == 0), stop=(c == NCH - 1),
                         waits=[ld] + wm_rel if c == 0 else (),
                         inc=SM if c == NCH - 1 else None)
            nm = vop("v", "tensor_scalar", AT_pr[0:1, 0:128], PS[0:1, 6, 0:128],
                     consts[0:1, 4:5], None, ALU.mult, waits=[csl], inc=SV)
            nb = mm(PS[:, 6, 0:128], ones1B[0:1, :], AT_pr[0:1, 0:128],
                    start=True, stop=True, waits=[nm], inc=SM)
            nbe = vop("v", "tensor_copy", AT_wm[:, :], PS[:, 6, 0:128],
                      waits=[nb], inc=SV)
            wm_rel[0] = nbe
            wp = None
            for c in range(NCH):
                wp = vop("v", "tensor_tensor", dst_ws[:, c, :], dst_ws[:, c, :],
                         AT_wm[:, :], ALU.subtract, waits=[nbe] if c == 0 else (),
                         inc=SV if c == NCH - 1 else None)
            return wp

        for fo in range(8):
            pp = fo % 2
            wsl = AT_ws[:, pp, :, :]
            wp = prep_128slab(wsl, pp, fo)
            last_e = None
            for tf in range(8):
                bk = tf % 2
                mml = None
                for c in range(NCH):
                    mml = mm(PS[:, bk, :], wsl[:, c, :],
                             ln2full[:, c, tf * 512:(tf + 1) * 512],
                             start=(c == 0), stop=(c == NCH - 1),
                             waits=[wp, d3, b01_rel[bk]] if c == 0 else (),
                             inc=SM if c == NCH - 1 else None)
                if fo < 4:
                    last_e = vop("s", "copy", AT_q[:, fo, tf * 512:(tf + 1) * 512],
                                 PS[:, bk, :], waits=[mml], inc=SS)
                else:
                    last_e = vop("s", "copy", AT_kb[:, pp, tf * 512:(tf + 1) * 512],
                                 PS[:, bk, :],
                                 waits=[mml] + ([kb_rel[pp]] if tf == 0 else []),
                                 inc=SS)
                b01_rel[bk] = last_e
            wsl_rel[pp] = last_e
            qk_last = last_e
            if fo >= 4:
                dkk = dma("y", k_dram[(fo - 4) * 128:(fo - 3) * 128, :], AT_kb[:, pp, :],
                          waits=[last_e], inc=DS)
                kb_rel[pp] = dkk
                k_dma.append(dkk)
        vw_ld = dma("g", VW[:, :, :],
                    wint_in[:, 1024:1536].rearrange("(c p) n -> p c n", p=128),
                    waits=[d3, k_dma[2], kb_rel[0]], inc=DW)
        csl = None
        for c in range(NCH):
            csl = mm(PS[0:1, 6, 0:512], onesB[:, :], VW[:, c, :],
                     start=(c == 0), stop=(c == NCH - 1),
                     waits=[vw_ld] + wm_rel if c == 0 else (),
                     inc=SM if c == NCH - 1 else None)
        nm = vop("v", "tensor_scalar", AT_pr[0:1, 0:512], PS[0:1, 6, 0:512],
                 consts[0:1, 4:5], None, ALU.mult, waits=[csl], inc=SV)
        nbe = None
        for half in range(2):
            nb = mm(PS[:, 6, 0:256], ones1B[0:1, :],
                    AT_pr[0:1, half * 256:half * 256 + 256],
                    start=True, stop=True, waits=[nm, nbe], inc=SM)
            nbe = vop("v", "tensor_copy", nmb[:, half * 256:half * 256 + 256],
                      PS[:, 6, 0:256], waits=[nb], inc=SV)
        vwp = None
        for c in range(NCH):
            vwp = vop("v", "tensor_tensor", VW[:, c, :], VW[:, c, :], nmb[:, 0:512],
                      ALU.subtract, waits=[nbe] if c == 0 else (),
                      inc=SV if c == NCH - 1 else None)
        vones = ms(AT_vb[:, :, :, :, :], 1.0, waits=ln_dma, inc=SG)
        v_dma = []
        b23_rel = [None, None]
        vb_rel = [None, None]
        for tc in range(32):
            pb = (tc // 2) % 2
            sub = tc % 2
            bk = 2 + tc % 2
            mml = None
            for c in range(NCH):
                mml = mm(PS[:, bk, :], ln2full[:, c, tc * 128:(tc + 1) * 128],
                         VW[:, c, :], start=(c == 0), stop=(c == NCH - 1),
                         waits=[vwp, d3, b23_rel[tc % 2]] + ([vones] if tc < 2 else [])
                         if c == 0 else (), inc=SM if c == NCH - 1 else None)
            evv = vop("v", "tensor_copy", AT_vb[:, pb, sub, :, 0:64],
                      PS[:, bk, :].rearrange("p (a b) -> p a b", a=8),
                      waits=[mml] + ([vb_rel[pb]] if sub == 0 else []), inc=SV)
            b23_rel[tc % 2] = evv
            vd = dma("y",
                     v_dram.rearrange("h (a p) f -> p a h f", p=128)[:, tc, :, :],
                     AT_vb[:, pb, sub, :, :], waits=[evv], inc=DS)
            if sub == 1:
                vb_rel[pb] = vd
            v_dma.append(vd)
        x_back = dma("y", xar[:, :, :], x_page[:, :, :],
                     waits=[qk_last, vwp, v_dma[-1], SM.now()], inc=DL)

        dump("atq", AT_q[:, 0, :], [qk_last, SM.now(), SS.now()])
        bar1 = barrier()
        ks_rel = [None, None]
        vs_rel = [None, None]
        exp_g = [None, None]
        av_g = [None, None]
        psO_rel = [None, None]
        bpt_rel = [None, None]
        ah_rel = {}
        ao_dma = []
        for hp in range(4):
            kss = dma("s", KS[:, :], k_dram[hp * 128:(hp + 1) * 128, :],
                      waits=[k_dma[hp], ks_rel[0]] + (bar1 if hp == 0 else []),
                      inc=DK)
            if hp == 0:
                ksd = dump("ks0", KS[0:64, :], [kss])
                if ksd is not None:
                    ks_rel[0] = [ks_rel[0], ksd] if ks_rel[0] else [ksd]
            vsd = []
            for h01 in range(2):
                vsd.append(dma("s", AT_vs[:, h01, :, :],
                               v_dram[2 * hp + h01, :, :].rearrange("(a p) f -> p a f", p=128),
                               waits=[v_dma[-1], vs_rel[h01]] + (bar1 if hp == 0 else []),
                               inc=DK))
            hp_av = None
            for qc in range(8):
                nk = (qc + 1) * 4
                avv = [None, None]
                for kc in range(nk):
                    bS = (kc % 2) * 2
                    qkk = None
                    for h01 in range(2):
                        qkk = mm(PS[:, bS + h01, :],
                                 KS[h01 * 64:h01 * 64 + 64,
                                    kc * 128:(kc + 1) * 128],
                                 AT_q[h01 * 64:h01 * 64 + 64, hp,
                                      qc * 512:(qc + 1) * 512],
                                 start=True, stop=True,
                                 waits=[kss, exp_g[kc % 2]] if h01 == 0 else (),
                                 inc=SM if h01 == 1 else None,
                                 tp=(64, 0) if h01 == 1 else None)
                    ex = act(BPt[:, kc % 2, :],
                             PS[:, bS:bS + 2, :].rearrange("p a b -> p (a b)"),
                             AF.Exp, bias=consts[:, 2:3], scale=consts[:, 0:1],
                             waits=[qkk, bpt_rel[kc % 2]], inc=SS)
                    exp_g[kc % 2] = ex
                    mask_w = [ex]
                    if kc >= qc * 4:
                        afl = None
                        for h01 in range(2):
                            afl = vop("g", "affine_select",
                                      BPt[:, kc % 2, h01 * 512:(h01 + 1) * 512],
                                      BPt[:, kc % 2, h01 * 512:(h01 + 1) * 512],
                                      pattern=[[1, 512]], compare_op=ALU.is_ge,
                                      fill=0.0, base=qc * 512 - kc * 128,
                                      channel_multiplier=-1,
                                      waits=[ex] if h01 == 0 else (),
                                      inc=SG if h01 == 1 else None)
                        mask_w = [afl]
                    for h01 in range(2):
                        avv[h01] = mm(PS[:, 4 + h01, :],
                                      AT_vs[:, h01, kc, :],
                                      BPt[:, kc % 2, h01 * 512:(h01 + 1) * 512],
                                      start=(kc == 0), stop=(kc == nk - 1),
                                      waits=mask_w + vsd +
                                      ([psO_rel[h01]] if kc == 0 else []),
                                      inc=SM if h01 == 1 or kc == nk - 1 else None)
                    bpt_rel[kc % 2] = avv[1]
                hp_av = avv[1]
                for h01 in range(2):
                    rc = vop("v", "reciprocal", RtA[0:64, :], PS[64:128, 4 + h01, :],
                             waits=[avv[h01], ah_rel.get(h01)], inc=SV)
                    dv = vop("v", "tensor_tensor", AT_ah[0:64, h01, :],
                             PS[0:64, 4 + h01, :], RtA[0:64, :], ALU.mult,
                             waits=[rc], inc=SV)
                    psO_rel[h01] = dv
                    half = qc // 4
                    tl0 = qc * 512 - half * NT
                    add = dma("y",
                              attnown_d[half, (2 * hp + h01) * 64:(2 * hp + h01 + 1) * 64,
                                        tl0:tl0 + 512],
                              AT_ah[0:64, h01, :], waits=[dv], inc=DS)
                    ah_rel[h01] = add
                    ao_dma.append(add)
            ks_rel[0] = hp_av
            vs_rel[0] = hp_av
            vs_rel[1] = hp_av
        a2a = sch.op("g", lambda e: e.collective_compute(
            "AllGather", ALU.bypass,
            replica_groups=[[0, 1], [2, 3], [4, 5], [6, 7]],
            ins=[attnown_d.rearrange("a b c -> (a b) c").opt()],
            outs=[attnall_d.rearrange("r a b c -> (r a b) c").opt()]),
            ao_dma, CC)
        dump("attnown", attnown_d[0, 0:64, :], [a2a])
        dump("kdram", k_dram[0:64, 0:2048], [a2a])
        dump("vdram", v_dram[0, 0:2048, :].rearrange("(a b) f -> a (b f)", b=16), [a2a])
        dump("attnall", attnall_d[0, 0, 0:64, :], [a2a])

        bar2 = barrier()
        wout_ld = dma("g", AT_wout[:, :, :],
                      wout_in.rearrange("(c p) n -> p c n", p=128),
                      waits=bar2, inc=DW)
        slab4 = at_slab[:, :, :].rearrange("p (r c) n -> p r c n", r=2)

        def slab_fn(e):
            reg = e.alloc_register("rreg")
            e.load(reg, ridxt[0:1, 0:1])
            for mh in range(2):
                ctx = e.If_eq(reg, 0) if mh == 0 else e.Else()
                with ctx:
                    for r in range(2):
                        e.dma_start(out=slab4[:, r, :, :],
                                    in_=attnall_d[r, mh, :, :].rearrange("(c p) n -> p c n", p=128)
                                    ).then_inc(DL.h, 16)
            return None
        sch.op("y", slab_fn, [a2a, ridx_ld] + bar2, None)
        DL.bump(32)
        sl_ld = DL.now()
        dump("slab", at_slab[:, :, :].rearrange("p a b -> p (a b)"), [sl_ld])
        dump("wout2", AT_wout[:, :, :].rearrange("p a b -> p (a b)"), [wout_ld])
        dump("xrest", xar[:, 0, :], [x_back])
        x2l = None
        b_rel = [bar2[1], bar2[1]]
        for fo in range(NCH):
            for tf in range(4):
                bk = (fo * 4 + tf) % 2
                mml = None
                for c in range(NCH):
                    mml = mm(PS[:, bk, :], AT_wout[:, c, fo * 128:(fo + 1) * 128],
                             at_slab[:, c, tf * 512:(tf + 1) * 512],
                             start=(c == 0), stop=(c == NCH - 1),
                             waits=[wout_ld, sl_ld, x_back, b_rel[bk]] if c == 0 else (),
                             inc=SM if c == NCH - 1 else None)
                x2l = vop("v", "tensor_tensor", xar[:, fo, tf * 512:(tf + 1) * 512],
                          xar[:, fo, tf * 512:(tf + 1) * 512], PS[:, bk, :],
                          ALU.add, waits=[mml], inc=SV)
                b_rel[bk] = x2l
        dump("x2", xar[:, 0, :], [x2l])
        dump("wout3", AT_wout[:, :, :].rearrange("p a b -> p (a b)"), [x2l])
        dump("slab3", at_slab[:, :, :].rearrange("p a b -> p (a b)"), [x2l])
        return x2l

    x1 = emit_ffn(0, x_ld)
    last = x1
    if parts >= 2:
        last = emit_attn(x1)
    if parts >= 3:
        barx = barrier()
        last = emit_ffn(1, barx[1])
    od = dma("y", out_d.rearrange("(c p) n -> p c n", p=128), xar[:, :, :],
             waits=[last], inc=DS)
    sch.op("y", lambda e: e.nop(), [od, DS.now()], None)

    with nc.Block() as block:
        @block.tensor
        def _(e):
            sch.replay("t", e)

        @block.scalar
        def _(e):
            sch.replay("s", e)

        @block.vector
        def _(e):
            sch.replay("v", e)

        @block.gpsimd
        def _(e):
            sch.replay("g", e)

        @block.sync
        def _(e):
            sch.replay("y", e)

    return nc


_NCACHE = {}


def _prep_inputs(inputs):
    f32 = np.float32
    Wq = np.asarray(inputs["L0_ffn_Wq"], f32)
    Wdown = np.asarray(inputs["L0_ffn_Wdown"], f32)
    Wup = np.asarray(inputs["L0_ffn_Wup"], f32)
    Wg = np.asarray(inputs["L0_ffn_Wg"], f32)
    Wo = np.asarray(inputs["L0_ffn_Wo"], f32)
    We = np.asarray(inputs["L0_ffn_We"], f32)
    Win = np.asarray(inputs["L0_attn_in_proj_weight"], f32)
    Wout = np.asarray(inputs["L0_attn_out_proj_weight"], f32)
    x = np.asarray(inputs["L0_attn_query"], f32)

    wqd = np.ascontiguousarray(np.concatenate([Wq, Wdown], axis=1))
    idx = np.arange(2048).reshape(16, 2, 64)
    perm = np.concatenate([idx[:, 0, :].reshape(-1), idx[:, 1, :].reshape(-1)])
    wup = np.ascontiguousarray(Wup[:, perm])
    bup = np.asarray(inputs["L0_ffn_bup"], f32)[perm]
    wintT = np.ascontiguousarray(Win.T)
    woutT = np.ascontiguousarray(Wout.T)

    mr = np.zeros((9, 16 * 128), np.float32)
    sub = np.arange(16 * 128)
    blk = (sub // 16) % 8
    for j in range(8):
        mr[j, blk == j] = SQB
    mr[8, :] = SQB
    mrq = mr.copy()
    mrq[8, :] = -SQB
    com = {
        "mrq": mrq,
        "mrk": mr,
        "wqd": wqd,
        "wup": wup,
        "wg": np.ascontiguousarray(Wg),
        "wo": np.ascontiguousarray(Wo),
        "bo": np.ascontiguousarray(np.asarray(inputs["L0_ffn_bo"], f32).reshape(1, NCH, 128)),
        "we": np.ascontiguousarray(We),
        "wout": woutT,
        "g1c": np.ascontiguousarray(np.asarray(inputs["L0_ffn_gamma1"], f32).reshape(NCH, 128).T),
        "g2c": np.ascontiguousarray(np.asarray(inputs["L0_ffn_gamma2"], f32).reshape(NCH, 128).T),
        "bqc": np.ascontiguousarray(np.asarray(inputs["L0_ffn_bq"], f32).reshape(NCH, 128).T),
        "bdc": np.ascontiguousarray(np.asarray(inputs["L0_ffn_bdown"], f32).reshape(64, 1)),
        "bupc": np.ascontiguousarray(bup.reshape(16, 128).T),
        "bgc": np.ascontiguousarray(np.asarray(inputs["L0_ffn_bg"], f32).reshape(8, 1)),
    }
    in_maps = []
    for c in range(8):
        b, half = c // 2, c % 2
        m = dict(com)
        m["xt"] = np.ascontiguousarray(x[b, half * NT:(half + 1) * NT, :].T)
        cols = np.concatenate([np.arange(half * 512, half * 512 + 512),
                               1024 + np.arange(half * 512, half * 512 + 512),
                               2048 + np.arange(half * 512, half * 512 + 512)])
        m["wint"] = np.ascontiguousarray(wintT[:, cols])
        m["ridx"] = np.array([[half]], np.int32)
        in_maps.append(m)
    return in_maps


def kernel(**inputs):
    in_maps = _prep_inputs(inputs)
    nc = build()
    res = run_bass_kernel_spmd(nc, in_maps, core_ids=list(range(8)))
    out = np.empty((4, T, D), np.float32)
    for c in range(8):
        b, half = c // 2, c % 2
        out[b, half * NT:(half + 1) * NT, :] = res.results[c]["out"].T
    return out


# version 02
# speedup vs baseline: 1.1831x; 1.1831x over previous
import numpy as np
import ml_dtypes
import concourse.bass as bass
import concourse.mybir as mybir
from concourse.bass_utils import run_bass_kernel_spmd

F32 = mybir.dt.float32
BF16 = mybir.dt.bfloat16
U8 = mybir.dt.uint8
AF = mybir.ActivationFunctionType
ALU = mybir.AluOpType
BFdt = ml_dtypes.bfloat16

D = 1024
NCH = 8
NT = 2048
SL = 128
NSL = NT // SL
ZSL = 512
NZS = NT // ZSL
T = 4096
HL = 8
CPT = 12.0
CBIG = 24.0
CG = 8.0
SQB = 20.0


class Sem:
    def __init__(self, nc, name):
        self.h = nc.alloc_semaphore(name=name)
        self.n = 0

    def bump(self, k=1):
        self.n += k

    def now(self):
        return (self, self.n)


class Sched:
    def __init__(self):
        self.ops = []

    def op(self, eng, fn, waits=(), inc=None, inck=1):
        w = []
        for x in waits:
            if x is None:
                continue
            if isinstance(x, list):
                w.extend(y for y in x if y is not None)
            else:
                w.append(x)
        self.ops.append((eng, fn, w, (inc, inck) if inc is not None else None))
        if inc is not None:
            inc.bump(inck)
            return inc.now()
        return None

    def replay(self, eng_name, handle):
        for eng, fn, waits, post in self.ops:
            if eng != eng_name:
                continue
            seen = {}
            for (sem, val) in waits:
                if val > 0 and seen.get(sem, -1) < val:
                    seen[sem] = val
            for sem, val in seen.items():
                handle.wait_ge(sem.h, val)
            inst = fn(handle)
            if post is not None:
                sem, k = post
                if inst is None:
                    inst = handle.nop()
                inst.then_inc(sem.h, k)


def build(dbg=(), parts=3, nocrow=False):
    nc = bass.Bass("TRN2", target_bir_lowering=False)

    def din(name, shape, dtype=F32):
        return nc.dram_tensor(name, list(shape), dtype, kind="ExternalInput").ap()

    xt_in = din("xt", (D, NT))
    wqd_in = din("wqd", (D, 1088))
    wup_in = din("wup", (64, 2048))
    wg_in = din("wg", (D, 8))
    wo_in = din("wo", (D, D))
    bo_in = din("bo", (1, NCH, 128))
    we_in = din("we", (8, D, D))
    wint_in = din("wint", (D, 1536))
    wout_in = din("wout", (D, D))
    g1_in = din("g1c", (128, 8))
    g2_in = din("g2c", (128, 8))
    bq_in = din("bqc", (128, 8))
    bd_in = din("bdc", (64, 1))
    bup_in = din("bupc", (128, 16))
    bg_in = din("bgc", (8, 1))
    ridx_in = din("ridx", (1, 1), mybir.dt.int32)
    mrq_in = din("mrq", (9, 16 * SL))
    mrk_in = din("mrk", (9, 16 * SL))

    out_d = nc.dram_tensor("out", [D, NT], F32, kind="ExternalOutput").ap()
    dbg_d = {}
    for name, shape in dbg:
        dbg_d[name] = nc.dram_tensor("dbg_" + name, list(shape), F32,
                                     kind="ExternalOutput").ap()

    x_page = nc.dram_tensor("x_page", [128, NCH, NT], F32).ap()
    k_dram = nc.dram_tensor("k_dram", [512, T], BF16).ap()
    v_dram = nc.dram_tensor("v_dram", [HL, T, 128], BF16).ap()
    ln2own_d = nc.dram_tensor("ln2own", [D, NT], BF16).ap()
    ln2all_d = nc.dram_tensor("ln2all", [2 * D, NT], BF16).ap()
    attnown_d = nc.dram_tensor("attnown", [2, 512, NT], BF16).ap()
    attnall_d = nc.dram_tensor("attnall", [2, 2, 512, NT], BF16).ap()

    xar = nc.alloc_sbuf_tensor("xar", [128, NCH, NT], F32)
    h2ar = nc.alloc_sbuf_tensor("h2ar", [128, NCH, NT], BF16)
    wqdb = nc.alloc_sbuf_tensor("wqdb", [128, NCH, 1088], BF16)
    wupb = nc.alloc_sbuf_tensor("wupb", [64, 2048], BF16)
    wgb = nc.alloc_sbuf_tensor("wgb", [128, NCH, 8], BF16)
    woextb = nc.alloc_sbuf_tensor("woextb", [2, NCH, 128], BF16)
    rowsA = nc.alloc_sbuf_tensor("rowsA", [1, 2, NT], F32)
    mrlnb = nc.alloc_sbuf_tensor("mrlnb", [1, NT], BF16)
    g1t = nc.alloc_sbuf_tensor("g1t", [128, 8], F32)
    g2t = nc.alloc_sbuf_tensor("g2t", [128, 8], F32)
    bqt = nc.alloc_sbuf_tensor("bqt", [128, 8], F32)
    bdt = nc.alloc_sbuf_tensor("bdt", [64, 1], F32)
    bupt = nc.alloc_sbuf_tensor("bupt", [128, 16], F32)
    bgst = nc.alloc_sbuf_tensor("bgst", [8, 1], F32)
    cgt = nc.alloc_sbuf_tensor("cgt", [8, 1], F32)
    ridxt = nc.alloc_sbuf_tensor("ridxt", [1, 1], mybir.dt.int32)
    onesF = nc.alloc_sbuf_tensor("onesF", [128, 1], F32)
    onesB = nc.alloc_sbuf_tensor("onesB", [128, 1], BF16)
    ones1B = nc.alloc_sbuf_tensor("ones1B", [1, 128], BF16)
    ones1F = nc.alloc_sbuf_tensor("ones1F", [1, 128], F32)
    ones8F = nc.alloc_sbuf_tensor("ones8F", [8, 64], F32)
    consts = nc.alloc_sbuf_tensor("consts", [128, 8], F32)
    crow2 = nc.alloc_sbuf_tensor("crow2", [1, 4], F32)
    identB = nc.alloc_sbuf_tensor("identB", [64, 64], BF16)
    selB = nc.alloc_sbuf_tensor("selB", [8, 8, 128], BF16)
    scratch = nc.alloc_sbuf_tensor("scratch", [128, 65 * 1024], U8)

    PS = nc.alloc_psum_tensor("PS", [128, 8, 512], F32)

    def carve(off, shape, dtype):
        esz = 2 if dtype == BF16 else 4
        n = int(np.prod(shape[1:])) * esz
        ap = scratch[:, off:off + n].bitcast(dtype)
        if len(shape) == 3:
            ap = ap.rearrange("p (a b) -> p a b", a=shape[1])
        elif len(shape) == 4:
            ap = ap.rearrange("p (a b c) -> p a b c", a=shape[1], b=shape[2])
        elif len(shape) == 5:
            ap = ap.rearrange("p (a b c d) -> p a b c d", a=shape[1], b=shape[2],
                              c=shape[3])
        if shape[0] != 128:
            ap = ap[0:shape[0]]
        return ap

    A_xm = carve(0, (128, NCH, SL), F32)
    A_ht = carve(4096, (128, NCH, SL), BF16)
    A_q = carve(6144, (128, NCH, SL), BF16)
    A_dv = carve(8192, (64, SL), BF16)
    A_kt = carve(8448, (128, NCH, SL), BF16)
    A_vt = carve(10496, (128, NCH, SL), BF16)
    A_at = carve(12544, (128, NCH, SL), BF16)
    A_cr = carve(14592, (2, SL), BF16)
    A_tmp = carve(14848, (128, SL), F32)
    A_wo = carve(15360, (128, NCH, D), BF16)
    Qp = carve(31744, (73, 16 * SL), BF16)
    Kp = carve(35840, (73, 16 * SL), BF16)
    Vp = carve(39936, (64, 16 * SL), BF16)
    TEMP = carve(44032, (64, 16 * SL), BF16)
    A_sq = carve(48128, (128, 2, 1024), F32)
    PtT = carve(56320, (128, 2, 512), BF16)
    Vg = carve(58368, (128, 4, 128), BF16)
    Rt = carve(59392, (64, 512), F32)
    r2row = carve(61440, (1, SL), F32)
    nmb = carve(61952, (128, 1088), BF16)
    nmrQ = carve(64128, (1, 1088), BF16)
    rmrow = carve(0, (1, NT), F32)
    rtrow = carve(8192, (1, NT), F32)
    B_we = carve(0, (128, 2, NCH, D), BF16)
    B_z = carve(32768, (128, 2, NCH, ZSL), BF16)
    B_gall = carve(49152, (128, 8, ZSL), BF16)
    B_ge = carve(57344, (8, ZSL), F32)
    B_gt = carve(59392, (8, ZSL), BF16)
    RtB = carve(60416, (64, ZSL), F32)
    AT_lnp = carve(0, (128, 2, NCH, 512), BF16)
    AT_q = carve(16384, (128, 4, T), BF16)
    AT_kb = carve(49152, (128, 2, T), BF16)
    AT_vb = carve(0, (128, 2, 2, HL, 128), BF16)
    AT_ws = carve(8192, (128, 2, NCH, 128), BF16)
    AT_wm = carve(12288, (128, 128), BF16)
    AT_pr = carve(12544, (1, 512), BF16)
    VW = carve(49152, (128, NCH, 512), BF16)
    AT_vs = carve(0, (128, 2, 32, 128), BF16)
    KS = carve(49152, (128, T), BF16)
    BPt = carve(57344, (128, 2, 1024), BF16)
    RtA = carve(61440, (64, 512), F32)
    AT_ah = carve(63488, (64, 2, 512), BF16)
    AT_wout = carve(16384, (128, NCH, D), BF16)

    ln2full = xar[:, :, :].bitcast(BF16)
    at_slab = h2ar

    SM = Sem(nc, "SM")
    SV = Sem(nc, "SV")
    SS = Sem(nc, "SS")
    SG = Sem(nc, "SG")
    DL = Sem(nc, "DL")
    DS = Sem(nc, "DS")
    DW = Sem(nc, "DW")
    DK = Sem(nc, "DK")
    CC = Sem(nc, "CC")
    ESEM = {"v": SV, "s": SS, "g": SG}
    CPOP = {"v": "tensor_copy", "s": "copy", "g": "tensor_copy"}

    sch = Sched()

    def mm(out, lhsT, rhs, start, stop, waits=(), inc=None, tp=None):
        return sch.op("t", lambda e, o=out, l=lhsT, r=rhs, a=start, b=stop, t_=tp:
                      e.matmul(o, l, r, start=a, stop=b, tile_position=t_),
                      waits, inc)

    def transp(out, in_, ident, waits=(), inc=None):
        return sch.op("t", lambda e, o=out, i=in_, d=ident: e.transpose(o, i, d),
                      waits, inc)

    def act(out, in_, func, bias=0.0, scale=1.0, accum=None, waits=(), inc=None):
        return sch.op("s", lambda e, o=out, i=in_, f=func, b=bias, s=scale, a=accum:
                      e.activation(o, i, f, bias=b, scale=s, accum_out=a),
                      waits, inc)

    def vop(eng, name, *args, waits=(), inc=None, **kw):
        def fn(e, n=name, a=args, k=kw):
            return getattr(e, n)(*a, **k)
        return sch.op(eng, fn, waits, inc)

    def dma(eng, out, in_, waits=(), inc=None):
        return sch.op(eng, lambda e, o=out, i=in_: e.dma_start(out=o, in_=i),
                      waits, inc, inck=16)

    def barrier():
        snaps = [SM.now(), SV.now(), SS.now(), SG.now(), DL.now(), DS.now(),
                 DW.now(), DK.now()]
        outs = []
        outs.append(sch.op("t", lambda e: e.nop(), snaps, SM))
        outs.append(sch.op("v", lambda e: e.nop(), snaps, SV))
        outs.append(sch.op("s", lambda e: e.nop(), snaps, SS))
        outs.append(sch.op("g", lambda e: e.nop(), snaps, SG))
        return outs

    def dump(name, ap, waits):
        if name in dbg_d:
            return dma("g", dbg_d[name], ap, waits=waits, inc=DS)
        return None

    x_ld = dma("y", xar[:, :, :], xt_in.rearrange("(c p) n -> p c n", p=128), inc=DL)
    wqd_ld = dma("g", wqdb[:, :, :], wqd_in.rearrange("(c p) n -> p c n", p=128), inc=DW)
    wup_ld = dma("g", wupb[0:64, :], wup_in, inc=DW)
    wg_ld = dma("g", wgb[:, :, :], wg_in.rearrange("(c p) n -> p c n", p=128), inc=DW)

    for dst, src, p in ((g1t, g1_in, 128), (g2t, g2_in, 128), (bqt, bq_in, 128),
                        (bdt, bd_in, 64), (bupt, bup_in, 128), (bgst, bg_in, 8)):
        dma("y", dst[0:p, :], src, inc=DL)
    ridx_ld = dma("y", ridxt[0:1, :], ridx_in, inc=DL)
    w_small = DL.now()

    def ms(ap, val, waits=(), inc=None):
        return vop("g", "memset", ap, val, waits=waits, inc=inc)

    ms(onesF[:, :], 1.0)
    ms(onesB[:, :], 1.0)
    ms(ones1B[0:1, :], 1.0)
    ms(ones1F[0:1, :], 1.0)
    ms(ones8F[0:8, :], 1.0)
    for col, val in ((0, 0.125), (1, -CPT), (2, -CBIG), (3, 0.5), (4, 1.0 / D),
                     (5, 1e-5), (6, 1e-6), (7, 0.0)):
        ms(consts[:, col:col + 1], val)
    ms(crow2[0:1, 0:1], -1e-5)
    ms(crow2[0:1, 1:2], 1.0 + 1e-6)
    ms(crow2[0:1, 2:3], 0.0)
    ms(crow2[0:1, 3:4], -0.5)
    ms(cgt[0:8, :], -CG)
    wx_ms = ms(woextb[0:2, :, :], -1.0)
    bo_ld = dma("g", woextb[1:2, :, :], bo_in, waits=[wx_ms], inc=DW)
    ms(identB[0:64, :], 0.0)
    vop("g", "affine_select", identB[0:64, :], identB[0:64, :],
        pattern=[[-1, 64]], compare_op=ALU.not_equal, fill=1.0, base=0,
        channel_multiplier=1)
    mr_ld1 = dma("g", Qp[64:73, :], mrq_in, inc=DW)
    mr_ld2 = dma("g", Kp[64:73, :], mrk_in, inc=DW)
    ms(Vg[:, :, :], 1.0)
    ms(selB[0:8, :, :], 0.0)
    vop("g", "affine_select", selB[0:8, :, :].rearrange("p a b -> p (a b)"),
        selB[0:8, :, :].rearrange("p a b -> p (a b)"),
        pattern=[[-1, 8], [0, 128]], compare_op=ALU.not_equal, fill=1.0, base=0,
        channel_multiplier=1)
    w_consts = sch.op("g", lambda e: e.nop(), [], SG)

    bg_sh = vop("v", "tensor_tensor", bgst[0:8, :], bgst[0:8, :], cgt[0:8, :],
                ALU.add, waits=[w_small, w_consts], inc=SV)

    wv = None
    for c in range(NCH):
        wv = vop("v", "tensor_scalar", wqdb[:, c, :], wqdb[:, c, :],
                 g1t[:, c:c + 1], None, ALU.mult,
                 waits=[wqd_ld, w_small] if c == 0 else (),
                 inc=SV if c == NCH - 1 else None)
    cs_mm = []
    for fo in range(3):
        lo, hi = fo * 512, min(fo * 512 + 512, 1088)
        last = None
        for c in range(NCH):
            last = mm(PS[0:1, fo, 0:hi - lo], onesB[:, :], wqdb[:, c, lo:hi],
                      start=(c == 0), stop=(c == NCH - 1),
                      waits=[wv] if c == 0 else (), inc=SM if c == NCH - 1 else None)
        cs_mm.append(last)
    nmr = nmrQ[0:1, :]
    nm_ops = []
    for fo in range(3):
        lo, hi = fo * 512, min(fo * 512 + 512, 1088)
        nm_ops.append(vop("v", "tensor_scalar", nmr[:, lo:hi], PS[0:1, fo, 0:hi - lo],
                          consts[0:1, 4:5], None, ALU.mult, waits=[cs_mm[fo]], inc=SV))
    wbc = []
    for fo in range(3):
        lo, hi = fo * 512, min(fo * 512 + 512, 1088)
        g = mm(PS[:, 3, 0:hi - lo], ones1B[0:1, :], nmr[:, lo:hi],
               start=True, stop=True, waits=[nm_ops[fo], wbc[-1] if wbc else None],
               inc=SM)
        wbc.append(vop("v", "tensor_copy", nmb[:, lo:hi], PS[:, 3, 0:hi - lo],
                       waits=[g], inc=SV))
    wprep = None
    for c in range(NCH):
        wprep = vop("v", "tensor_tensor", wqdb[:, c, :], wqdb[:, c, :],
                    nmb[:, :], ALU.subtract, waits=[wbc[-1]] if c == 0 else (),
                    inc=SV if c == NCH - 1 else None)

    def ln_stats(x_ready, r_rln, r_R1, guards=()):
        sqd = {}
        s1_mm = s2_mm = None
        for c in range(NCH):
            for hh in range(2):
                pp = (2 * c + hh) % 2
                sq = act(A_sq[:, pp, :], xar[:, c, hh * 1024:(hh + 1) * 1024],
                         AF.Square, waits=[x_ready, sqd.get(pp)] + list(guards), inc=SS)
                for q in range(2):
                    qq = hh * 2 + q
                    s2_mm = mm(PS[0:1, 4 + qq, :], onesF[:, :],
                               A_sq[:, pp, q * 512:(q + 1) * 512],
                               start=(c == 0), stop=(c == NCH - 1),
                               waits=[sq] if q == 0 else (),
                               inc=SM if q == 1 else None)
                sqd[pp] = s2_mm
            for q in range(4):
                s1_mm = mm(PS[0:1, q, :], onesF[:, :], xar[:, c, q * 512:(q + 1) * 512],
                           waits=([x_ready] + list(guards)) if c == 0 and q == 0 else (),
                           start=(c == 0), stop=(c == NCH - 1),
                           inc=SM if c == NCH - 1 and q == 3 else None)
        S1 = PS[0:1, 0:4, :].rearrange("p a b -> p (a b)")
        S2 = PS[0:1, 4:8, :].rearrange("p a b -> p (a b)")
        r_m = rmrow[0:1, :]
        r_t = rtrow[0:1, :]
        a1 = vop("v", "tensor_scalar", r_m, S1, consts[0:1, 4:5], None, ALU.mult,
                 waits=[s1_mm], inc=SV)
        a2 = vop("v", "tensor_tensor", r_t, r_m, r_m, ALU.mult, waits=[a1], inc=SV)
        a3 = vop("v", "scalar_tensor_tensor", r_rln, S2, consts[0:1, 4:5], r_t,
                 ALU.mult, ALU.subtract, waits=[s2_mm, a2], inc=SV)
        a4 = act(r_t, r_rln, AF.Ln, bias=consts[0:1, 5:6], waits=[a3], inc=SS)
        a5 = act(r_rln, r_t, AF.Exp, bias=crow2[0:1, 2:3], scale=crow2[0:1, 3:4],
                 waits=[a4], inc=SS)
        done = a5
        if r_R1 is not None:
            b1 = vop("v", "tensor_tensor", r_t, r_rln, r_rln, ALU.mult, waits=[a5], inc=SV)
            b2 = vop("v", "tensor_scalar", r_t, r_t, crow2[0:1, 0:1], crow2[0:1, 1:2],
                     ALU.mult, ALU.add, waits=[b1], inc=SV)
            b3 = act(r_t, r_t, AF.Ln, bias=crow2[0:1, 2:3], waits=[b2], inc=SS)
            b4 = act(r_t, r_t, AF.Exp, bias=crow2[0:1, 2:3], scale=crow2[0:1, 3:4],
                     waits=[b3], inc=SS)
            b5 = vop("v", "tensor_tensor", r_R1, r_rln, r_t, ALU.mult, waits=[b4], inc=SV)
            b6 = vop("v", "tensor_tensor", mrlnb[0:1, :], r_m, r_rln, ALU.mult,
                     waits=[b5], inc=SV)
            done = b6
        return done, a3

    def emit_ffn(call, x_ready):
        r_R1 = rowsA[0:1, 0, :]
        r_rln = rowsA[0:1, 1, :]
        mr1 = dma("g", Qp[64:73, :], mrq_in, waits=[x_ready], inc=DW)
        mr2 = dma("g", Kp[64:73, :], mrk_in, waits=[x_ready], inc=DW)
        mrows = [mr1, mr2]
        vg_ms = vop("g", "memset", Vg[:, :, :], 1.0, waits=[x_ready], inc=SG)
        rows_done, stats_rel = ln_stats(x_ready, r_rln, r_R1, guards=[wprep])
        wo_ld = dma("g", A_wo[:, :, :], wo_in.rearrange("(c p) n -> p c n", p=128),
                    waits=[x_ready, rows_done], inc=DW)
        dump(f"wqdb_{call}", wqdb[:, :, :].rearrange("p a b -> p (a b)"), [rows_done])
        dump(f"wupb_{call}", wupb[0:64, :], [rows_done])
        dump(f"awo_{call}", A_wo[:, :, :].rearrange("p a b -> p (a b)"), [wo_ld])
        dump(f"qkp_{call}", Kp[0:73, :], [mrows, SG.now(), SV.now(), SS.now()])
        dump(f"rows_{call}", rowsA[0:1, :, :].rearrange("p a b -> p (a b)"),
             [rows_done])

        bank4_free = [stats_rel]
        r2guard = [stats_rel]
        b56_free = [stats_rel, stats_rel]
        b7_free = [stats_rel]
        b03_free = [stats_rel]
        sq_guard = [stats_rel, stats_rel]
        prev_tmp = None
        exp_guard = [None, None]
        av_guard = [None, None]
        tr_guard = [None, None]
        dv_guard = [vg_ms, vg_ms]

        for s in range(NSL):
            t0 = s * SL
            tsl = slice(t0, t0 + SL)
            b_r1 = mm(PS[:, 4, 0:SL], ones1F[0:1, :], r_R1[:, t0:t0 + SL],
                      start=True, stop=True, waits=[rows_done] + bank4_free, inc=SM)
            hts = None
            for c in range(NCH):
                hts = vop("v", "scalar_tensor_tensor", A_ht[:, c, :], xar[:, c, tsl],
                          onesF[:, :], PS[:, 4, 0:SL], ALU.mult, ALU.mult,
                          waits=[b_r1] if c == 0 else (),
                          inc=SV if c == NCH - 1 else None)
            bank4_free = [hts]
            if s == 0:
                dump(f"ht_{call}", A_ht[:, :, :].rearrange("p a b -> p (a b)"), [hts])
            ev = {}
            for fo in range(9):
                bk = 5 + fo % 2
                fl = fo * 128
                fw = 64 if fo == 8 else 128
                mml = None
                for c in range(NCH):
                    mml = mm(PS[0:fw, bk, 0:SL], wqdb[:, c, fl:fl + fw], A_ht[:, c, :],
                             start=(c == 0), stop=(c == NCH - 1),
                             waits=[hts, wprep, b56_free[fo % 2]] if c == 0 else (),
                             inc=SM if c == NCH - 1 else None)
                if fo == 8:
                    e = act(A_dv[0:64, :], PS[0:64, bk, 0:SL], AF.Identity,
                            bias=bdt[0:64, :], waits=[mml], inc=SS)
                else:
                    e = act(A_q[:, fo, :], PS[:, bk, 0:SL], AF.Identity,
                            bias=bqt[:, fo:fo + 1], waits=[mml], inc=SS)
                ev[fo] = e
                b56_free[fo % 2] = e
            kev = {}
            for fo in range(16):
                bk = 5 + fo % 2
                mml = mm(PS[:, bk, 0:SL], wupb[0:64, fo * 128:(fo + 1) * 128],
                         A_dv[0:64, :], start=True, stop=True,
                         waits=[ev[8], b56_free[fo % 2]], inc=SM)
                dst = A_kt[:, fo, :] if fo < 8 else A_vt[:, fo - 8, :]
                e = act(dst, PS[:, bk, 0:SL], AF.Identity, bias=bupt[:, fo:fo + 1],
                        waits=[mml], inc=SS)
                kev[fo] = e
                b56_free[fo % 2] = e
            q_done = [ev[i] for i in range(8)]
            kv_done = [kev[i] for i in range(16)]
            if s == 0:
                dump(f"q_{call}", A_q[:, :, :].rearrange("p a b -> p (a b)"), q_done)
                dump(f"kt_{call}", A_kt[:, :, :].rearrange("p a b -> p (a b)"), kv_done)
            engs = ["v", "s", "g"]
            for h in range(16):
                eng = engs[h % 3]
                for (src, dstt) in (
                        (A_q[(h % 2) * 64:(h % 2) * 64 + 64, h // 2, :], Qp),
                        (A_kt[(h % 2) * 64:(h % 2) * 64 + 64, h // 2, :], Kp),
                        (A_vt[(h % 2) * 64:(h % 2) * 64 + 64, h // 2, :], Vp)):
                    dst = dstt[0:64, :].rearrange("p (a b) -> p a b", b=16)[:, :, h]
                    vop(eng, CPOP[eng], dst, src,
                        waits=q_done + kv_done + [w_consts] + b03_free,
                        inc=ESEM[eng])
            builds = [SV.now(), SS.now(), SG.now()]
            NR = 16 * SL // 128 // 4
            ptv_all = PS[:, 6:8, :].rearrange("p a b -> p (a b)").bitcast(BF16)
            last_div = None
            for r in range(NR):
                bS = r % 2
                bO = 2 + r % 2
                qkl = None
                for gi in range(4):
                    g_ = r * 4 + gi
                    col = slice(g_ * 128, (g_ + 1) * 128)
                    qkl = mm(PS[:, bS, gi * 128:(gi + 1) * 128], Kp[0:73, col],
                             Qp[0:73, col], start=(gi == 0), stop=(gi == 3),
                             waits=builds + mrows + [exp_guard[bS]] if gi == 0 else (),
                             inc=SM if gi == 3 else None)
                trl = None
                for gi in range(4):
                    g_ = r * 4 + gi
                    col = slice(g_ * 128, (g_ + 1) * 128)
                    ptv = ptv_all[:, (r % 2) * 1024 + gi * 64:(r % 2) * 1024 + gi * 64 + 64]
                    trl = transp(ptv, Vp[0:64, col], identB[0:64, :],
                                 waits=builds + [tr_guard[r % 2]] if gi == 0 else (),
                                 inc=SM if gi == 3 else None)
                ex = act(PtT[:, r % 2, :], PS[:, bS, :], AF.Exp,
                         bias=consts[:, 1:2], scale=consts[:, 0:1],
                         waits=[qkl, av_guard[r % 2]], inc=SS)
                exp_guard[bS] = ex
                evs = None
                for gi in range(4):
                    src = ptv_all[:, (r % 2) * 1024 + gi * 64:(r % 2) * 1024 + gi * 64 + 64]
                    evs = vop("v", "tensor_copy", Vg[:, gi, 0:64], src,
                              waits=[trl, av_guard[r % 2]] if gi == 0 else (),
                              inc=SV if gi == 3 else None)
                tr_guard[r % 2] = evs
                avl = None
                for gi in range(4):
                    avl = mm(PS[:, bO, gi * 128:(gi + 1) * 128], Vg[:, gi, :],
                             PtT[:, r % 2, gi * 128:(gi + 1) * 128],
                             start=(gi == 0), stop=(gi == 3),
                             waits=[evs, ex, dv_guard[r % 2]] if gi == 0 else (),
                             inc=SM if gi == 3 else None)
                av_guard[r % 2] = avl
                rc = vop("v", "reciprocal", Rt[0:64, :], PS[64:128, bO, :],
                         waits=[avl], inc=SV)
                last_div = vop("v", "tensor_tensor", TEMP[0:64, r * 512:(r + 1) * 512],
                               PS[0:64, bO, :], Rt[0:64, :], ALU.mult,
                               waits=[rc], inc=SV)
                dv_guard[r % 2] = last_div
            b03_free = [exp_guard[0], exp_guard[1], dv_guard[0], dv_guard[1]]
            if s == 0:
                dump(f"temp_{call}", TEMP[0:64, :], [last_div])
            ocs = []
            for h in range(16):
                eng = engs[h % 3]
                src = TEMP[0:64, :].rearrange("p (a b) -> p a b", b=16)[:, :, h]
                dst = A_at[(h % 2) * 64:(h % 2) * 64 + 64, h // 2, :]
                ocs.append(vop(eng, CPOP[eng], dst, src, waits=[last_div],
                               inc=ESEM[eng]))
            oc_done = [SV.now(), SS.now(), SG.now()]
            cr0 = vop("g", "memset", A_cr[0:2, :], 1.0,
                      waits=oc_done, inc=SG)
            cr1 = vop("v", "tensor_copy", A_cr[0:1, :], mrlnb[0:1, tsl],
                      waits=[rows_done, cr0], inc=SV)
            rlb = mm(PS[:, 4, 0:SL], ones1F[0:1, :], r_rln[:, t0:t0 + SL],
                     start=True, stop=True, waits=[rows_done] + bank4_free, inc=SM)
            xm = {}
            for fo in range(NCH):
                bk = 5 + fo % 2
                for c in range(NCH):
                    mml = mm(PS[:, bk, 0:SL], A_wo[:, c, fo * 128:(fo + 1) * 128],
                             A_at[:, c, :], start=(c == 0),
                             stop=(nocrow and c == NCH - 1),
                             waits=oc_done + [wo_ld, b56_free[fo % 2]] if c == 0 else (),
                             inc=SM if (nocrow and c == NCH - 1) else None)
                if not nocrow:
                    mml = mm(PS[:, bk, 0:SL], woextb[0:2, fo, :], A_cr[0:2, :],
                             start=False, stop=True, waits=[cr0, cr1], inc=SM)
                t1 = vop("v", "tensor_tensor", A_tmp[:, :], xar[:, fo, tsl],
                         PS[:, 4, 0:SL], ALU.mult, waits=[rlb] if fo == 0 else (),
                         inc=SV)
                xm[fo] = vop("v", "tensor_tensor", A_xm[:, fo, :], A_tmp[:, :],
                             PS[:, bk, 0:SL], ALU.add, waits=[mml, t1], inc=SV)
                b56_free[fo % 2] = xm[fo]
            bank4_free = [xm[NCH - 1]]
            if s == 0:
                dsnap = dump(f"xmid_{call}", A_xm[:, :, :].rearrange("p a b -> p (a b)"),
                             [xm[i] for i in range(NCH)])
                if dsnap is not None:
                    bank4_free = bank4_free + [dsnap]
                    for _k in range(2):
                        b56_free[_k] = [b56_free[_k], dsnap]
            s2m = None
            for c in range(NCH):
                sq = act(A_sq[:, c % 2, 0:SL], A_xm[:, c, :], AF.Square,
                         waits=[xm[c], sq_guard[c % 2]], inc=SS)
                s2m = mm(PS[0:1, 7, 0:SL], onesF[:, :], A_sq[:, c % 2, 0:SL],
                         start=(c == 0), stop=(c == NCH - 1),
                         waits=[sq] + (b7_free if c == 0 else []),
                         inc=SM)
                sq_guard[c % 2] = s2m
            r2s = act(r2row[0:1, :], PS[0:1, 7, 0:SL], AF.Ln,
                      bias=consts[0:1, 6:7], scale=consts[0:1, 4:5],
                      waits=[s2m, r2guard[0]], inc=SS)
            b7_free = [r2s]
            r2r = act(r2row[0:1, :], r2row[0:1, :], AF.Exp,
                      bias=crow2[0:1, 2:3], scale=crow2[0:1, 3:4],
                      waits=[r2s], inc=SS)
            r2b = mm(PS[:, 4, 0:SL], ones1F[0:1, :], r2row[0:1, :],
                     start=True, stop=True, waits=[r2r] + bank4_free, inc=SM)
            r2guard[0] = r2b
            h2l = None
            tmpl = None
            for c in range(NCH):
                h2l = vop("v", "scalar_tensor_tensor", h2ar[:, c, tsl], A_xm[:, c, :],
                          g2t[:, c:c + 1], PS[:, 4, 0:SL], ALU.mult, ALU.mult,
                          waits=[r2b] if c == 0 else (),
                          inc=SV if c == NCH - 1 else None)
            for c in range(NCH):
                tmpl = vop("v", "scalar_tensor_tensor", xar[:, c, tsl], A_xm[:, c, :],
                           consts[:, 3:4], xar[:, c, tsl], ALU.mult, ALU.add,
                           waits=[h2l] if c == 0 else (),
                           inc=SV if c == NCH - 1 else None)
            bank4_free = [h2l]
            prev_tmp = tmpl
        bar = barrier()
        dump(f"h2_{call}", h2ar[:, 0, :], bar)
        x1l = None
        we_ld = [None, None]
        first_zs_waits = bar
        for zs in range(NZS):
            z0 = zs * ZSL
            zsl = slice(z0, z0 + ZSL)
            gl = None
            for c in range(NCH):
                gl = mm(PS[0:8, 0, 0:ZSL], wgb[:, c, :], h2ar[:, c, zsl],
                        start=(c == 0), stop=(c == NCH - 1),
                        waits=first_zs_waits + [wg_ld] if c == 0 else (),
                        inc=SM if c == NCH - 1 else None)
            gex = act(B_ge[0:8, :], PS[0:8, 0, 0:ZSL], AF.Exp, bias=bgst[0:8, :],
                      waits=[gl, bg_sh], inc=SS)
            gsum = mm(PS[0:64, 1, 0:ZSL], ones8F[0:8, :], B_ge[0:8, :],
                      start=True, stop=True, waits=[gex], inc=SM)
            grc = vop("v", "reciprocal", RtB[0:64, :], PS[0:64, 1, 0:ZSL],
                      waits=[gsum], inc=SV)
            gT = vop("v", "tensor_tensor", B_gt[0:8, :], B_ge[0:8, :], RtB[0:8, :],
                     ALU.mult, waits=[grc], inc=SV)
            if zs == 0:
                dump(f"gate_{call}", B_gt[0:8, :], [gT])
            gevs = []
            gbl = [grc, grc]
            for e in range(8):
                bk = e % 2
                gb = mm(PS[:, bk, 0:ZSL], selB[0:8, e, :], B_gt[0:8, :],
                        start=True, stop=True, waits=[gT, gbl[bk], w_consts], inc=SM)
                gv = vop("v", "tensor_copy", B_gall[:, e, :], PS[:, bk, 0:ZSL],
                         waits=[gb], inc=SV)
                gbl[bk] = gv
                gevs.append(gv)
            mm_last = None
            ze_rel = [None, None]
            we_rel = getattr(emit_ffn, "_we_rel", [None, None])
            for e in range(8):
                wb = e % 2
                we_ld[wb] = dma("g", B_we[:, wb, :, :],
                                we_in[e].rearrange("(c p) n -> p c n", p=128),
                                waits=[we_ld[wb], we_rel[wb],
                                       bar[0] if zs == 0 and e < 2 else None],
                                inc=DW)
                zl = None
                for c in range(NCH):
                    zl = vop("v", "tensor_tensor", B_z[:, wb, c, :], h2ar[:, c, zsl],
                             B_gall[:, e, :], ALU.mult,
                             waits=[gevs[e], ze_rel[wb]] if c == 0 else (),
                             inc=SV if c == NCH - 1 else None)
                ze_rel[wb] = None
                for fo in range(NCH):
                    for c in range(NCH):
                        mm_last = mm(PS[:, fo, 0:ZSL],
                                     B_we[:, wb, c, fo * 128:(fo + 1) * 128],
                                     B_z[:, wb, c, :],
                                     start=(e == 0 and c == 0),
                                     stop=(e == 7 and c == NCH - 1),
                                     waits=[zl, we_ld[wb]] + ([gevs[-1]] if e == 0 else [])
                                     if c == 0 else (),
                                     inc=SM if c == NCH - 1 and fo == NCH - 1 else None)
                ze_rel[wb] = mm_last
                we_rel[wb] = mm_last
            emit_ffn._we_rel = we_rel
            x1c = []
            for fo in range(NCH):
                x1l = vop("v", "scalar_tensor_tensor", xar[:, fo, zsl],
                          PS[:, fo, 0:ZSL], consts[:, 3:4], xar[:, fo, zsl],
                          ALU.mult, ALU.add, waits=[mm_last, prev_tmp], inc=SV)
                x1c.append(x1l)
            first_zs_waits = x1c
        dump(f"x1_{call}", xar[:, 0, :], [x1l])
        return x1l

    def emit_attn(x1_done):
        bar0 = barrier()
        r_rln = rowsA[0:1, 0, :]
        rows_done, stats_rel = ln_stats(x1_done, r_rln, None)
        lnp_rel = [stats_rel, stats_rel]
        ln_dma = []
        bcast_rel = [stats_rel, stats_rel]
        for qd in range(4):
            pp = qd % 2
            rb = mm(PS[:, pp, :], ones1F[0:1, :], r_rln[:, qd * 512:(qd + 1) * 512],
                    start=True, stop=True, waits=[rows_done, bcast_rel[pp]], inc=SM)
            lnl = None
            for c in range(NCH):
                lnl = vop("v", "tensor_tensor", AT_lnp[:, pp, c, :],
                          xar[:, c, qd * 512:(qd + 1) * 512], PS[:, pp, :],
                          ALU.mult, waits=[rb, lnp_rel[pp]] if c == 0 else (),
                          inc=SV if c == NCH - 1 else None)
            bcast_rel[pp] = lnl
            d = dma("y", ln2own_d.rearrange("(c p) (a n) -> p a c n", p=128, a=4)[:, qd, :, :],
                    AT_lnp[:, pp, :, :], waits=[lnl], inc=DS)
            lnp_rel[pp] = d
            ln_dma.append(d)
        d2 = dma("y", x_page[:, :, :], xar[:, :, :], waits=[x1_done], inc=DS)
        ag1 = sch.op("g", lambda e: e.collective_compute(
            "AllGather", ALU.bypass,
            replica_groups=[[0, 1], [2, 3], [4, 5], [6, 7]],
            ins=[ln2own_d[:, :].opt()], outs=[ln2all_d[:, :].opt()]),
            ln_dma + [bar0[3]], CC)
        d3a = dma("y", ln2full[:, :, 0:NT],
                  ln2all_d[0:D, :].rearrange("(c p) n -> p c n", p=128),
                  waits=[ag1, d2], inc=DL)
        d3 = dma("y", ln2full[:, :, NT:2 * NT],
                 ln2all_d[D:2 * D, :].rearrange("(c p) n -> p c n", p=128),
                 waits=[ag1, d2], inc=DL)
        d3 = [d3a, d3]
        dump("ln2full", ln2full[:, 0, :], [d3])
        bar_lp = barrier()

        wm_rel = [stats_rel]
        wsl_rel = [None, None]
        b01_rel = [bcast_rel[0], bcast_rel[1]]
        kb_rel = [None, None]
        k_dma = []
        qk_last = None

        def prep_128slab(dst_ws, pp, fo):
            ld = dma("g", dst_ws,
                     wint_in[:, fo * 128:(fo + 1) * 128].rearrange("(c p) n -> p c n", p=128),
                     waits=[wsl_rel[pp]] + (ln_dma if fo < 2 else []), inc=DW)
            csl = None
            for c in range(NCH):
                csl = mm(PS[0:1, 6, 0:128], onesB[:, :], dst_ws[:, c, :],
                         start=(c == 0), stop=(c == NCH - 1),
                         waits=[ld] + wm_rel if c == 0 else (),
                         inc=SM if c == NCH - 1 else None)
            nm = vop("v", "tensor_scalar", AT_pr[0:1, 0:128], PS[0:1, 6, 0:128],
                     consts[0:1, 4:5], None, ALU.mult, waits=[csl], inc=SV)
            nb = mm(PS[:, 6, 0:128], ones1B[0:1, :], AT_pr[0:1, 0:128],
                    start=True, stop=True, waits=[nm], inc=SM)
            nbe = vop("v", "tensor_copy", AT_wm[:, :], PS[:, 6, 0:128],
                      waits=[nb], inc=SV)
            wm_rel[0] = nbe
            wp = None
            for c in range(NCH):
                wp = vop("v", "tensor_tensor", dst_ws[:, c, :], dst_ws[:, c, :],
                         AT_wm[:, :], ALU.subtract, waits=[nbe] if c == 0 else (),
                         inc=SV if c == NCH - 1 else None)
            return wp

        for fo in range(8):
            pp = fo % 2
            wsl = AT_ws[:, pp, :, :]
            wp = prep_128slab(wsl, pp, fo)
            last_e = None
            for tf in range(8):
                bk = tf % 2
                mml = None
                for c in range(NCH):
                    mml = mm(PS[:, bk, :], wsl[:, c, :],
                             ln2full[:, c, tf * 512:(tf + 1) * 512],
                             start=(c == 0), stop=(c == NCH - 1),
                             waits=[wp, d3, b01_rel[bk]] if c == 0 else (),
                             inc=SM if c == NCH - 1 else None)
                if fo < 4:
                    last_e = vop("s", "copy", AT_q[:, fo, tf * 512:(tf + 1) * 512],
                                 PS[:, bk, :], waits=[mml], inc=SS)
                else:
                    last_e = vop("s", "copy", AT_kb[:, pp, tf * 512:(tf + 1) * 512],
                                 PS[:, bk, :],
                                 waits=[mml] + ([kb_rel[pp]] if tf == 0 else []),
                                 inc=SS)
                b01_rel[bk] = last_e
            wsl_rel[pp] = last_e
            qk_last = last_e
            if fo >= 4:
                dkk = dma("y", k_dram[(fo - 4) * 128:(fo - 3) * 128, :], AT_kb[:, pp, :],
                          waits=[last_e], inc=DS)
                kb_rel[pp] = dkk
                k_dma.append(dkk)
        vw_ld = dma("g", VW[:, :, :],
                    wint_in[:, 1024:1536].rearrange("(c p) n -> p c n", p=128),
                    waits=[d3, k_dma[2], kb_rel[0]], inc=DW)
        csl = None
        for c in range(NCH):
            csl = mm(PS[0:1, 6, 0:512], onesB[:, :], VW[:, c, :],
                     start=(c == 0), stop=(c == NCH - 1),
                     waits=[vw_ld] + wm_rel if c == 0 else (),
                     inc=SM if c == NCH - 1 else None)
        nm = vop("v", "tensor_scalar", AT_pr[0:1, 0:512], PS[0:1, 6, 0:512],
                 consts[0:1, 4:5], None, ALU.mult, waits=[csl], inc=SV)
        nbe = None
        for half in range(2):
            nb = mm(PS[:, 6, 0:256], ones1B[0:1, :],
                    AT_pr[0:1, half * 256:half * 256 + 256],
                    start=True, stop=True, waits=[nm, nbe], inc=SM)
            nbe = vop("v", "tensor_copy", nmb[:, half * 256:half * 256 + 256],
                      PS[:, 6, 0:256], waits=[nb], inc=SV)
        vwp = None
        for c in range(NCH):
            vwp = vop("v", "tensor_tensor", VW[:, c, :], VW[:, c, :], nmb[:, 0:512],
                      ALU.subtract, waits=[nbe] if c == 0 else (),
                      inc=SV if c == NCH - 1 else None)
        vones = ms(AT_vb[:, :, :, :, :], 1.0, waits=ln_dma, inc=SG)
        v_dma = []
        b23_rel = [None, None]
        vb_rel = [None, None]
        for tc in range(32):
            pb = (tc // 2) % 2
            sub = tc % 2
            bk = 2 + tc % 2
            mml = None
            for c in range(NCH):
                mml = mm(PS[:, bk, :], ln2full[:, c, tc * 128:(tc + 1) * 128],
                         VW[:, c, :], start=(c == 0), stop=(c == NCH - 1),
                         waits=[vwp, d3, b23_rel[tc % 2]] + ([vones] if tc < 2 else [])
                         if c == 0 else (), inc=SM if c == NCH - 1 else None)
            evv = vop("v", "tensor_copy", AT_vb[:, pb, sub, :, 0:64],
                      PS[:, bk, :].rearrange("p (a b) -> p a b", a=8),
                      waits=[mml] + ([vb_rel[pb]] if sub == 0 else []), inc=SV)
            b23_rel[tc % 2] = evv
            vd = dma("y",
                     v_dram.rearrange("h (a p) f -> p a h f", p=128)[:, tc, :, :],
                     AT_vb[:, pb, sub, :, :], waits=[evv], inc=DS)
            if sub == 1:
                vb_rel[pb] = vd
            v_dma.append(vd)
        x_back = dma("y", xar[:, :, :], x_page[:, :, :],
                     waits=[qk_last, vwp, v_dma[-1], SM.now()], inc=DL)

        dump("atq", AT_q[:, 0, :], [qk_last, SM.now(), SS.now()])
        bar1 = barrier()
        ks_rel = [None, None]
        vs_rel = [None, None]
        exp_g = [None, None]
        av_g = [None, None]
        psO_rel = [None, None]
        bpt_rel = [None, None]
        ah_rel = {}
        ao_dma = []
        for hp in range(4):
            kss = dma("s", KS[:, :], k_dram[hp * 128:(hp + 1) * 128, :],
                      waits=[k_dma[hp], ks_rel[0]] + (bar1 if hp == 0 else []),
                      inc=DK)
            if hp == 0:
                ksd = dump("ks0", KS[0:64, :], [kss])
                if ksd is not None:
                    ks_rel[0] = [ks_rel[0], ksd] if ks_rel[0] else [ksd]
            vsd = []
            for h01 in range(2):
                vsd.append(dma("s", AT_vs[:, h01, :, :],
                               v_dram[2 * hp + h01, :, :].rearrange("(a p) f -> p a f", p=128),
                               waits=[v_dma[-1], vs_rel[h01]] + (bar1 if hp == 0 else []),
                               inc=DK))
            hp_av = None
            for qc in range(8):
                nk = (qc + 1) * 4
                avv = [None, None]
                pend = None
                for kc in range(nk + 1):
                    mask_w = None
                    if kc < nk:
                        bS = (kc % 2) * 2
                        qkk = None
                        for h01 in range(2):
                            qkk = mm(PS[:, bS + h01, :],
                                     KS[h01 * 64:h01 * 64 + 64,
                                        kc * 128:(kc + 1) * 128],
                                     AT_q[h01 * 64:h01 * 64 + 64, hp,
                                          qc * 512:(qc + 1) * 512],
                                     start=True, stop=True,
                                     waits=[kss, exp_g[kc % 2]] if h01 == 0 else (),
                                     inc=SM if h01 == 1 else None,
                                     tp=(64, 0) if h01 == 1 else None)
                        ex = act(BPt[:, kc % 2, :],
                                 PS[:, bS:bS + 2, :].rearrange("p a b -> p (a b)"),
                                 AF.Exp, bias=consts[:, 2:3], scale=consts[:, 0:1],
                                 waits=[qkk, bpt_rel[kc % 2]], inc=SS)
                        exp_g[kc % 2] = ex
                        mask_w = [ex]
                        if kc >= qc * 4:
                            afl = None
                            for h01 in range(2):
                                afl = vop("g", "affine_select",
                                          BPt[:, kc % 2, h01 * 512:(h01 + 1) * 512],
                                          BPt[:, kc % 2, h01 * 512:(h01 + 1) * 512],
                                          pattern=[[1, 512]], compare_op=ALU.is_ge,
                                          fill=0.0, base=qc * 512 - kc * 128,
                                          channel_multiplier=-1,
                                          waits=[ex] if h01 == 0 else (),
                                          inc=SG if h01 == 1 else None)
                            mask_w = [afl]
                    if pend is not None:
                        pkc, pmask = pend
                        for h01 in range(2):
                            avv[h01] = mm(PS[:, 4 + h01, :],
                                          AT_vs[:, h01, pkc, :],
                                          BPt[:, pkc % 2, h01 * 512:(h01 + 1) * 512],
                                          start=(pkc == 0), stop=(pkc == nk - 1),
                                          waits=pmask + vsd +
                                          ([psO_rel[h01]] if pkc == 0 else []),
                                          inc=SM if h01 == 1 or pkc == nk - 1 else None)
                        bpt_rel[pkc % 2] = avv[1]
                    if kc < nk:
                        pend = (kc, mask_w)
                hp_av = avv[1]
                for h01 in range(2):
                    rc = vop("v", "reciprocal", RtA[0:64, :], PS[64:128, 4 + h01, :],
                             waits=[avv[h01], ah_rel.get(h01)], inc=SV)
                    dv = vop("v", "tensor_tensor", AT_ah[0:64, h01, :],
                             PS[0:64, 4 + h01, :], RtA[0:64, :], ALU.mult,
                             waits=[rc], inc=SV)
                    psO_rel[h01] = dv
                    half = qc // 4
                    tl0 = qc * 512 - half * NT
                    add = dma("y",
                              attnown_d[half, (2 * hp + h01) * 64:(2 * hp + h01 + 1) * 64,
                                        tl0:tl0 + 512],
                              AT_ah[0:64, h01, :], waits=[dv], inc=DS)
                    ah_rel[h01] = add
                    ao_dma.append(add)
            ks_rel[0] = hp_av
            vs_rel[0] = hp_av
            vs_rel[1] = hp_av
        a2a = sch.op("g", lambda e: e.collective_compute(
            "AllGather", ALU.bypass,
            replica_groups=[[0, 1], [2, 3], [4, 5], [6, 7]],
            ins=[attnown_d.rearrange("a b c -> (a b) c").opt()],
            outs=[attnall_d.rearrange("r a b c -> (r a b) c").opt()]),
            ao_dma, CC)
        dump("attnown", attnown_d[0, 0:64, :], [a2a])
        dump("kdram", k_dram[0:64, 0:2048], [a2a])
        dump("vdram", v_dram[0, 0:2048, :].rearrange("(a b) f -> a (b f)", b=16), [a2a])
        dump("attnall", attnall_d[0, 0, 0:64, :], [a2a])

        bar2 = barrier()
        wout_ld = dma("g", AT_wout[:, :, :],
                      wout_in.rearrange("(c p) n -> p c n", p=128),
                      waits=bar2, inc=DW)
        slab4 = at_slab[:, :, :].rearrange("p (r c) n -> p r c n", r=2)

        def slab_fn(e):
            reg = e.alloc_register("rreg")
            e.load(reg, ridxt[0:1, 0:1])
            for mh in range(2):
                ctx = e.If_eq(reg, 0) if mh == 0 else e.Else()
                with ctx:
                    for r in range(2):
                        e.dma_start(out=slab4[:, r, :, :],
                                    in_=attnall_d[r, mh, :, :].rearrange("(c p) n -> p c n", p=128)
                                    ).then_inc(DL.h, 16)
            return None
        sch.op("y", slab_fn, [a2a, ridx_ld] + bar2, None)
        DL.bump(32)
        sl_ld = DL.now()
        dump("slab", at_slab[:, :, :].rearrange("p a b -> p (a b)"), [sl_ld])
        dump("wout2", AT_wout[:, :, :].rearrange("p a b -> p (a b)"), [wout_ld])
        dump("xrest", xar[:, 0, :], [x_back])
        x2l = None
        b_rel = [bar2[1], bar2[1]]
        for fo in range(NCH):
            for tf in range(4):
                bk = (fo * 4 + tf) % 2
                mml = None
                for c in range(NCH):
                    mml = mm(PS[:, bk, :], AT_wout[:, c, fo * 128:(fo + 1) * 128],
                             at_slab[:, c, tf * 512:(tf + 1) * 512],
                             start=(c == 0), stop=(c == NCH - 1),
                             waits=[wout_ld, sl_ld, x_back, b_rel[bk]] if c == 0 else (),
                             inc=SM if c == NCH - 1 else None)
                x2l = vop("v", "tensor_tensor", xar[:, fo, tf * 512:(tf + 1) * 512],
                          xar[:, fo, tf * 512:(tf + 1) * 512], PS[:, bk, :],
                          ALU.add, waits=[mml], inc=SV)
                b_rel[bk] = x2l
        dump("x2", xar[:, 0, :], [x2l])
        dump("wout3", AT_wout[:, :, :].rearrange("p a b -> p (a b)"), [x2l])
        dump("slab3", at_slab[:, :, :].rearrange("p a b -> p (a b)"), [x2l])
        return x2l

    x1 = emit_ffn(0, x_ld)
    last = x1
    if parts >= 2:
        last = emit_attn(x1)
    if parts >= 3:
        barx = barrier()
        last = emit_ffn(1, barx[1])
    od = dma("y", out_d.rearrange("(c p) n -> p c n", p=128), xar[:, :, :],
             waits=[last], inc=DS)
    sch.op("y", lambda e: e.nop(), [od, DS.now()], None)

    with nc.Block() as block:
        @block.tensor
        def _(e):
            sch.replay("t", e)

        @block.scalar
        def _(e):
            sch.replay("s", e)

        @block.vector
        def _(e):
            sch.replay("v", e)

        @block.gpsimd
        def _(e):
            sch.replay("g", e)

        @block.sync
        def _(e):
            sch.replay("y", e)

    return nc


_NCACHE = {}


def _prep_inputs(inputs):
    f32 = np.float32
    Wq = np.asarray(inputs["L0_ffn_Wq"], f32)
    Wdown = np.asarray(inputs["L0_ffn_Wdown"], f32)
    Wup = np.asarray(inputs["L0_ffn_Wup"], f32)
    Wg = np.asarray(inputs["L0_ffn_Wg"], f32)
    Wo = np.asarray(inputs["L0_ffn_Wo"], f32)
    We = np.asarray(inputs["L0_ffn_We"], f32)
    Win = np.asarray(inputs["L0_attn_in_proj_weight"], f32)
    Wout = np.asarray(inputs["L0_attn_out_proj_weight"], f32)
    x = np.asarray(inputs["L0_attn_query"], f32)

    wqd = np.ascontiguousarray(np.concatenate([Wq, Wdown], axis=1))
    idx = np.arange(2048).reshape(16, 2, 64)
    perm = np.concatenate([idx[:, 0, :].reshape(-1), idx[:, 1, :].reshape(-1)])
    wup = np.ascontiguousarray(Wup[:, perm])
    bup = np.asarray(inputs["L0_ffn_bup"], f32)[perm]
    wintT = np.ascontiguousarray(Win.T)
    woutT = np.ascontiguousarray(Wout.T)

    mr = np.zeros((9, 16 * 128), np.float32)
    sub = np.arange(16 * 128)
    blk = (sub // 16) % 8
    for j in range(8):
        mr[j, blk == j] = SQB
    mr[8, :] = SQB
    mrq = mr.copy()
    mrq[8, :] = -SQB
    com = {
        "mrq": mrq,
        "mrk": mr,
        "wqd": wqd,
        "wup": wup,
        "wg": np.ascontiguousarray(Wg),
        "wo": np.ascontiguousarray(Wo),
        "bo": np.ascontiguousarray(np.asarray(inputs["L0_ffn_bo"], f32).reshape(1, NCH, 128)),
        "we": np.ascontiguousarray(We),
        "wout": woutT,
        "g1c": np.ascontiguousarray(np.asarray(inputs["L0_ffn_gamma1"], f32).reshape(NCH, 128).T),
        "g2c": np.ascontiguousarray(np.asarray(inputs["L0_ffn_gamma2"], f32).reshape(NCH, 128).T),
        "bqc": np.ascontiguousarray(np.asarray(inputs["L0_ffn_bq"], f32).reshape(NCH, 128).T),
        "bdc": np.ascontiguousarray(np.asarray(inputs["L0_ffn_bdown"], f32).reshape(64, 1)),
        "bupc": np.ascontiguousarray(bup.reshape(16, 128).T),
        "bgc": np.ascontiguousarray(np.asarray(inputs["L0_ffn_bg"], f32).reshape(8, 1)),
    }
    in_maps = []
    for c in range(8):
        b, half = c // 2, c % 2
        m = dict(com)
        m["xt"] = np.ascontiguousarray(x[b, half * NT:(half + 1) * NT, :].T)
        cols = np.concatenate([np.arange(half * 512, half * 512 + 512),
                               1024 + np.arange(half * 512, half * 512 + 512),
                               2048 + np.arange(half * 512, half * 512 + 512)])
        m["wint"] = np.ascontiguousarray(wintT[:, cols])
        m["ridx"] = np.array([[half]], np.int32)
        in_maps.append(m)
    return in_maps


def kernel(**inputs):
    in_maps = _prep_inputs(inputs)
    nc = build()
    res = run_bass_kernel_spmd(nc, in_maps, core_ids=list(range(8)))
    out = np.empty((4, T, D), np.float32)
    for c in range(8):
        b, half = c // 2, c % 2
        out[b, half * NT:(half + 1) * NT, :] = res.results[c]["out"].T
    return out


# version 03
# speedup vs baseline: 1.2143x; 1.0263x over previous
import numpy as np
import ml_dtypes
import concourse.bass as bass
import concourse.mybir as mybir
from concourse.bass_utils import run_bass_kernel_spmd

F32 = mybir.dt.float32
BF16 = mybir.dt.bfloat16
U8 = mybir.dt.uint8
AF = mybir.ActivationFunctionType
ALU = mybir.AluOpType
BFdt = ml_dtypes.bfloat16

D = 1024
NCH = 8
NT = 2048
SL = 128
NSL = NT // SL
ZSL = 512
NZS = NT // ZSL
T = 4096
HL = 8
CPT = 12.0
CBIG = 24.0
CG = 8.0
SQB = 20.0


class Sem:
    def __init__(self, nc, name):
        self.h = nc.alloc_semaphore(name=name)
        self.n = 0

    def bump(self, k=1):
        self.n += k

    def now(self):
        return (self, self.n)


class Sched:
    def __init__(self):
        self.ops = []

    def op(self, eng, fn, waits=(), inc=None, inck=1):
        w = []
        for x in waits:
            if x is None:
                continue
            if isinstance(x, list):
                w.extend(y for y in x if y is not None)
            else:
                w.append(x)
        self.ops.append((eng, fn, w, (inc, inck) if inc is not None else None))
        if inc is not None:
            inc.bump(inck)
            return inc.now()
        return None

    def replay(self, eng_name, handle):
        for eng, fn, waits, post in self.ops:
            if eng != eng_name:
                continue
            seen = {}
            for (sem, val) in waits:
                if val > 0 and seen.get(sem, -1) < val:
                    seen[sem] = val
            for sem, val in seen.items():
                handle.wait_ge(sem.h, val)
            inst = fn(handle)
            if post is not None:
                sem, k = post
                if inst is None:
                    inst = handle.nop()
                inst.then_inc(sem.h, k)


def build(dbg=(), parts=3, nocrow=False):
    nc = bass.Bass("TRN2", target_bir_lowering=False)

    def din(name, shape, dtype=F32):
        return nc.dram_tensor(name, list(shape), dtype, kind="ExternalInput").ap()

    xt_in = din("xt", (D, NT))
    wqd_in = din("wqd", (D, 1088))
    wup_in = din("wup", (64, 2048))
    wg_in = din("wg", (D, 8))
    wo_in = din("wo", (D, D))
    bo_in = din("bo", (1, NCH, 128))
    we_in = din("we", (8, D, D))
    wint_in = din("wint", (D, 1536))
    wout_in = din("wout", (D, D))
    g1_in = din("g1c", (128, 8))
    g2_in = din("g2c", (128, 8))
    bq_in = din("bqc", (128, 8))
    bd_in = din("bdc", (64, 1))
    bup_in = din("bupc", (128, 16))
    bg_in = din("bgc", (8, 1))
    ridx_in = din("ridx", (1, 1), mybir.dt.int32)
    mrq_in = din("mrq", (9, 16 * SL))
    mrk_in = din("mrk", (9, 16 * SL))

    out_d = nc.dram_tensor("out", [D, NT], F32, kind="ExternalOutput").ap()
    dbg_d = {}
    for name, shape in dbg:
        dbg_d[name] = nc.dram_tensor("dbg_" + name, list(shape), F32,
                                     kind="ExternalOutput").ap()

    x_page = nc.dram_tensor("x_page", [128, NCH, NT], F32).ap()
    k_dram = nc.dram_tensor("k_dram", [512, T], BF16).ap()
    v_dram = nc.dram_tensor("v_dram", [HL, T, 128], BF16).ap()
    ln2own_d = nc.dram_tensor("ln2own", [D, NT], BF16).ap()
    ln2all_d = nc.dram_tensor("ln2all", [2 * D, NT], BF16).ap()
    attnown_d = nc.dram_tensor("attnown", [2, 512, NT], BF16).ap()
    attnall_d = nc.dram_tensor("attnall", [2, 2, 512, NT], BF16).ap()

    xar = nc.alloc_sbuf_tensor("xar", [128, NCH, NT], F32)
    h2ar = nc.alloc_sbuf_tensor("h2ar", [128, NCH, NT], BF16)
    wqdb = nc.alloc_sbuf_tensor("wqdb", [128, NCH, 1088], BF16)
    wupb = nc.alloc_sbuf_tensor("wupb", [64, 2048], BF16)
    wgb = nc.alloc_sbuf_tensor("wgb", [128, NCH, 8], BF16)
    woextb = nc.alloc_sbuf_tensor("woextb", [2, NCH, 128], BF16)
    rowsA = nc.alloc_sbuf_tensor("rowsA", [1, 2, NT], F32)
    mrlnb = nc.alloc_sbuf_tensor("mrlnb", [1, NT], BF16)
    g1t = nc.alloc_sbuf_tensor("g1t", [128, 8], F32)
    g2t = nc.alloc_sbuf_tensor("g2t", [128, 8], F32)
    bqt = nc.alloc_sbuf_tensor("bqt", [128, 8], F32)
    bdt = nc.alloc_sbuf_tensor("bdt", [64, 1], F32)
    bupt = nc.alloc_sbuf_tensor("bupt", [128, 16], F32)
    bgst = nc.alloc_sbuf_tensor("bgst", [8, 1], F32)
    cgt = nc.alloc_sbuf_tensor("cgt", [8, 1], F32)
    ridxt = nc.alloc_sbuf_tensor("ridxt", [1, 1], mybir.dt.int32)
    onesF = nc.alloc_sbuf_tensor("onesF", [128, 1], F32)
    onesB = nc.alloc_sbuf_tensor("onesB", [128, 1], BF16)
    ones1B = nc.alloc_sbuf_tensor("ones1B", [1, 128], BF16)
    ones1F = nc.alloc_sbuf_tensor("ones1F", [1, 128], F32)
    ones8F = nc.alloc_sbuf_tensor("ones8F", [8, 64], F32)
    consts = nc.alloc_sbuf_tensor("consts", [128, 8], F32)
    crow2 = nc.alloc_sbuf_tensor("crow2", [1, 4], F32)
    identB = nc.alloc_sbuf_tensor("identB", [64, 64], BF16)
    selB = nc.alloc_sbuf_tensor("selB", [8, 8, 128], BF16)
    scratch = nc.alloc_sbuf_tensor("scratch", [128, 65 * 1024], U8)

    PS = nc.alloc_psum_tensor("PS", [128, 8, 512], F32)

    def carve(off, shape, dtype):
        esz = 2 if dtype == BF16 else 4
        n = int(np.prod(shape[1:])) * esz
        ap = scratch[:, off:off + n].bitcast(dtype)
        if len(shape) == 3:
            ap = ap.rearrange("p (a b) -> p a b", a=shape[1])
        elif len(shape) == 4:
            ap = ap.rearrange("p (a b c) -> p a b c", a=shape[1], b=shape[2])
        elif len(shape) == 5:
            ap = ap.rearrange("p (a b c d) -> p a b c d", a=shape[1], b=shape[2],
                              c=shape[3])
        if shape[0] != 128:
            ap = ap[0:shape[0]]
        return ap

    A_xm = carve(0, (128, NCH, SL), F32)
    A_ht = carve(4096, (128, NCH, SL), BF16)
    A_q = carve(6144, (128, NCH, SL), BF16)
    A_dv = carve(8192, (64, SL), BF16)
    A_kt = carve(8448, (128, NCH, SL), BF16)
    A_vt = carve(10496, (128, NCH, SL), BF16)
    A_at = carve(12544, (128, NCH, SL), BF16)
    A_cr = carve(14592, (2, SL), BF16)
    A_tmp = carve(14848, (128, SL), F32)
    A_wo = carve(15360, (128, NCH, D), BF16)
    Qp = carve(31744, (73, 16 * SL), BF16)
    Kp = carve(35840, (73, 16 * SL), BF16)
    Vp = carve(39936, (64, 16 * SL), BF16)
    TEMP = carve(44032, (64, 16 * SL), BF16)
    A_sq = carve(48128, (128, 2, 1024), F32)
    PtT = carve(56320, (128, 2, 512), BF16)
    Vg = carve(58368, (128, 4, 128), BF16)
    Rt = carve(59392, (128, 512), F32)
    r2row = carve(61440, (1, SL), F32)
    nmb = carve(61952, (128, 1088), BF16)
    nmrQ = carve(64128, (1, 1088), BF16)
    rmrow = carve(0, (1, NT), F32)
    rtrow = carve(8192, (1, NT), F32)
    B_we = carve(0, (128, 2, NCH, D), BF16)
    B_z = carve(32768, (128, 2, NCH, ZSL), BF16)
    B_gall = carve(49152, (128, 8, ZSL), BF16)
    B_ge = carve(57344, (8, ZSL), F32)
    B_gt = carve(59392, (8, ZSL), BF16)
    RtB = carve(60416, (128, ZSL), F32)
    AT_lnp = carve(0, (128, 2, NCH, 512), BF16)
    AT_q = carve(16384, (128, 4, T), BF16)
    AT_kb = carve(49152, (128, 2, T), BF16)
    AT_vb = carve(0, (128, 2, 2, HL, 128), BF16)
    AT_ws = carve(8192, (128, 2, NCH, 128), BF16)
    AT_wm = carve(12288, (128, 128), BF16)
    AT_pr = carve(12544, (1, 512), BF16)
    VW = carve(49152, (128, NCH, 512), BF16)
    AT_vs = carve(0, (128, 2, 32, 128), BF16)
    KS = carve(49152, (128, T), BF16)
    BPt = carve(57344, (128, 2, 1024), BF16)
    RtA = carve(61440, (128, 512), F32)
    AT_ah = carve(63488, (64, 2, 512), BF16)
    RtA2 = carve(63488, (128, 512), F32)
    AT_wout = carve(16384, (128, NCH, D), BF16)

    ln2full = xar[:, :, :].bitcast(BF16)
    at_slab = h2ar

    SM = Sem(nc, "SM")
    SV = Sem(nc, "SV")
    SS = Sem(nc, "SS")
    SG = Sem(nc, "SG")
    DL = Sem(nc, "DL")
    DS = Sem(nc, "DS")
    DW = Sem(nc, "DW")
    DK = Sem(nc, "DK")
    CC = Sem(nc, "CC")
    ESEM = {"v": SV, "s": SS, "g": SG}
    CPOP = {"v": "tensor_copy", "s": "copy", "g": "tensor_copy"}

    sch = Sched()

    def mm(out, lhsT, rhs, start, stop, waits=(), inc=None, tp=None):
        return sch.op("t", lambda e, o=out, l=lhsT, r=rhs, a=start, b=stop, t_=tp:
                      e.matmul(o, l, r, start=a, stop=b, tile_position=t_),
                      waits, inc)

    def transp(out, in_, ident, waits=(), inc=None):
        return sch.op("t", lambda e, o=out, i=in_, d=ident: e.transpose(o, i, d),
                      waits, inc)

    def act(out, in_, func, bias=0.0, scale=1.0, accum=None, waits=(), inc=None):
        return sch.op("s", lambda e, o=out, i=in_, f=func, b=bias, s=scale, a=accum:
                      e.activation(o, i, f, bias=b, scale=s, accum_out=a),
                      waits, inc)

    def vop(eng, name, *args, waits=(), inc=None, **kw):
        def fn(e, n=name, a=args, k=kw):
            return getattr(e, n)(*a, **k)
        return sch.op(eng, fn, waits, inc)

    def dma(eng, out, in_, waits=(), inc=None):
        return sch.op(eng, lambda e, o=out, i=in_: e.dma_start(out=o, in_=i),
                      waits, inc, inck=16)

    def barrier():
        snaps = [SM.now(), SV.now(), SS.now(), SG.now(), DL.now(), DS.now(),
                 DW.now(), DK.now()]
        outs = []
        outs.append(sch.op("t", lambda e: e.nop(), snaps, SM))
        outs.append(sch.op("v", lambda e: e.nop(), snaps, SV))
        outs.append(sch.op("s", lambda e: e.nop(), snaps, SS))
        outs.append(sch.op("g", lambda e: e.nop(), snaps, SG))
        return outs

    def dump(name, ap, waits):
        if name in dbg_d:
            return dma("g", dbg_d[name], ap, waits=waits, inc=DS)
        return None

    x_ld = dma("y", xar[:, :, :], xt_in.rearrange("(c p) n -> p c n", p=128), inc=DL)
    wqd_ld = dma("g", wqdb[:, :, :], wqd_in.rearrange("(c p) n -> p c n", p=128), inc=DW)
    wup_ld = dma("g", wupb[0:64, :], wup_in, inc=DW)
    wg_ld = dma("g", wgb[:, :, :], wg_in.rearrange("(c p) n -> p c n", p=128), inc=DW)

    for dst, src, p in ((g1t, g1_in, 128), (g2t, g2_in, 128), (bqt, bq_in, 128),
                        (bdt, bd_in, 64), (bupt, bup_in, 128), (bgst, bg_in, 8)):
        dma("y", dst[0:p, :], src, inc=DL)
    ridx_ld = dma("y", ridxt[0:1, :], ridx_in, inc=DL)
    w_small = DL.now()

    def ms(ap, val, waits=(), inc=None):
        return vop("g", "memset", ap, val, waits=waits, inc=inc)

    ms(onesF[:, :], 1.0)
    ms(onesB[:, :], 1.0)
    ms(ones1B[0:1, :], 1.0)
    ms(ones1F[0:1, :], 1.0)
    ms(ones8F[0:8, :], 1.0)
    for col, val in ((0, 0.125), (1, -CPT), (2, -CBIG), (3, 0.5), (4, 1.0 / D),
                     (5, 1e-5), (6, 1e-6), (7, 0.0)):
        ms(consts[:, col:col + 1], val)
    ms(crow2[0:1, 0:1], -1e-5)
    ms(crow2[0:1, 1:2], 1.0 + 1e-6)
    ms(crow2[0:1, 2:3], 0.0)
    ms(crow2[0:1, 3:4], -0.5)
    ms(cgt[0:8, :], -CG)
    wx_ms = ms(woextb[0:2, :, :], -1.0)
    bo_ld = dma("g", woextb[1:2, :, :], bo_in, waits=[wx_ms], inc=DW)
    ms(identB[0:64, :], 0.0)
    vop("g", "affine_select", identB[0:64, :], identB[0:64, :],
        pattern=[[-1, 64]], compare_op=ALU.not_equal, fill=1.0, base=0,
        channel_multiplier=1)
    mr_ld1 = dma("g", Qp[64:73, :], mrq_in, inc=DW)
    mr_ld2 = dma("g", Kp[64:73, :], mrk_in, inc=DW)
    ms(Vg[:, :, :], 1.0)
    ms(selB[0:8, :, :], 0.0)
    vop("g", "affine_select", selB[0:8, :, :].rearrange("p a b -> p (a b)"),
        selB[0:8, :, :].rearrange("p a b -> p (a b)"),
        pattern=[[-1, 8], [0, 128]], compare_op=ALU.not_equal, fill=1.0, base=0,
        channel_multiplier=1)
    w_consts = sch.op("g", lambda e: e.nop(), [], SG)

    bg_sh = vop("v", "tensor_tensor", bgst[0:8, :], bgst[0:8, :], cgt[0:8, :],
                ALU.add, waits=[w_small, w_consts], inc=SV)

    wv = None
    for c in range(NCH):
        wv = vop("v", "tensor_scalar", wqdb[:, c, :], wqdb[:, c, :],
                 g1t[:, c:c + 1], None, ALU.mult,
                 waits=[wqd_ld, w_small] if c == 0 else (),
                 inc=SV if c == NCH - 1 else None)
    cs_mm = []
    for fo in range(3):
        lo, hi = fo * 512, min(fo * 512 + 512, 1088)
        last = None
        for c in range(NCH):
            last = mm(PS[0:1, fo, 0:hi - lo], onesB[:, :], wqdb[:, c, lo:hi],
                      start=(c == 0), stop=(c == NCH - 1),
                      waits=[wv] if c == 0 else (), inc=SM if c == NCH - 1 else None)
        cs_mm.append(last)
    nmr = nmrQ[0:1, :]
    nm_ops = []
    for fo in range(3):
        lo, hi = fo * 512, min(fo * 512 + 512, 1088)
        nm_ops.append(vop("v", "tensor_scalar", nmr[:, lo:hi], PS[0:1, fo, 0:hi - lo],
                          consts[0:1, 4:5], None, ALU.mult, waits=[cs_mm[fo]], inc=SV))
    wbc = []
    for fo in range(3):
        lo, hi = fo * 512, min(fo * 512 + 512, 1088)
        g = mm(PS[:, 3, 0:hi - lo], ones1B[0:1, :], nmr[:, lo:hi],
               start=True, stop=True, waits=[nm_ops[fo], wbc[-1] if wbc else None],
               inc=SM)
        wbc.append(vop("v", "tensor_copy", nmb[:, lo:hi], PS[:, 3, 0:hi - lo],
                       waits=[g], inc=SV))
    wprep = None
    for c in range(NCH):
        wprep = vop("v", "tensor_tensor", wqdb[:, c, :], wqdb[:, c, :],
                    nmb[:, :], ALU.subtract, waits=[wbc[-1]] if c == 0 else (),
                    inc=SV if c == NCH - 1 else None)

    def ln_stats(x_ready, r_rln, r_R1, guards=()):
        sqd = {}
        s1_mm = s2_mm = None
        for c in range(NCH):
            for hh in range(2):
                pp = (2 * c + hh) % 2
                sq = act(A_sq[:, pp, :], xar[:, c, hh * 1024:(hh + 1) * 1024],
                         AF.Square, waits=[x_ready, sqd.get(pp)] + list(guards), inc=SS)
                for q in range(2):
                    qq = hh * 2 + q
                    s2_mm = mm(PS[0:1, 4 + qq, :], onesF[:, :],
                               A_sq[:, pp, q * 512:(q + 1) * 512],
                               start=(c == 0), stop=(c == NCH - 1),
                               waits=[sq] if q == 0 else (),
                               inc=SM if q == 1 else None)
                sqd[pp] = s2_mm
            for q in range(4):
                s1_mm = mm(PS[0:1, q, :], onesF[:, :], xar[:, c, q * 512:(q + 1) * 512],
                           waits=([x_ready] + list(guards)) if c == 0 and q == 0 else (),
                           start=(c == 0), stop=(c == NCH - 1),
                           inc=SM if c == NCH - 1 and q == 3 else None)
        S1 = PS[0:1, 0:4, :].rearrange("p a b -> p (a b)")
        S2 = PS[0:1, 4:8, :].rearrange("p a b -> p (a b)")
        r_m = rmrow[0:1, :]
        r_t = rtrow[0:1, :]
        a1 = vop("v", "tensor_scalar", r_m, S1, consts[0:1, 4:5], None, ALU.mult,
                 waits=[s1_mm], inc=SV)
        a2 = vop("v", "tensor_tensor", r_t, r_m, r_m, ALU.mult, waits=[a1], inc=SV)
        a3 = vop("v", "scalar_tensor_tensor", r_rln, S2, consts[0:1, 4:5], r_t,
                 ALU.mult, ALU.subtract, waits=[s2_mm, a2], inc=SV)
        a4 = act(r_t, r_rln, AF.Ln, bias=consts[0:1, 5:6], waits=[a3], inc=SS)
        a5 = act(r_rln, r_t, AF.Exp, bias=crow2[0:1, 2:3], scale=crow2[0:1, 3:4],
                 waits=[a4], inc=SS)
        done = a5
        if r_R1 is not None:
            b1 = vop("v", "tensor_tensor", r_t, r_rln, r_rln, ALU.mult, waits=[a5], inc=SV)
            b2 = vop("v", "tensor_scalar", r_t, r_t, crow2[0:1, 0:1], crow2[0:1, 1:2],
                     ALU.mult, ALU.add, waits=[b1], inc=SV)
            b3 = act(r_t, r_t, AF.Ln, bias=crow2[0:1, 2:3], waits=[b2], inc=SS)
            b4 = act(r_t, r_t, AF.Exp, bias=crow2[0:1, 2:3], scale=crow2[0:1, 3:4],
                     waits=[b3], inc=SS)
            b5 = vop("v", "tensor_tensor", r_R1, r_rln, r_t, ALU.mult, waits=[b4], inc=SV)
            b6 = vop("v", "tensor_tensor", mrlnb[0:1, :], r_m, r_rln, ALU.mult,
                     waits=[b5], inc=SV)
            done = b6
        return done, a3

    def emit_ffn(call, x_ready):
        r_R1 = rowsA[0:1, 0, :]
        r_rln = rowsA[0:1, 1, :]
        mr1 = dma("g", Qp[64:73, :], mrq_in, waits=[x_ready], inc=DW)
        mr2 = dma("g", Kp[64:73, :], mrk_in, waits=[x_ready], inc=DW)
        mrows = [mr1, mr2]
        vg_ms = vop("g", "memset", Vg[:, :, :], 1.0, waits=[x_ready], inc=SG)
        rows_done, stats_rel = ln_stats(x_ready, r_rln, r_R1, guards=[wprep])
        wo_ld = dma("g", A_wo[:, :, :], wo_in.rearrange("(c p) n -> p c n", p=128),
                    waits=[x_ready, rows_done], inc=DW)
        dump(f"wqdb_{call}", wqdb[:, :, :].rearrange("p a b -> p (a b)"), [rows_done])
        dump(f"wupb_{call}", wupb[0:64, :], [rows_done])
        dump(f"awo_{call}", A_wo[:, :, :].rearrange("p a b -> p (a b)"), [wo_ld])
        dump(f"qkp_{call}", Kp[0:73, :], [mrows, SG.now(), SV.now(), SS.now()])
        dump(f"rows_{call}", rowsA[0:1, :, :].rearrange("p a b -> p (a b)"),
             [rows_done])

        bank4_free = [stats_rel]
        r2guard = [stats_rel]
        b56_free = [stats_rel, stats_rel]
        b7_free = [stats_rel]
        b03_free = [stats_rel]
        sq_guard = [stats_rel, stats_rel]
        prev_tmp = None
        exp_guard = [None, None]
        av_guard = [None, None]
        tr_guard = [None, None]
        dv_guard = [vg_ms, vg_ms]

        for s in range(NSL):
            t0 = s * SL
            tsl = slice(t0, t0 + SL)
            b_r1 = mm(PS[:, 4, 0:SL], ones1F[0:1, :], r_R1[:, t0:t0 + SL],
                      start=True, stop=True, waits=[rows_done] + bank4_free, inc=SM)
            hts = None
            for c in range(NCH):
                hts = vop("v", "scalar_tensor_tensor", A_ht[:, c, :], xar[:, c, tsl],
                          onesF[:, :], PS[:, 4, 0:SL], ALU.mult, ALU.mult,
                          waits=[b_r1] if c == 0 else (),
                          inc=SV if c == NCH - 1 else None)
            bank4_free = [hts]
            if s == 0:
                dump(f"ht_{call}", A_ht[:, :, :].rearrange("p a b -> p (a b)"), [hts])
            ev = {}
            for fo in range(9):
                bk = 5 + fo % 2
                fl = fo * 128
                fw = 64 if fo == 8 else 128
                mml = None
                for c in range(NCH):
                    mml = mm(PS[0:fw, bk, 0:SL], wqdb[:, c, fl:fl + fw], A_ht[:, c, :],
                             start=(c == 0), stop=(c == NCH - 1),
                             waits=[hts, wprep, b56_free[fo % 2]] if c == 0 else (),
                             inc=SM if c == NCH - 1 else None)
                if fo == 8:
                    e = act(A_dv[0:64, :], PS[0:64, bk, 0:SL], AF.Identity,
                            bias=bdt[0:64, :], waits=[mml], inc=SS)
                else:
                    e = act(A_q[:, fo, :], PS[:, bk, 0:SL], AF.Identity,
                            bias=bqt[:, fo:fo + 1], waits=[mml], inc=SS)
                ev[fo] = e
                b56_free[fo % 2] = e
            kev = {}
            for fo in range(16):
                bk = 5 + fo % 2
                mml = mm(PS[:, bk, 0:SL], wupb[0:64, fo * 128:(fo + 1) * 128],
                         A_dv[0:64, :], start=True, stop=True,
                         waits=[ev[8], b56_free[fo % 2]], inc=SM)
                dst = A_kt[:, fo, :] if fo < 8 else A_vt[:, fo - 8, :]
                e = act(dst, PS[:, bk, 0:SL], AF.Identity, bias=bupt[:, fo:fo + 1],
                        waits=[mml], inc=SS)
                kev[fo] = e
                b56_free[fo % 2] = e
            q_done = [ev[i] for i in range(8)]
            kv_done = [kev[i] for i in range(16)]
            if s == 0:
                dump(f"q_{call}", A_q[:, :, :].rearrange("p a b -> p (a b)"), q_done)
                dump(f"kt_{call}", A_kt[:, :, :].rearrange("p a b -> p (a b)"), kv_done)
            engs = ["v", "g", "v", "s", "g", "v", "g", "v",
                    "g", "s", "v", "g", "v", "g", "v", "g"]
            for h in range(16):
                eng = engs[h]
                for (src, dstt) in (
                        (A_q[(h % 2) * 64:(h % 2) * 64 + 64, h // 2, :], Qp),
                        (A_kt[(h % 2) * 64:(h % 2) * 64 + 64, h // 2, :], Kp),
                        (A_vt[(h % 2) * 64:(h % 2) * 64 + 64, h // 2, :], Vp)):
                    dst = dstt[0:64, :].rearrange("p (a b) -> p a b", b=16)[:, :, h]
                    vop(eng, CPOP[eng], dst, src,
                        waits=q_done + kv_done + [w_consts] + b03_free,
                        inc=ESEM[eng])
            builds = [SV.now(), SS.now(), SG.now()]
            NR = 16 * SL // 128 // 4
            ptv_all = PS[:, 6:8, :].rearrange("p a b -> p (a b)").bitcast(BF16)
            last_div = None
            pendr = None
            for r in range(NR + 1):
                if r < NR:
                    bS = r % 2
                    qkl = None
                    for gi in range(4):
                        g_ = r * 4 + gi
                        col = slice(g_ * 128, (g_ + 1) * 128)
                        qkl = mm(PS[:, bS, gi * 128:(gi + 1) * 128], Kp[0:73, col],
                                 Qp[0:73, col], start=(gi == 0), stop=(gi == 3),
                                 waits=builds + mrows + [exp_guard[bS]] if gi == 0 else (),
                                 inc=SM if gi == 3 else None)
                    trl = None
                    for gi in range(4):
                        g_ = r * 4 + gi
                        col = slice(g_ * 128, (g_ + 1) * 128)
                        ptv = ptv_all[:, (r % 2) * 1024 + gi * 64:(r % 2) * 1024 + gi * 64 + 64]
                        trl = transp(ptv, Vp[0:64, col], identB[0:64, :],
                                     waits=builds + [tr_guard[r % 2]] if gi == 0 else (),
                                     inc=SM if gi == 3 else None)
                    ex = act(PtT[:, r % 2, :], PS[:, bS, :], AF.Exp,
                             bias=consts[:, 1:2], scale=consts[:, 0:1],
                             waits=[qkl, av_guard[r % 2]], inc=SS)
                    exp_guard[bS] = ex
                if pendr is not None:
                    p = pendr
                    bO = 2 + p % 2
                    avl = None
                    for gi in range(4):
                        avl = mm(PS[:, bO, gi * 128:(gi + 1) * 128], Vg[:, gi, :],
                                 PtT[:, p % 2, gi * 128:(gi + 1) * 128],
                                 start=(gi == 0), stop=(gi == 3),
                                 waits=[tr_guard[p % 2], exp_guard[p % 2],
                                        dv_guard[p % 2]] if gi == 0 else (),
                                 inc=SM if gi == 3 else None)
                    av_guard[p % 2] = avl
                    lnr = act(Rt[64:128, :], PS[64:128, bO, :], AF.Ln,
                              waits=[avl, last_div], inc=SS)
                    exr = act(Rt[64:128, :], Rt[64:128, :], AF.Exp, scale=-1.0,
                              waits=[lnr], inc=SS)
                    last_div = vop("v", "tensor_tensor",
                                   TEMP[0:64, p * 512:(p + 1) * 512],
                                   PS[0:64, bO, :], Rt[64:128, :], ALU.mult,
                                   waits=[exr], inc=SV)
                    dv_guard[p % 2] = last_div
                if r < NR:
                    evs = None
                    for gi in range(4):
                        src = ptv_all[:, (r % 2) * 1024 + gi * 64:(r % 2) * 1024 + gi * 64 + 64]
                        evs = vop("v", "tensor_copy", Vg[:, gi, 0:64], src,
                                  waits=[trl, av_guard[0], av_guard[1]] if gi == 0 else (),
                                  inc=SV if gi == 3 else None)
                    tr_guard[r % 2] = evs
                    pendr = r
            b03_free = [exp_guard[0], exp_guard[1], dv_guard[0], dv_guard[1]]
            if s == 0:
                dump(f"temp_{call}", TEMP[0:64, :], [last_div])
            ocs = []
            for h in range(16):
                eng = engs[h]
                src = TEMP[0:64, :].rearrange("p (a b) -> p a b", b=16)[:, :, h]
                dst = A_at[(h % 2) * 64:(h % 2) * 64 + 64, h // 2, :]
                ocs.append(vop(eng, CPOP[eng], dst, src, waits=[last_div],
                               inc=ESEM[eng]))
            oc_done = [SV.now(), SS.now(), SG.now()]
            cr0 = vop("g", "memset", A_cr[0:2, :], 1.0,
                      waits=oc_done, inc=SG)
            cr1 = vop("v", "tensor_copy", A_cr[0:1, :], mrlnb[0:1, tsl],
                      waits=[rows_done, cr0], inc=SV)
            rlb = mm(PS[:, 4, 0:SL], ones1F[0:1, :], r_rln[:, t0:t0 + SL],
                     start=True, stop=True, waits=[rows_done] + bank4_free, inc=SM)
            xm = {}
            for fo in range(NCH):
                bk = 5 + fo % 2
                for c in range(NCH):
                    mml = mm(PS[:, bk, 0:SL], A_wo[:, c, fo * 128:(fo + 1) * 128],
                             A_at[:, c, :], start=(c == 0),
                             stop=(nocrow and c == NCH - 1),
                             waits=oc_done + [wo_ld, b56_free[fo % 2]] if c == 0 else (),
                             inc=SM if (nocrow and c == NCH - 1) else None)
                if not nocrow:
                    mml = mm(PS[:, bk, 0:SL], woextb[0:2, fo, :], A_cr[0:2, :],
                             start=False, stop=True, waits=[cr0, cr1], inc=SM)
                t1 = vop("v", "tensor_tensor", A_tmp[:, :], xar[:, fo, tsl],
                         PS[:, 4, 0:SL], ALU.mult, waits=[rlb] if fo == 0 else (),
                         inc=SV)
                xm[fo] = vop("v", "tensor_tensor", A_xm[:, fo, :], A_tmp[:, :],
                             PS[:, bk, 0:SL], ALU.add, waits=[mml, t1], inc=SV)
                b56_free[fo % 2] = xm[fo]
            bank4_free = [xm[NCH - 1]]
            if s == 0:
                dsnap = dump(f"xmid_{call}", A_xm[:, :, :].rearrange("p a b -> p (a b)"),
                             [xm[i] for i in range(NCH)])
                if dsnap is not None:
                    bank4_free = bank4_free + [dsnap]
                    for _k in range(2):
                        b56_free[_k] = [b56_free[_k], dsnap]
            s2m = None
            for c in range(NCH):
                sq = act(A_sq[:, c % 2, 0:SL], A_xm[:, c, :], AF.Square,
                         waits=[xm[c], sq_guard[c % 2]], inc=SS)
                s2m = mm(PS[0:1, 7, 0:SL], onesF[:, :], A_sq[:, c % 2, 0:SL],
                         start=(c == 0), stop=(c == NCH - 1),
                         waits=[sq] + (b7_free if c == 0 else []),
                         inc=SM)
                sq_guard[c % 2] = s2m
            r2s = act(r2row[0:1, :], PS[0:1, 7, 0:SL], AF.Ln,
                      bias=consts[0:1, 6:7], scale=consts[0:1, 4:5],
                      waits=[s2m, r2guard[0]], inc=SS)
            b7_free = [r2s]
            r2r = act(r2row[0:1, :], r2row[0:1, :], AF.Exp,
                      bias=crow2[0:1, 2:3], scale=crow2[0:1, 3:4],
                      waits=[r2s], inc=SS)
            r2b = mm(PS[:, 4, 0:SL], ones1F[0:1, :], r2row[0:1, :],
                     start=True, stop=True, waits=[r2r] + bank4_free, inc=SM)
            r2guard[0] = r2b
            h2l = None
            tmpl = None
            for c in range(NCH):
                h2l = vop("v", "scalar_tensor_tensor", h2ar[:, c, tsl], A_xm[:, c, :],
                          g2t[:, c:c + 1], PS[:, 4, 0:SL], ALU.mult, ALU.mult,
                          waits=[r2b] if c == 0 else (),
                          inc=SV if c == NCH - 1 else None)
            for c in range(NCH):
                tmpl = vop("v", "scalar_tensor_tensor", xar[:, c, tsl], A_xm[:, c, :],
                           consts[:, 3:4], xar[:, c, tsl], ALU.mult, ALU.add,
                           waits=[h2l] if c == 0 else (),
                           inc=SV if c == NCH - 1 else None)
            bank4_free = [h2l]
            prev_tmp = tmpl
        bar = barrier()
        dump(f"h2_{call}", h2ar[:, 0, :], bar)
        x1l = None
        we_ld = [None, None]
        first_zs_waits = bar
        for zs in range(NZS):
            z0 = zs * ZSL
            zsl = slice(z0, z0 + ZSL)
            gl = None
            for c in range(NCH):
                gl = mm(PS[0:8, 0, 0:ZSL], wgb[:, c, :], h2ar[:, c, zsl],
                        start=(c == 0), stop=(c == NCH - 1),
                        waits=first_zs_waits + [wg_ld] if c == 0 else (),
                        inc=SM if c == NCH - 1 else None)
            gex = act(B_ge[0:8, :], PS[0:8, 0, 0:ZSL], AF.Exp, bias=bgst[0:8, :],
                      waits=[gl, bg_sh], inc=SS)
            gsum = mm(PS[0:64, 1, 0:ZSL], ones8F[0:8, :], B_ge[0:8, :],
                      start=True, stop=True, waits=[gex], inc=SM)
            glog = act(RtB[0:8, :], PS[0:8, 1, 0:ZSL], AF.Ln,
                       waits=[gsum], inc=SS)
            grc = act(RtB[0:8, :], RtB[0:8, :], AF.Exp, scale=-1.0,
                      waits=[glog], inc=SS)
            gT = vop("v", "tensor_tensor", B_gt[0:8, :], B_ge[0:8, :], RtB[0:8, :],
                     ALU.mult, waits=[grc], inc=SV)
            if zs == 0:
                dump(f"gate_{call}", B_gt[0:8, :], [gT])
            gevs = []
            gbl = [grc, grc]
            for e in range(8):
                bk = e % 2
                gb = mm(PS[:, bk, 0:ZSL], selB[0:8, e, :], B_gt[0:8, :],
                        start=True, stop=True, waits=[gT, gbl[bk], w_consts], inc=SM)
                gv = vop("v", "tensor_copy", B_gall[:, e, :], PS[:, bk, 0:ZSL],
                         waits=[gb], inc=SV)
                gbl[bk] = gv
                gevs.append(gv)
            mm_last = None
            ze_rel = [None, None]
            we_rel = getattr(emit_ffn, "_we_rel", [None, None])
            for e in range(8):
                wb = e % 2
                we_ld[wb] = dma("g", B_we[:, wb, :, :],
                                we_in[e].rearrange("(c p) n -> p c n", p=128),
                                waits=[we_ld[wb], we_rel[wb],
                                       bar[0] if zs == 0 and e < 2 else None],
                                inc=DW)
                zl = None
                for c in range(NCH):
                    zl = vop("v", "tensor_tensor", B_z[:, wb, c, :], h2ar[:, c, zsl],
                             B_gall[:, e, :], ALU.mult,
                             waits=[gevs[e], ze_rel[wb]] if c == 0 else (),
                             inc=SV if c == NCH - 1 else None)
                ze_rel[wb] = None
                for fo in range(NCH):
                    for c in range(NCH):
                        mm_last = mm(PS[:, fo, 0:ZSL],
                                     B_we[:, wb, c, fo * 128:(fo + 1) * 128],
                                     B_z[:, wb, c, :],
                                     start=(e == 0 and c == 0),
                                     stop=(e == 7 and c == NCH - 1),
                                     waits=[zl, we_ld[wb]] + ([gevs[-1]] if e == 0 else [])
                                     if c == 0 else (),
                                     inc=SM if c == NCH - 1 and fo == NCH - 1 else None)
                ze_rel[wb] = mm_last
                we_rel[wb] = mm_last
            emit_ffn._we_rel = we_rel
            x1c = []
            for fo in range(NCH):
                x1l = vop("v", "scalar_tensor_tensor", xar[:, fo, zsl],
                          PS[:, fo, 0:ZSL], consts[:, 3:4], xar[:, fo, zsl],
                          ALU.mult, ALU.add, waits=[mm_last, prev_tmp], inc=SV)
                x1c.append(x1l)
            first_zs_waits = x1c
        dump(f"x1_{call}", xar[:, 0, :], [x1l])
        return x1l

    def emit_attn(x1_done):
        bar0 = barrier()
        r_rln = rowsA[0:1, 0, :]
        rows_done, stats_rel = ln_stats(x1_done, r_rln, None)
        lnp_rel = [stats_rel, stats_rel]
        ln_dma = []
        bcast_rel = [stats_rel, stats_rel]
        for qd in range(4):
            pp = qd % 2
            rb = mm(PS[:, pp, :], ones1F[0:1, :], r_rln[:, qd * 512:(qd + 1) * 512],
                    start=True, stop=True, waits=[rows_done, bcast_rel[pp]], inc=SM)
            lnl = None
            for c in range(NCH):
                lnl = vop("v", "tensor_tensor", AT_lnp[:, pp, c, :],
                          xar[:, c, qd * 512:(qd + 1) * 512], PS[:, pp, :],
                          ALU.mult, waits=[rb, lnp_rel[pp]] if c == 0 else (),
                          inc=SV if c == NCH - 1 else None)
            bcast_rel[pp] = lnl
            d = dma("y", ln2own_d.rearrange("(c p) (a n) -> p a c n", p=128, a=4)[:, qd, :, :],
                    AT_lnp[:, pp, :, :], waits=[lnl], inc=DS)
            lnp_rel[pp] = d
            ln_dma.append(d)
        d2 = dma("y", x_page[:, :, :], xar[:, :, :], waits=[x1_done], inc=DS)
        ag1 = sch.op("g", lambda e: e.collective_compute(
            "AllGather", ALU.bypass,
            replica_groups=[[0, 1], [2, 3], [4, 5], [6, 7]],
            ins=[ln2own_d[:, :].opt()], outs=[ln2all_d[:, :].opt()]),
            ln_dma + [bar0[3]], CC)
        d3a = dma("y", ln2full[:, :, 0:NT],
                  ln2all_d[0:D, :].rearrange("(c p) n -> p c n", p=128),
                  waits=[ag1, d2], inc=DL)
        d3 = dma("y", ln2full[:, :, NT:2 * NT],
                 ln2all_d[D:2 * D, :].rearrange("(c p) n -> p c n", p=128),
                 waits=[ag1, d2], inc=DL)
        d3 = [d3a, d3]
        dump("ln2full", ln2full[:, 0, :], [d3])
        bar_lp = barrier()

        wm_rel = [stats_rel]
        wsl_rel = [None, None]
        b01_rel = [bcast_rel[0], bcast_rel[1]]
        kb_rel = [None, None]
        k_dma = []
        qk_last = None

        def prep_128slab(dst_ws, pp, fo):
            ld = dma("g", dst_ws,
                     wint_in[:, fo * 128:(fo + 1) * 128].rearrange("(c p) n -> p c n", p=128),
                     waits=[wsl_rel[pp]] + (ln_dma if fo < 2 else []), inc=DW)
            csl = None
            for c in range(NCH):
                csl = mm(PS[0:1, 6, 0:128], onesB[:, :], dst_ws[:, c, :],
                         start=(c == 0), stop=(c == NCH - 1),
                         waits=[ld] + wm_rel if c == 0 else (),
                         inc=SM if c == NCH - 1 else None)
            nm = vop("v", "tensor_scalar", AT_pr[0:1, 0:128], PS[0:1, 6, 0:128],
                     consts[0:1, 4:5], None, ALU.mult, waits=[csl], inc=SV)
            nb = mm(PS[:, 6, 0:128], ones1B[0:1, :], AT_pr[0:1, 0:128],
                    start=True, stop=True, waits=[nm], inc=SM)
            nbe = vop("v", "tensor_copy", AT_wm[:, :], PS[:, 6, 0:128],
                      waits=[nb], inc=SV)
            wm_rel[0] = nbe
            wp = None
            for c in range(NCH):
                wp = vop("v", "tensor_tensor", dst_ws[:, c, :], dst_ws[:, c, :],
                         AT_wm[:, :], ALU.subtract, waits=[nbe] if c == 0 else (),
                         inc=SV if c == NCH - 1 else None)
            return wp

        for fo in range(8):
            pp = fo % 2
            wsl = AT_ws[:, pp, :, :]
            wp = prep_128slab(wsl, pp, fo)
            last_e = None
            for tf in range(8):
                bk = tf % 2
                mml = None
                for c in range(NCH):
                    mml = mm(PS[:, bk, :], wsl[:, c, :],
                             ln2full[:, c, tf * 512:(tf + 1) * 512],
                             start=(c == 0), stop=(c == NCH - 1),
                             waits=[wp, d3, b01_rel[bk]] if c == 0 else (),
                             inc=SM if c == NCH - 1 else None)
                if fo < 4:
                    last_e = vop("s", "copy", AT_q[:, fo, tf * 512:(tf + 1) * 512],
                                 PS[:, bk, :], waits=[mml], inc=SS)
                else:
                    last_e = vop("s", "copy", AT_kb[:, pp, tf * 512:(tf + 1) * 512],
                                 PS[:, bk, :],
                                 waits=[mml] + ([kb_rel[pp]] if tf == 0 else []),
                                 inc=SS)
                b01_rel[bk] = last_e
            wsl_rel[pp] = last_e
            qk_last = last_e
            if fo >= 4:
                dkk = dma("y", k_dram[(fo - 4) * 128:(fo - 3) * 128, :], AT_kb[:, pp, :],
                          waits=[last_e], inc=DS)
                kb_rel[pp] = dkk
                k_dma.append(dkk)
        vw_ld = dma("g", VW[:, :, :],
                    wint_in[:, 1024:1536].rearrange("(c p) n -> p c n", p=128),
                    waits=[d3, k_dma[2], kb_rel[0]], inc=DW)
        csl = None
        for c in range(NCH):
            csl = mm(PS[0:1, 6, 0:512], onesB[:, :], VW[:, c, :],
                     start=(c == 0), stop=(c == NCH - 1),
                     waits=[vw_ld] + wm_rel if c == 0 else (),
                     inc=SM if c == NCH - 1 else None)
        nm = vop("v", "tensor_scalar", AT_pr[0:1, 0:512], PS[0:1, 6, 0:512],
                 consts[0:1, 4:5], None, ALU.mult, waits=[csl], inc=SV)
        nbe = None
        for half in range(2):
            nb = mm(PS[:, 6, 0:256], ones1B[0:1, :],
                    AT_pr[0:1, half * 256:half * 256 + 256],
                    start=True, stop=True, waits=[nm, nbe], inc=SM)
            nbe = vop("v", "tensor_copy", nmb[:, half * 256:half * 256 + 256],
                      PS[:, 6, 0:256], waits=[nb], inc=SV)
        vwp = None
        for c in range(NCH):
            vwp = vop("v", "tensor_tensor", VW[:, c, :], VW[:, c, :], nmb[:, 0:512],
                      ALU.subtract, waits=[nbe] if c == 0 else (),
                      inc=SV if c == NCH - 1 else None)
        vones = ms(AT_vb[:, :, :, :, :], 1.0, waits=ln_dma, inc=SG)
        v_dma = []
        b23_rel = [None, None]
        vb_rel = [None, None]
        for tc in range(32):
            pb = (tc // 2) % 2
            sub = tc % 2
            bk = 2 + tc % 2
            mml = None
            for c in range(NCH):
                mml = mm(PS[:, bk, :], ln2full[:, c, tc * 128:(tc + 1) * 128],
                         VW[:, c, :], start=(c == 0), stop=(c == NCH - 1),
                         waits=[vwp, d3, b23_rel[tc % 2]] + ([vones] if tc < 2 else [])
                         if c == 0 else (), inc=SM if c == NCH - 1 else None)
            evv = vop("v", "tensor_copy", AT_vb[:, pb, sub, :, 0:64],
                      PS[:, bk, :].rearrange("p (a b) -> p a b", a=8),
                      waits=[mml] + ([vb_rel[pb]] if sub == 0 else []), inc=SV)
            b23_rel[tc % 2] = evv
            vd = dma("y",
                     v_dram.rearrange("h (a p) f -> p a h f", p=128)[:, tc, :, :],
                     AT_vb[:, pb, sub, :, :], waits=[evv], inc=DS)
            if sub == 1:
                vb_rel[pb] = vd
            v_dma.append(vd)
        x_back = dma("y", xar[:, :, :], x_page[:, :, :],
                     waits=[qk_last, vwp, v_dma[-1], SM.now()], inc=DL)

        dump("atq", AT_q[:, 0, :], [qk_last, SM.now(), SS.now()])
        bar1 = barrier()
        ks_rel = [None, None]
        vs_rel = [None, None]
        exp_g = [None, None]
        av_g = [None, None]
        psO_rel = [None, None]
        bpt_rel = [None, None]
        rta_rel = [None, None]
        ah_rel = {}
        ao_dma = []
        for hp in range(4):
            kss = dma("s", KS[:, :], k_dram[hp * 128:(hp + 1) * 128, :],
                      waits=[k_dma[hp], ks_rel[0]] + (bar1 if hp == 0 else []),
                      inc=DK)
            if hp == 0:
                ksd = dump("ks0", KS[0:64, :], [kss])
                if ksd is not None:
                    ks_rel[0] = [ks_rel[0], ksd] if ks_rel[0] else [ksd]
            vsd = []
            for h01 in range(2):
                vsd.append(dma("s", AT_vs[:, h01, :, :],
                               v_dram[2 * hp + h01, :, :].rearrange("(a p) f -> p a f", p=128),
                               waits=[v_dma[-1], vs_rel[h01]] + (bar1 if hp == 0 else []),
                               inc=DK))
            hp_av = None
            for qc in range(8):
                nk = (qc + 1) * 4
                avv = [None, None]
                pend = None
                for kc in range(nk + 1):
                    mask_w = None
                    if kc < nk:
                        bS = (kc % 2) * 2
                        qkk = None
                        for h01 in range(2):
                            qkk = mm(PS[:, bS + h01, :],
                                     KS[h01 * 64:h01 * 64 + 64,
                                        kc * 128:(kc + 1) * 128],
                                     AT_q[h01 * 64:h01 * 64 + 64, hp,
                                          qc * 512:(qc + 1) * 512],
                                     start=True, stop=True,
                                     waits=[kss, exp_g[kc % 2]] if h01 == 0 else (),
                                     inc=SM if h01 == 1 else None,
                                     tp=(64, 0) if h01 == 1 else None)
                        ex = act(BPt[:, kc % 2, :],
                                 PS[:, bS:bS + 2, :].rearrange("p a b -> p (a b)"),
                                 AF.Exp, bias=consts[:, 2:3], scale=consts[:, 0:1],
                                 waits=[qkk, bpt_rel[kc % 2]], inc=SS)
                        exp_g[kc % 2] = ex
                        mask_w = [ex]
                        if kc >= qc * 4:
                            afl = None
                            for h01 in range(2):
                                afl = vop("g", "affine_select",
                                          BPt[:, kc % 2, h01 * 512:(h01 + 1) * 512],
                                          BPt[:, kc % 2, h01 * 512:(h01 + 1) * 512],
                                          pattern=[[1, 512]], compare_op=ALU.is_ge,
                                          fill=0.0, base=qc * 512 - kc * 128,
                                          channel_multiplier=-1,
                                          waits=[ex] if h01 == 0 else (),
                                          inc=SG if h01 == 1 else None)
                            mask_w = [afl]
                    if pend is not None:
                        pkc, pmask = pend
                        for h01 in range(2):
                            avv[h01] = mm(PS[:, 4 + h01, :],
                                          AT_vs[:, h01, pkc, :],
                                          BPt[:, pkc % 2, h01 * 512:(h01 + 1) * 512],
                                          start=(pkc == 0), stop=(pkc == nk - 1),
                                          waits=pmask + vsd +
                                          ([psO_rel[h01]] if pkc == 0 else []),
                                          inc=SM if h01 == 1 or pkc == nk - 1 else None)
                        bpt_rel[pkc % 2] = avv[1]
                    if kc < nk:
                        pend = (kc, mask_w)
                hp_av = avv[1]
                for h01 in range(2):
                    rrow = RtA[64:128, :] if h01 == 0 else RtA2[64:128, :]
                    lna = act(rrow, PS[64:128, 4 + h01, :], AF.Ln,
                              waits=[avv[h01], rta_rel[h01]], inc=SS)
                    exa = act(rrow, rrow, AF.Exp, scale=-1.0,
                              waits=[lna], inc=SS)
                    dv = vop("v", "tensor_tensor", AT_ah[0:64, h01, :],
                             PS[0:64, 4 + h01, :], rrow, ALU.mult,
                             waits=[exa, ah_rel.get(h01)], inc=SV)
                    psO_rel[h01] = dv
                    rta_rel[h01] = dv
                    half = qc // 4
                    tl0 = qc * 512 - half * NT
                    add = dma("y",
                              attnown_d[half, (2 * hp + h01) * 64:(2 * hp + h01 + 1) * 64,
                                        tl0:tl0 + 512],
                              AT_ah[0:64, h01, :], waits=[dv], inc=DS)
                    ah_rel[h01] = add
                    ao_dma.append(add)
            ks_rel[0] = hp_av
            vs_rel[0] = hp_av
            vs_rel[1] = hp_av
        a2a = sch.op("g", lambda e: e.collective_compute(
            "AllGather", ALU.bypass,
            replica_groups=[[0, 1], [2, 3], [4, 5], [6, 7]],
            ins=[attnown_d.rearrange("a b c -> (a b) c").opt()],
            outs=[attnall_d.rearrange("r a b c -> (r a b) c").opt()]),
            ao_dma, CC)
        dump("attnown", attnown_d[0, 0:64, :], [a2a])
        dump("kdram", k_dram[0:64, 0:2048], [a2a])
        dump("vdram", v_dram[0, 0:2048, :].rearrange("(a b) f -> a (b f)", b=16), [a2a])
        dump("attnall", attnall_d[0, 0, 0:64, :], [a2a])

        bar2 = barrier()
        wout_ld = dma("g", AT_wout[:, :, :],
                      wout_in.rearrange("(c p) n -> p c n", p=128),
                      waits=bar2, inc=DW)
        slab4 = at_slab[:, :, :].rearrange("p (r c) n -> p r c n", r=2)

        def slab_fn(e):
            reg = e.alloc_register("rreg")
            e.load(reg, ridxt[0:1, 0:1])
            for mh in range(2):
                ctx = e.If_eq(reg, 0) if mh == 0 else e.Else()
                with ctx:
                    for r in range(2):
                        e.dma_start(out=slab4[:, r, :, :],
                                    in_=attnall_d[r, mh, :, :].rearrange("(c p) n -> p c n", p=128)
                                    ).then_inc(DL.h, 16)
            return None
        sch.op("y", slab_fn, [a2a, ridx_ld] + bar2, None)
        DL.bump(32)
        sl_ld = DL.now()
        dump("slab", at_slab[:, :, :].rearrange("p a b -> p (a b)"), [sl_ld])
        dump("wout2", AT_wout[:, :, :].rearrange("p a b -> p (a b)"), [wout_ld])
        dump("xrest", xar[:, 0, :], [x_back])
        x2l = None
        b_rel = [bar2[1], bar2[1]]
        for fo in range(NCH):
            for tf in range(4):
                bk = (fo * 4 + tf) % 2
                mml = None
                for c in range(NCH):
                    mml = mm(PS[:, bk, :], AT_wout[:, c, fo * 128:(fo + 1) * 128],
                             at_slab[:, c, tf * 512:(tf + 1) * 512],
                             start=(c == 0), stop=(c == NCH - 1),
                             waits=[wout_ld, sl_ld, x_back, b_rel[bk]] if c == 0 else (),
                             inc=SM if c == NCH - 1 else None)
                x2l = vop("v", "tensor_tensor", xar[:, fo, tf * 512:(tf + 1) * 512],
                          xar[:, fo, tf * 512:(tf + 1) * 512], PS[:, bk, :],
                          ALU.add, waits=[mml], inc=SV)
                b_rel[bk] = x2l
        dump("x2", xar[:, 0, :], [x2l])
        dump("wout3", AT_wout[:, :, :].rearrange("p a b -> p (a b)"), [x2l])
        dump("slab3", at_slab[:, :, :].rearrange("p a b -> p (a b)"), [x2l])
        return x2l

    x1 = emit_ffn(0, x_ld)
    last = x1
    if parts >= 2:
        last = emit_attn(x1)
    if parts >= 3:
        barx = barrier()
        last = emit_ffn(1, barx[1])
    od = dma("y", out_d.rearrange("(c p) n -> p c n", p=128), xar[:, :, :],
             waits=[last], inc=DS)
    sch.op("y", lambda e: e.nop(), [od, DS.now()], None)

    with nc.Block() as block:
        @block.tensor
        def _(e):
            sch.replay("t", e)

        @block.scalar
        def _(e):
            sch.replay("s", e)

        @block.vector
        def _(e):
            sch.replay("v", e)

        @block.gpsimd
        def _(e):
            sch.replay("g", e)

        @block.sync
        def _(e):
            sch.replay("y", e)

    return nc


_NCACHE = {}


def _prep_inputs(inputs):
    f32 = np.float32
    Wq = np.asarray(inputs["L0_ffn_Wq"], f32)
    Wdown = np.asarray(inputs["L0_ffn_Wdown"], f32)
    Wup = np.asarray(inputs["L0_ffn_Wup"], f32)
    Wg = np.asarray(inputs["L0_ffn_Wg"], f32)
    Wo = np.asarray(inputs["L0_ffn_Wo"], f32)
    We = np.asarray(inputs["L0_ffn_We"], f32)
    Win = np.asarray(inputs["L0_attn_in_proj_weight"], f32)
    Wout = np.asarray(inputs["L0_attn_out_proj_weight"], f32)
    x = np.asarray(inputs["L0_attn_query"], f32)

    wqd = np.ascontiguousarray(np.concatenate([Wq, Wdown], axis=1))
    idx = np.arange(2048).reshape(16, 2, 64)
    perm = np.concatenate([idx[:, 0, :].reshape(-1), idx[:, 1, :].reshape(-1)])
    wup = np.ascontiguousarray(Wup[:, perm])
    bup = np.asarray(inputs["L0_ffn_bup"], f32)[perm]
    wintT = np.ascontiguousarray(Win.T)
    woutT = np.ascontiguousarray(Wout.T)

    mr = np.zeros((9, 16 * 128), np.float32)
    sub = np.arange(16 * 128)
    blk = (sub // 16) % 8
    for j in range(8):
        mr[j, blk == j] = SQB
    mr[8, :] = SQB
    mrq = mr.copy()
    mrq[8, :] = -SQB
    com = {
        "mrq": mrq,
        "mrk": mr,
        "wqd": wqd,
        "wup": wup,
        "wg": np.ascontiguousarray(Wg),
        "wo": np.ascontiguousarray(Wo),
        "bo": np.ascontiguousarray(np.asarray(inputs["L0_ffn_bo"], f32).reshape(1, NCH, 128)),
        "we": np.ascontiguousarray(We),
        "wout": woutT,
        "g1c": np.ascontiguousarray(np.asarray(inputs["L0_ffn_gamma1"], f32).reshape(NCH, 128).T),
        "g2c": np.ascontiguousarray(np.asarray(inputs["L0_ffn_gamma2"], f32).reshape(NCH, 128).T),
        "bqc": np.ascontiguousarray(np.asarray(inputs["L0_ffn_bq"], f32).reshape(NCH, 128).T),
        "bdc": np.ascontiguousarray(np.asarray(inputs["L0_ffn_bdown"], f32).reshape(64, 1)),
        "bupc": np.ascontiguousarray(bup.reshape(16, 128).T),
        "bgc": np.ascontiguousarray(np.asarray(inputs["L0_ffn_bg"], f32).reshape(8, 1)),
    }
    in_maps = []
    for c in range(8):
        b, half = c // 2, c % 2
        m = dict(com)
        m["xt"] = np.ascontiguousarray(x[b, half * NT:(half + 1) * NT, :].T)
        cols = np.concatenate([np.arange(half * 512, half * 512 + 512),
                               1024 + np.arange(half * 512, half * 512 + 512),
                               2048 + np.arange(half * 512, half * 512 + 512)])
        m["wint"] = np.ascontiguousarray(wintT[:, cols])
        m["ridx"] = np.array([[half]], np.int32)
        in_maps.append(m)
    return in_maps


def kernel(**inputs):
    in_maps = _prep_inputs(inputs)
    nc = build()
    res = run_bass_kernel_spmd(nc, in_maps, core_ids=list(range(8)))
    out = np.empty((4, T, D), np.float32)
    for c in range(8):
        b, half = c // 2, c % 2
        out[b, half * NT:(half + 1) * NT, :] = res.results[c]["out"].T
    return out


# version 05
# speedup vs baseline: 1.3217x; 1.0885x over previous
import numpy as np
import ml_dtypes
import concourse.bass as bass
import concourse.mybir as mybir
from concourse.bass_utils import run_bass_kernel_spmd

F32 = mybir.dt.float32
BF16 = mybir.dt.bfloat16
U8 = mybir.dt.uint8
AF = mybir.ActivationFunctionType
ALU = mybir.AluOpType
BFdt = ml_dtypes.bfloat16

D = 1024
NCH = 8
NT = 2048
SL = 128
NSL = NT // SL
ZSL = 512
NZS = NT // ZSL
T = 4096
HL = 8
CPT = 12.0
CBIG = 24.0
CG = 8.0
SQB = 20.0


class Sem:
    def __init__(self, nc, name):
        self.h = nc.alloc_semaphore(name=name)
        self.n = 0

    def bump(self, k=1):
        self.n += k

    def now(self):
        return (self, self.n)


class Sched:
    def __init__(self):
        self.ops = []

    def op(self, eng, fn, waits=(), inc=None, inck=1):
        w = []
        for x in waits:
            if x is None:
                continue
            if isinstance(x, list):
                w.extend(y for y in x if y is not None)
            else:
                w.append(x)
        self.ops.append((eng, fn, w, (inc, inck) if inc is not None else None))
        if inc is not None:
            inc.bump(inck)
            return inc.now()
        return None

    def replay(self, eng_name, handle):
        for eng, fn, waits, post in self.ops:
            if eng != eng_name:
                continue
            seen = {}
            for (sem, val) in waits:
                if val > 0 and seen.get(sem, -1) < val:
                    seen[sem] = val
            for sem, val in seen.items():
                handle.wait_ge(sem.h, val)
            inst = fn(handle)
            if post is not None:
                sem, k = post
                if inst is None:
                    inst = handle.nop()
                inst.then_inc(sem.h, k)


def build(dbg=(), parts=3, nocrow=False):
    nc = bass.Bass("TRN2", target_bir_lowering=False)

    def din(name, shape, dtype=F32):
        return nc.dram_tensor(name, list(shape), dtype, kind="ExternalInput").ap()

    xt_in = din("xt", (D, NT))
    wqd_in = din("wqd", (D, 1088))
    wup_in = din("wup", (64, 2048))
    wg_in = din("wg", (D, 8))
    wo_in = din("wo", (D, D))
    bo_in = din("bo", (1, NCH, 128))
    we_in = din("we", (8, D, D))
    wint_in = din("wint", (D, 1536))
    wout_in = din("wout", (D, D))
    g1_in = din("g1c", (128, 8))
    g2_in = din("g2c", (128, 8))
    bq_in = din("bqc", (128, 8))
    bd_in = din("bdc", (64, 1))
    bup_in = din("bupc", (128, 16))
    bg_in = din("bgc", (8, 1))
    ridx_in = din("ridx", (1, 1), mybir.dt.int32)
    mrq_in = din("mrq", (9, 16 * SL))
    mrk_in = din("mrk", (9, 16 * SL))

    out_d = nc.dram_tensor("out", [D, NT], F32, kind="ExternalOutput").ap()
    dbg_d = {}
    for name, shape in dbg:
        dbg_d[name] = nc.dram_tensor("dbg_" + name, list(shape), F32,
                                     kind="ExternalOutput").ap()

    x_page = nc.dram_tensor("x_page", [128, NCH, NT], F32).ap()
    k_dram = nc.dram_tensor("k_dram", [512, T], BF16).ap()
    v_dram = nc.dram_tensor("v_dram", [HL, T, 128], BF16).ap()
    ln2own_d = nc.dram_tensor("ln2own", [D, NT], BF16).ap()
    ln2all_d = nc.dram_tensor("ln2all", [2 * D, NT], BF16).ap()
    attnown_d = nc.dram_tensor("attnown", [2, 512, NT], BF16).ap()
    attnall_d = nc.dram_tensor("attnall", [2, 2, 512, NT], BF16).ap()

    xar = nc.alloc_sbuf_tensor("xar", [128, NCH, NT], F32)
    h2ar = nc.alloc_sbuf_tensor("h2ar", [128, NCH, NT], BF16)
    wqdb = nc.alloc_sbuf_tensor("wqdb", [128, NCH, 1088], BF16)
    wupb = nc.alloc_sbuf_tensor("wupb", [64, 2048], BF16)
    wgb = nc.alloc_sbuf_tensor("wgb", [128, NCH, 8], BF16)
    woextb = nc.alloc_sbuf_tensor("woextb", [2, NCH, 128], BF16)
    rowsA = nc.alloc_sbuf_tensor("rowsA", [1, 2, NT], F32)
    mrlnb = nc.alloc_sbuf_tensor("mrlnb", [1, NT], BF16)
    g1t = nc.alloc_sbuf_tensor("g1t", [128, 8], F32)
    g2t = nc.alloc_sbuf_tensor("g2t", [128, 8], F32)
    bqt = nc.alloc_sbuf_tensor("bqt", [128, 8], F32)
    bdt = nc.alloc_sbuf_tensor("bdt", [64, 1], F32)
    bupt = nc.alloc_sbuf_tensor("bupt", [128, 16], F32)
    bgst = nc.alloc_sbuf_tensor("bgst", [8, 1], F32)
    cgt = nc.alloc_sbuf_tensor("cgt", [8, 1], F32)
    ridxt = nc.alloc_sbuf_tensor("ridxt", [1, 1], mybir.dt.int32)
    onesF = nc.alloc_sbuf_tensor("onesF", [128, 1], F32)
    onesB = nc.alloc_sbuf_tensor("onesB", [128, 1], BF16)
    ones1B = nc.alloc_sbuf_tensor("ones1B", [1, 128], BF16)
    ones1F = nc.alloc_sbuf_tensor("ones1F", [1, 128], F32)
    ones8F = nc.alloc_sbuf_tensor("ones8F", [8, 64], F32)
    consts = nc.alloc_sbuf_tensor("consts", [128, 8], F32)
    crow2 = nc.alloc_sbuf_tensor("crow2", [1, 4], F32)
    identB = nc.alloc_sbuf_tensor("identB", [64, 64], BF16)
    selB = nc.alloc_sbuf_tensor("selB", [8, 8, 128], BF16)
    scratch = nc.alloc_sbuf_tensor("scratch", [128, 65 * 1024], U8)

    PS = nc.alloc_psum_tensor("PS", [128, 8, 512], F32)

    def carve(off, shape, dtype):
        esz = 2 if dtype == BF16 else 4
        n = int(np.prod(shape[1:])) * esz
        ap = scratch[:, off:off + n].bitcast(dtype)
        if len(shape) == 3:
            ap = ap.rearrange("p (a b) -> p a b", a=shape[1])
        elif len(shape) == 4:
            ap = ap.rearrange("p (a b c) -> p a b c", a=shape[1], b=shape[2])
        elif len(shape) == 5:
            ap = ap.rearrange("p (a b c d) -> p a b c d", a=shape[1], b=shape[2],
                              c=shape[3])
        if shape[0] != 128:
            ap = ap[0:shape[0]]
        return ap

    A_xm = carve(0, (128, NCH, SL), F32)
    A_ht = carve(4096, (128, NCH, SL), BF16)
    A_q = carve(6144, (128, NCH, SL), BF16)
    A_dv = carve(8192, (64, SL), BF16)
    A_kt = carve(8448, (128, NCH, SL), BF16)
    A_vt = carve(10496, (128, NCH, SL), BF16)
    A_at = carve(12544, (128, NCH, SL), BF16)
    A_cr = carve(14592, (2, SL), BF16)
    A_tmp = carve(14848, (128, SL), F32)
    A_wo = carve(15360, (128, NCH, D), BF16)
    Qp = carve(31744, (73, 16 * SL), BF16)
    Kp = carve(35840, (73, 16 * SL), BF16)
    Vp = carve(39936, (64, 16 * SL), BF16)
    TEMP = carve(44032, (64, 16 * SL), BF16)
    A_sq = carve(48128, (128, 2, 1024), F32)
    PtT = carve(56320, (128, 2, 512), BF16)
    Vg = carve(58368, (128, 4, 128), BF16)
    Rt = carve(59392, (128, 512), F32)
    r2row = carve(61440, (1, SL), F32)
    nmb = carve(61952, (128, 1088), BF16)
    nmrQ = carve(64128, (1, 1088), BF16)
    rmrow = carve(0, (1, NT), F32)
    rtrow = carve(8192, (1, NT), F32)
    B_we = carve(0, (128, 2, NCH, D), BF16)
    B_z = carve(32768, (128, 2, NCH, ZSL), BF16)
    B_gall = carve(49152, (128, 8, ZSL), BF16)
    B_ge = carve(57344, (8, ZSL), F32)
    B_gt = carve(59392, (8, ZSL), BF16)
    RtB = carve(60416, (128, ZSL), F32)
    AT_lnp = carve(0, (128, 2, NCH, 512), BF16)
    AT_q = carve(16384, (128, 4, T), BF16)
    AT_kb = carve(49152, (128, 2, T), BF16)
    AT_vb = carve(0, (128, 2, 2, HL, 128), BF16)
    AT_ws = carve(8192, (128, 2, NCH, 128), BF16)
    AT_wm = carve(12288, (128, 128), BF16)
    AT_pr = carve(12544, (1, 512), BF16)
    VW = carve(49152, (128, NCH, 512), BF16)
    AT_vs = carve(0, (128, 2, 32, 128), BF16)
    KS = carve(49152, (128, T), BF16)
    BPt = carve(57344, (128, 2, 1024), BF16)
    RtA = carve(61440, (128, 512), F32)
    AT_ah = carve(63488, (64, 2, 512), BF16)
    RtA2 = carve(63488, (128, 512), F32)
    AT_wout = carve(16384, (128, NCH, D), BF16)

    ln2full = xar[:, :, :].bitcast(BF16)
    at_slab = h2ar

    SM = Sem(nc, "SM")
    SV = Sem(nc, "SV")
    SS = Sem(nc, "SS")
    SG = Sem(nc, "SG")
    DL = Sem(nc, "DL")
    DS = Sem(nc, "DS")
    DW = Sem(nc, "DW")
    DK = Sem(nc, "DK")
    CC = Sem(nc, "CC")
    ESEM = {"v": SV, "s": SS, "g": SG}
    CPOP = {"v": "tensor_copy", "s": "copy", "g": "tensor_copy"}

    sch = Sched()

    def mm(out, lhsT, rhs, start, stop, waits=(), inc=None, tp=None):
        return sch.op("t", lambda e, o=out, l=lhsT, r=rhs, a=start, b=stop, t_=tp:
                      e.matmul(o, l, r, start=a, stop=b, tile_position=t_),
                      waits, inc)

    def transp(out, in_, ident, waits=(), inc=None):
        return sch.op("t", lambda e, o=out, i=in_, d=ident: e.transpose(o, i, d),
                      waits, inc)

    def act(out, in_, func, bias=0.0, scale=1.0, accum=None, waits=(), inc=None):
        return sch.op("s", lambda e, o=out, i=in_, f=func, b=bias, s=scale, a=accum:
                      e.activation(o, i, f, bias=b, scale=s, accum_out=a),
                      waits, inc)

    def vop(eng, name, *args, waits=(), inc=None, **kw):
        def fn(e, n=name, a=args, k=kw):
            return getattr(e, n)(*a, **k)
        return sch.op(eng, fn, waits, inc)

    def dma(eng, out, in_, waits=(), inc=None):
        return sch.op(eng, lambda e, o=out, i=in_: e.dma_start(out=o, in_=i),
                      waits, inc, inck=16)

    def barrier():
        snaps = [SM.now(), SV.now(), SS.now(), SG.now(), DL.now(), DS.now(),
                 DW.now(), DK.now()]
        outs = []
        outs.append(sch.op("t", lambda e: e.nop(), snaps, SM))
        outs.append(sch.op("v", lambda e: e.nop(), snaps, SV))
        outs.append(sch.op("s", lambda e: e.nop(), snaps, SS))
        outs.append(sch.op("g", lambda e: e.nop(), snaps, SG))
        return outs

    def dump(name, ap, waits):
        if name in dbg_d:
            return dma("g", dbg_d[name], ap, waits=waits, inc=DS)
        return None

    x_ld = dma("y", xar[:, :, :], xt_in.rearrange("(c p) n -> p c n", p=128), inc=DL)
    wqd_ld = dma("g", wqdb[:, :, :], wqd_in.rearrange("(c p) n -> p c n", p=128), inc=DW)
    wup_ld = dma("g", wupb[0:64, :], wup_in, inc=DW)
    wg_ld = dma("g", wgb[:, :, :], wg_in.rearrange("(c p) n -> p c n", p=128), inc=DW)

    for dst, src, p in ((g1t, g1_in, 128), (g2t, g2_in, 128), (bqt, bq_in, 128),
                        (bdt, bd_in, 64), (bupt, bup_in, 128), (bgst, bg_in, 8)):
        dma("y", dst[0:p, :], src, inc=DL)
    ridx_ld = dma("y", ridxt[0:1, :], ridx_in, inc=DL)
    w_small = DL.now()

    def ms(ap, val, waits=(), inc=None):
        return vop("g", "memset", ap, val, waits=waits, inc=inc)

    ms(onesF[:, :], 1.0)
    ms(onesB[:, :], 1.0)
    ms(ones1B[0:1, :], 1.0)
    ms(ones1F[0:1, :], 1.0)
    ms(ones8F[0:8, :], 1.0)
    for col, val in ((0, 0.125), (1, -CPT), (2, -CBIG), (3, 0.5), (4, 1.0 / D),
                     (5, 1e-5), (6, 1e-6), (7, 0.0)):
        ms(consts[:, col:col + 1], val)
    ms(crow2[0:1, 0:1], -1e-5)
    ms(crow2[0:1, 1:2], 1.0 + 1e-6)
    ms(crow2[0:1, 2:3], 0.0)
    ms(crow2[0:1, 3:4], -0.5)
    ms(cgt[0:8, :], -CG)
    wx_ms = ms(woextb[0:2, :, :], -1.0)
    bo_ld = dma("g", woextb[1:2, :, :], bo_in, waits=[wx_ms], inc=DW)
    ms(identB[0:64, :], 0.0)
    vop("g", "affine_select", identB[0:64, :], identB[0:64, :],
        pattern=[[-1, 64]], compare_op=ALU.not_equal, fill=1.0, base=0,
        channel_multiplier=1)
    mr_ld1 = dma("g", Qp[64:73, :], mrq_in, inc=DW)
    mr_ld2 = dma("g", Kp[64:73, :], mrk_in, inc=DW)
    ms(Vg[:, :, :], 1.0)
    ms(selB[0:8, :, :], 0.0)
    vop("g", "affine_select", selB[0:8, :, :].rearrange("p a b -> p (a b)"),
        selB[0:8, :, :].rearrange("p a b -> p (a b)"),
        pattern=[[-1, 8], [0, 128]], compare_op=ALU.not_equal, fill=1.0, base=0,
        channel_multiplier=1)
    w_consts = sch.op("g", lambda e: e.nop(), [], SG)

    bg_sh = vop("v", "tensor_tensor", bgst[0:8, :], bgst[0:8, :], cgt[0:8, :],
                ALU.add, waits=[w_small, w_consts], inc=SV)

    wv = None
    for c in range(NCH):
        wv = vop("v", "tensor_scalar", wqdb[:, c, :], wqdb[:, c, :],
                 g1t[:, c:c + 1], None, ALU.mult,
                 waits=[wqd_ld, w_small] if c == 0 else (),
                 inc=SV if c == NCH - 1 else None)
    cs_mm = []
    for fo in range(3):
        lo, hi = fo * 512, min(fo * 512 + 512, 1088)
        last = None
        for c in range(NCH):
            last = mm(PS[0:1, fo, 0:hi - lo], onesB[:, :], wqdb[:, c, lo:hi],
                      start=(c == 0), stop=(c == NCH - 1),
                      waits=[wv] if c == 0 else (), inc=SM if c == NCH - 1 else None)
        cs_mm.append(last)
    nmr = nmrQ[0:1, :]
    nm_ops = []
    for fo in range(3):
        lo, hi = fo * 512, min(fo * 512 + 512, 1088)
        nm_ops.append(vop("v", "tensor_scalar", nmr[:, lo:hi], PS[0:1, fo, 0:hi - lo],
                          consts[0:1, 4:5], None, ALU.mult, waits=[cs_mm[fo]], inc=SV))
    wbc = []
    for fo in range(3):
        lo, hi = fo * 512, min(fo * 512 + 512, 1088)
        g = mm(PS[:, 3, 0:hi - lo], ones1B[0:1, :], nmr[:, lo:hi],
               start=True, stop=True, waits=[nm_ops[fo], wbc[-1] if wbc else None],
               inc=SM)
        wbc.append(vop("v", "tensor_copy", nmb[:, lo:hi], PS[:, 3, 0:hi - lo],
                       waits=[g], inc=SV))
    wprep = None
    for c in range(NCH):
        wprep = vop("v", "tensor_tensor", wqdb[:, c, :], wqdb[:, c, :],
                    nmb[:, :], ALU.subtract, waits=[wbc[-1]] if c == 0 else (),
                    inc=SV if c == NCH - 1 else None)

    def ln_stats(x_ready, r_rln, r_R1, guards=()):
        sqd = {}
        s1_mm = s2_mm = None
        for c in range(NCH):
            for hh in range(2):
                pp = (2 * c + hh) % 2
                sq = act(A_sq[:, pp, :], xar[:, c, hh * 1024:(hh + 1) * 1024],
                         AF.Square, waits=[x_ready, sqd.get(pp)] + list(guards), inc=SS)
                for q in range(2):
                    qq = hh * 2 + q
                    s2_mm = mm(PS[0:1, 4 + qq, :], onesF[:, :],
                               A_sq[:, pp, q * 512:(q + 1) * 512],
                               start=(c == 0), stop=(c == NCH - 1),
                               waits=[sq] if q == 0 else (),
                               inc=SM if q == 1 else None)
                sqd[pp] = s2_mm
            for q in range(4):
                s1_mm = mm(PS[0:1, q, :], onesF[:, :], xar[:, c, q * 512:(q + 1) * 512],
                           waits=([x_ready] + list(guards)) if c == 0 and q == 0 else (),
                           start=(c == 0), stop=(c == NCH - 1),
                           inc=SM if c == NCH - 1 and q == 3 else None)
        S1 = PS[0:1, 0:4, :].rearrange("p a b -> p (a b)")
        S2 = PS[0:1, 4:8, :].rearrange("p a b -> p (a b)")
        r_m = rmrow[0:1, :]
        r_t = rtrow[0:1, :]
        a1 = vop("v", "tensor_scalar", r_m, S1, consts[0:1, 4:5], None, ALU.mult,
                 waits=[s1_mm], inc=SV)
        a2 = vop("v", "tensor_tensor", r_t, r_m, r_m, ALU.mult, waits=[a1], inc=SV)
        a3 = vop("v", "scalar_tensor_tensor", r_rln, S2, consts[0:1, 4:5], r_t,
                 ALU.mult, ALU.subtract, waits=[s2_mm, a2], inc=SV)
        a4 = act(r_t, r_rln, AF.Ln, bias=consts[0:1, 5:6], waits=[a3], inc=SS)
        a5 = act(r_rln, r_t, AF.Exp, bias=crow2[0:1, 2:3], scale=crow2[0:1, 3:4],
                 waits=[a4], inc=SS)
        done = a5
        if r_R1 is not None:
            b1 = vop("v", "tensor_tensor", r_t, r_rln, r_rln, ALU.mult, waits=[a5], inc=SV)
            b2 = vop("v", "tensor_scalar", r_t, r_t, crow2[0:1, 0:1], crow2[0:1, 1:2],
                     ALU.mult, ALU.add, waits=[b1], inc=SV)
            b3 = act(r_t, r_t, AF.Ln, bias=crow2[0:1, 2:3], waits=[b2], inc=SS)
            b4 = act(r_t, r_t, AF.Exp, bias=crow2[0:1, 2:3], scale=crow2[0:1, 3:4],
                     waits=[b3], inc=SS)
            b5 = vop("v", "tensor_tensor", r_R1, r_rln, r_t, ALU.mult, waits=[b4], inc=SV)
            b6 = vop("v", "tensor_tensor", mrlnb[0:1, :], r_m, r_rln, ALU.mult,
                     waits=[b5], inc=SV)
            done = b6
        return done, a3

    def emit_ffn(call, x_ready):
        r_R1 = rowsA[0:1, 0, :]
        r_rln = rowsA[0:1, 1, :]
        mr1 = dma("g", Qp[64:73, :], mrq_in, waits=[x_ready], inc=DW)
        mr2 = dma("g", Kp[64:73, :], mrk_in, waits=[x_ready], inc=DW)
        mrows = [mr1, mr2]
        vg_ms = vop("g", "memset", Vg[:, :, :], 1.0, waits=[x_ready], inc=SG)
        rows_done, stats_rel = ln_stats(x_ready, r_rln, r_R1, guards=[wprep])
        wo_ld = dma("g", A_wo[:, :, :], wo_in.rearrange("(c p) n -> p c n", p=128),
                    waits=[x_ready, rows_done], inc=DW)
        dump(f"wqdb_{call}", wqdb[:, :, :].rearrange("p a b -> p (a b)"), [rows_done])
        dump(f"wupb_{call}", wupb[0:64, :], [rows_done])
        dump(f"awo_{call}", A_wo[:, :, :].rearrange("p a b -> p (a b)"), [wo_ld])
        dump(f"qkp_{call}", Kp[0:73, :], [mrows, SG.now(), SV.now(), SS.now()])
        dump(f"rows_{call}", rowsA[0:1, :, :].rearrange("p a b -> p (a b)"),
             [rows_done])

        B4A = [slice(0, 128), slice(128, 256)]
        B4RL = slice(256, 384)
        B4R2 = slice(384, 512)
        KV5 = [slice(0, 128), slice(128, 256), slice(256, 384), slice(384, 512)]
        CPAT = ["g", "v", "g", "v", "s", "g", "v", "g", "v", "g", "s", "v"]
        NR = 16 * SL // 128 // 4
        ptv_all = PS[:, 6, :].bitcast(BF16)

        b4a_guard = {0: [stats_rel], 1: [stats_rel]}
        rlb_guard = [stats_rel]
        r2guard = [stats_rel]
        b7_free = [stats_rel]
        bq23 = {0: [stats_rel], 1: [stats_rel]}
        bkv = {0: [stats_rel], 1: [stats_rel], 2: [stats_rel], 3: [stats_rel]}
        b03_free = [stats_rel]
        sq_guard = [stats_rel, stats_rel]
        prev_tmp = None
        exp_guard = [[stats_rel], [stats_rel]]
        av_guard = [None, None]
        tr_guard = [None, None]
        dv_guard = [vg_ms, vg_ms]
        SLC = {}

        def emitA(s):
            t0 = s * SL
            tsl = slice(t0, t0 + SL)
            par = s % 2
            b_r1 = mm(PS[:, 4, B4A[par]], ones1F[0:1, :], r_R1[:, tsl],
                      start=True, stop=True,
                      waits=[rows_done] + b4a_guard[par], inc=SM)
            hts = None
            for c in range(NCH):
                hts = vop("v", "scalar_tensor_tensor", A_ht[:, c, :], xar[:, c, tsl],
                          onesF[:, :], PS[:, 4, B4A[par]], ALU.mult, ALU.mult,
                          waits=[b_r1] if c == 0 else (),
                          inc=SV if c == NCH - 1 else None)
            b4a_guard[par] = [hts]
            pb = SLC.get(("builds", s - 1), [])
            bq23[0] = bq23[0] + [dv_guard[0]]
            bq23[1] = bq23[1] + [dv_guard[1]]
            ev = {}
            for fo in range(9):
                bk = 2 + fo % 2
                fw = 64 if fo == 8 else 128
                mml = None
                for c in range(NCH):
                    mml = mm(PS[0:fw, bk, 0:SL], wqdb[:, c, fo * 128:fo * 128 + fw],
                             A_ht[:, c, :], start=(c == 0), stop=(c == NCH - 1),
                             waits=[hts, wprep] + bq23[fo % 2] if c == 0 else (),
                             inc=SM if c == NCH - 1 else None)
                if fo == 8:
                    e = act(A_dv[0:64, :], PS[0:64, bk, 0:SL], AF.Identity,
                            bias=bdt[0:64, :], waits=[mml], inc=SS)
                elif fo % 2 == 0:
                    e = act(A_q[:, fo, :], PS[:, bk, 0:SL], AF.Identity,
                            bias=bqt[:, fo:fo + 1],
                            waits=[mml] + (pb if fo == 0 else []), inc=SS)
                else:
                    e = vop("v", "tensor_scalar", A_q[:, fo, :], PS[:, bk, 0:SL],
                            bqt[:, fo:fo + 1], None, ALU.add,
                            waits=[mml] + (pb if fo == 1 else []), inc=SV)
                ev[fo] = e
                bq23[fo % 2] = [e]
            kev = {}
            for fo in range(16):
                bk = 2 + fo % 2
                mml = mm(PS[:, bk, 0:SL], wupb[0:64, fo * 128:(fo + 1) * 128],
                         A_dv[0:64, :], start=True, stop=True,
                         waits=[ev[8]] + bq23[fo % 2], inc=SM)
                dst = A_kt[:, fo, :] if fo < 8 else A_vt[:, fo - 8, :]
                if fo % 2 == 0:
                    e = act(dst, PS[:, bk, 0:SL], AF.Identity,
                            bias=bupt[:, fo:fo + 1], waits=[mml], inc=SS)
                else:
                    e = vop("v", "tensor_scalar", dst, PS[:, bk, 0:SL],
                            bupt[:, fo:fo + 1], None, ALU.add,
                            waits=[mml], inc=SV)
                kev[fo] = e
                bq23[fo % 2] = [e]
            SLC[("qev", s)] = {0: kev[14], 1: kev[15]}
            SLC[("qkv_done", s)] = [ev[i] for i in range(8)] + \
                [kev[i] for i in range(16)]

        def emitB(s):
            w0 = SLC[("qkv_done", s)] + [w_consts] + \
                SLC.get(("b03", s - 1), b03_free)
            ci = 0
            first = {"v": True, "s": True, "g": True}
            for h in range(16):
                for (src, dstt) in (
                        (A_q[(h % 2) * 64:(h % 2) * 64 + 64, h // 2, :], Qp),
                        (A_kt[(h % 2) * 64:(h % 2) * 64 + 64, h // 2, :], Kp),
                        (A_vt[(h % 2) * 64:(h % 2) * 64 + 64, h // 2, :], Vp)):
                    eng = CPAT[ci % len(CPAT)]
                    ci += 1
                    dst = dstt[0:64, :].rearrange("p (a b) -> p a b", b=16)[:, :, h]
                    vop(eng, CPOP[eng], dst, src,
                        waits=w0 if first[eng] else (), inc=ESEM[eng])
                    first[eng] = False
            SLC[("builds", s)] = [SV.now(), SS.now(), SG.now()]

        def emitC(s):
            builds = SLC[("builds", s)]
            oc_prev = SLC.get(("oc_done", s - 1), [])
            last_div = SLC.get(("ldiv", s - 1))
            qev = SLC[("qev", s)]
            dv_guard[0] = [dv_guard[0], qev[0]]
            dv_guard[1] = [dv_guard[1], qev[1]]
            pendr = None
            for r in range(NR + 1):
                if r < NR:
                    bS = r % 2
                    qkl = None
                    for gi in range(4):
                        g_ = r * 4 + gi
                        col = slice(g_ * 128, (g_ + 1) * 128)
                        qkl = mm(PS[:, bS, gi * 128:(gi + 1) * 128], Kp[0:73, col],
                                 Qp[0:73, col], start=(gi == 0), stop=(gi == 3),
                                 waits=builds + mrows + [exp_guard[bS]] if gi == 0 else (),
                                 inc=SM if gi == 3 else None)
                    trl = None
                    for gi in range(4):
                        g_ = r * 4 + gi
                        col = slice(g_ * 128, (g_ + 1) * 128)
                        ptv = ptv_all[:, (r % 2) * 512 + gi * 64:(r % 2) * 512 + gi * 64 + 64]
                        trl = transp(ptv, Vp[0:64, col], identB[0:64, :],
                                     waits=builds + [tr_guard[r % 2]] if gi == 0 else (),
                                     inc=SM if gi == 3 else None)
                    ex = act(PtT[:, r % 2, :], PS[:, bS, :], AF.Exp,
                             bias=consts[:, 1:2], scale=consts[:, 0:1],
                             waits=[qkl, av_guard[r % 2]], inc=SS)
                    exp_guard[bS] = ex
                if pendr is not None:
                    p = pendr
                    bO = 2 + p % 2
                    avl = None
                    for gi in range(4):
                        avl = mm(PS[:, bO, gi * 128:(gi + 1) * 128], Vg[:, gi, :],
                                 PtT[:, p % 2, gi * 128:(gi + 1) * 128],
                                 start=(gi == 0), stop=(gi == 3),
                                 waits=[tr_guard[p % 2], exp_guard[p % 2],
                                        dv_guard[p % 2]] if gi == 0 else (),
                                 inc=SM if gi == 3 else None)
                    av_guard[p % 2] = avl
                    lnr = act(Rt[64:128, :], PS[64:128, bO, :], AF.Ln,
                              waits=[avl, last_div], inc=SS)
                    exr = act(Rt[64:128, :], Rt[64:128, :], AF.Exp, scale=-1.0,
                              waits=[lnr], inc=SS)
                    last_div = vop("v", "tensor_tensor",
                                   TEMP[0:64, p * 512:(p + 1) * 512],
                                   PS[0:64, bO, :], Rt[64:128, :], ALU.mult,
                                   waits=[exr] + (oc_prev if p == 0 else []),
                                   inc=SV)
                    dv_guard[p % 2] = last_div
                if r < NR:
                    evs = None
                    for gi in range(4):
                        src = ptv_all[:, (r % 2) * 512 + gi * 64:(r % 2) * 512 + gi * 64 + 64]
                        evs = vop("v", "tensor_copy", Vg[:, gi, 0:64], src,
                                  waits=[trl, av_guard[0], av_guard[1]] if gi == 0 else (),
                                  inc=SV if gi == 3 else None)
                    tr_guard[r % 2] = evs
                    pendr = r
            SLC[("b03", s)] = [exp_guard[0], exp_guard[1],
                               dv_guard[0], dv_guard[1]]
            SLC[("ldiv", s)] = last_div
            return last_div

        def emitD(s, last_div):
            w0 = [last_div, SLC.get(("xm_last", s - 1))]
            ci = 5
            first = {"v": True, "s": True, "g": True}
            for h in range(16):
                eng = CPAT[ci % len(CPAT)]
                ci += 1
                src = TEMP[0:64, :].rearrange("p (a b) -> p a b", b=16)[:, :, h]
                dst = A_at[(h % 2) * 64:(h % 2) * 64 + 64, h // 2, :]
                vop(eng, CPOP[eng], dst, src,
                    waits=w0 if first[eng] else (), inc=ESEM[eng])
                first[eng] = False
            SLC[("oc_done", s)] = [SV.now(), SS.now(), SG.now()]

        def emitE(s):
            t0 = s * SL
            tsl = slice(t0, t0 + SL)
            oc_done = SLC[("oc_done", s)]
            s2m_prev = SLC.get(("s2m", s - 1))
            cr0 = vop("g", "memset", A_cr[0:2, :], 1.0, waits=oc_done, inc=SG)
            cr1 = vop("v", "tensor_copy", A_cr[0:1, :], mrlnb[0:1, tsl],
                      waits=[rows_done, cr0], inc=SV)
            rlb = mm(PS[:, 4, B4RL], ones1F[0:1, :], r_rln[:, tsl],
                     start=True, stop=True,
                     waits=[rows_done] + rlb_guard, inc=SM)
            xm = {}
            for fo in range(NCH):
                bk = fo % 2
                mml = None
                for c in range(NCH):
                    mml = mm(PS[:, bk, 0:SL], A_wo[:, c, fo * 128:(fo + 1) * 128],
                             A_at[:, c, :], start=(c == 0),
                             stop=(nocrow and c == NCH - 1),
                             waits=oc_done + [wo_ld, exp_guard[bk]] +
                             ([xm[fo - 2]] if fo >= 2 else []) if c == 0 else (),
                             inc=SM if (nocrow and c == NCH - 1) else None)
                if not nocrow:
                    mml = mm(PS[:, bk, 0:SL], woextb[0:2, fo, :], A_cr[0:2, :],
                             start=False, stop=True, waits=[cr0, cr1], inc=SM)
                t1 = vop("v", "tensor_tensor", A_tmp[:, :], xar[:, fo, tsl],
                         PS[:, 4, B4RL], ALU.mult,
                         waits=[rlb] + ([s2m_prev] if fo == 0 else []), inc=SV)
                xm[fo] = vop("v", "tensor_tensor", A_xm[:, fo, :], A_tmp[:, :],
                             PS[:, bk, 0:SL], ALU.add, waits=[mml, t1], inc=SV)
            exp_guard[0] = [exp_guard[0], xm[6]]
            exp_guard[1] = [exp_guard[1], xm[7]]
            rlb_guard[0] = xm[NCH - 1]
            SLC[("xm", s)] = xm
            SLC[("xm_last", s)] = xm[NCH - 1]

        def emitF(s):
            xm = SLC[("xm", s)]
            s2m = None
            for c in range(NCH):
                sq = act(A_sq[:, c % 2, 0:SL], A_xm[:, c, :], AF.Square,
                         waits=[xm[c], sq_guard[c % 2]], inc=SS)
                s2m = mm(PS[0:1, 7, 0:SL], onesF[:, :], A_sq[:, c % 2, 0:SL],
                         start=(c == 0), stop=(c == NCH - 1),
                         waits=[sq] + (b7_free if c == 0 else []), inc=SM)
                sq_guard[c % 2] = s2m
            r2s = act(r2row[0:1, :], PS[0:1, 7, 0:SL], AF.Ln,
                      bias=consts[0:1, 6:7], scale=consts[0:1, 4:5],
                      waits=[s2m], inc=SS)
            b7_free[0] = r2s
            r2r = act(r2row[0:1, :], r2row[0:1, :], AF.Exp,
                      bias=crow2[0:1, 2:3], scale=crow2[0:1, 3:4],
                      waits=[r2s], inc=SS)
            r2b = mm(PS[:, 4, B4R2], ones1F[0:1, :], r2row[0:1, :],
                     start=True, stop=True, waits=[r2r] + r2guard, inc=SM)
            SLC[("s2m", s)] = s2m
            SLC[("r2b", s)] = r2b

        def emitG(s):
            t0 = s * SL
            tsl = slice(t0, t0 + SL)
            xm = SLC[("xm", s)]
            r2b = SLC[("r2b", s)]
            h2l = None
            tmpl = None
            for c in range(NCH):
                h2l = vop("v", "scalar_tensor_tensor", h2ar[:, c, tsl],
                          A_xm[:, c, :], g2t[:, c:c + 1], PS[:, 4, B4R2],
                          ALU.mult, ALU.mult, waits=[r2b] if c == 0 else (),
                          inc=SV if c == NCH - 1 else None)
            for c in range(NCH):
                tmpl = vop("v", "scalar_tensor_tensor", xar[:, c, tsl],
                           A_xm[:, c, :], consts[:, 3:4], xar[:, c, tsl],
                           ALU.mult, ALU.add, waits=[h2l] if c == 0 else (),
                           inc=SV if c == NCH - 1 else None)
            r2guard[0] = h2l
            return tmpl

        emitA(0)
        emitB(0)
        for s in range(NSL):
            ld = emitC(s)
            emitD(s, ld)
            if s + 1 < NSL:
                emitA(s + 1)
            emitE(s)
            if s + 1 < NSL:
                emitB(s + 1)
            emitF(s)
            prev_tmp = emitG(s)
        bar = barrier()
        dump(f"h2_{call}", h2ar[:, 0, :], bar)
        x1l = None
        we_ld = [None, None]
        first_zs_waits = bar
        for zs in range(NZS):
            z0 = zs * ZSL
            zsl = slice(z0, z0 + ZSL)
            gl = None
            for c in range(NCH):
                gl = mm(PS[0:8, 0, 0:ZSL], wgb[:, c, :], h2ar[:, c, zsl],
                        start=(c == 0), stop=(c == NCH - 1),
                        waits=first_zs_waits + [wg_ld] if c == 0 else (),
                        inc=SM if c == NCH - 1 else None)
            gex = act(B_ge[0:8, :], PS[0:8, 0, 0:ZSL], AF.Exp, bias=bgst[0:8, :],
                      waits=[gl, bg_sh], inc=SS)
            gsum = mm(PS[0:64, 1, 0:ZSL], ones8F[0:8, :], B_ge[0:8, :],
                      start=True, stop=True, waits=[gex], inc=SM)
            glog = act(RtB[0:8, :], PS[0:8, 1, 0:ZSL], AF.Ln,
                       waits=[gsum], inc=SS)
            grc = act(RtB[0:8, :], RtB[0:8, :], AF.Exp, scale=-1.0,
                      waits=[glog], inc=SS)
            gT = vop("v", "tensor_tensor", B_gt[0:8, :], B_ge[0:8, :], RtB[0:8, :],
                     ALU.mult, waits=[grc], inc=SV)
            if zs == 0:
                dump(f"gate_{call}", B_gt[0:8, :], [gT])
            gevs = []
            gbl = [grc, grc]
            for e in range(8):
                bk = e % 2
                gb = mm(PS[:, bk, 0:ZSL], selB[0:8, e, :], B_gt[0:8, :],
                        start=True, stop=True, waits=[gT, gbl[bk], w_consts], inc=SM)
                gv = vop("v", "tensor_copy", B_gall[:, e, :], PS[:, bk, 0:ZSL],
                         waits=[gb], inc=SV)
                gbl[bk] = gv
                gevs.append(gv)
            mm_last = None
            ze_rel = [None, None]
            we_rel = getattr(emit_ffn, "_we_rel", [None, None])
            for e in range(8):
                wb = e % 2
                we_ld[wb] = dma("g", B_we[:, wb, :, :],
                                we_in[e].rearrange("(c p) n -> p c n", p=128),
                                waits=[we_ld[wb], we_rel[wb],
                                       bar[0] if zs == 0 and e < 2 else None],
                                inc=DW)
                zl = None
                for c in range(NCH):
                    zl = vop("v", "tensor_tensor", B_z[:, wb, c, :], h2ar[:, c, zsl],
                             B_gall[:, e, :], ALU.mult,
                             waits=[gevs[e], ze_rel[wb]] if c == 0 else (),
                             inc=SV if c == NCH - 1 else None)
                ze_rel[wb] = None
                for fo in range(NCH):
                    for c in range(NCH):
                        mm_last = mm(PS[:, fo, 0:ZSL],
                                     B_we[:, wb, c, fo * 128:(fo + 1) * 128],
                                     B_z[:, wb, c, :],
                                     start=(e == 0 and c == 0),
                                     stop=(e == 7 and c == NCH - 1),
                                     waits=[zl, we_ld[wb]] + ([gevs[-1]] if e == 0 else [])
                                     if c == 0 else (),
                                     inc=SM if c == NCH - 1 and fo == NCH - 1 else None)
                ze_rel[wb] = mm_last
                we_rel[wb] = mm_last
            emit_ffn._we_rel = we_rel
            x1c = []
            for fo in range(NCH):
                x1l = vop("v", "scalar_tensor_tensor", xar[:, fo, zsl],
                          PS[:, fo, 0:ZSL], consts[:, 3:4], xar[:, fo, zsl],
                          ALU.mult, ALU.add, waits=[mm_last, prev_tmp], inc=SV)
                x1c.append(x1l)
            first_zs_waits = x1c
        dump(f"x1_{call}", xar[:, 0, :], [x1l])
        return x1l

    def emit_attn(x1_done):
        bar0 = barrier()
        r_rln = rowsA[0:1, 0, :]
        rows_done, stats_rel = ln_stats(x1_done, r_rln, None)
        lnp_rel = [stats_rel, stats_rel]
        ln_dma = []
        bcast_rel = [stats_rel, stats_rel]
        for qd in range(4):
            pp = qd % 2
            rb = mm(PS[:, pp, :], ones1F[0:1, :], r_rln[:, qd * 512:(qd + 1) * 512],
                    start=True, stop=True, waits=[rows_done, bcast_rel[pp]], inc=SM)
            lnl = None
            for c in range(NCH):
                lnl = vop("v", "tensor_tensor", AT_lnp[:, pp, c, :],
                          xar[:, c, qd * 512:(qd + 1) * 512], PS[:, pp, :],
                          ALU.mult, waits=[rb, lnp_rel[pp]] if c == 0 else (),
                          inc=SV if c == NCH - 1 else None)
            bcast_rel[pp] = lnl
            d = dma("y", ln2own_d.rearrange("(c p) (a n) -> p a c n", p=128, a=4)[:, qd, :, :],
                    AT_lnp[:, pp, :, :], waits=[lnl], inc=DS)
            lnp_rel[pp] = d
            ln_dma.append(d)
        d2 = dma("y", x_page[:, :, :], xar[:, :, :], waits=[x1_done], inc=DS)
        ag1 = sch.op("g", lambda e: e.collective_compute(
            "AllGather", ALU.bypass,
            replica_groups=[[0, 1], [2, 3], [4, 5], [6, 7]],
            ins=[ln2own_d[:, :].opt()], outs=[ln2all_d[:, :].opt()]),
            ln_dma + [bar0[3]], CC)
        d3a = dma("y", ln2full[:, :, 0:NT],
                  ln2all_d[0:D, :].rearrange("(c p) n -> p c n", p=128),
                  waits=[ag1, d2], inc=DL)
        d3 = dma("y", ln2full[:, :, NT:2 * NT],
                 ln2all_d[D:2 * D, :].rearrange("(c p) n -> p c n", p=128),
                 waits=[ag1, d2], inc=DL)
        d3 = [d3a, d3]
        dump("ln2full", ln2full[:, 0, :], [d3])
        bar_lp = barrier()

        wm_rel = [stats_rel]
        wsl_rel = [None, None]
        b01_rel = [bcast_rel[0], bcast_rel[1]]
        kb_rel = [None, None]
        k_dma = []
        qk_last = None

        def prep_128slab(dst_ws, pp, fo):
            ld = dma("g", dst_ws,
                     wint_in[:, fo * 128:(fo + 1) * 128].rearrange("(c p) n -> p c n", p=128),
                     waits=[wsl_rel[pp]] + (ln_dma if fo < 2 else []), inc=DW)
            csl = None
            for c in range(NCH):
                csl = mm(PS[0:1, 6, 0:128], onesB[:, :], dst_ws[:, c, :],
                         start=(c == 0), stop=(c == NCH - 1),
                         waits=[ld] + wm_rel if c == 0 else (),
                         inc=SM if c == NCH - 1 else None)
            nm = vop("v", "tensor_scalar", AT_pr[0:1, 0:128], PS[0:1, 6, 0:128],
                     consts[0:1, 4:5], None, ALU.mult, waits=[csl], inc=SV)
            nb = mm(PS[:, 6, 0:128], ones1B[0:1, :], AT_pr[0:1, 0:128],
                    start=True, stop=True, waits=[nm], inc=SM)
            nbe = vop("v", "tensor_copy", AT_wm[:, :], PS[:, 6, 0:128],
                      waits=[nb], inc=SV)
            wm_rel[0] = nbe
            wp = None
            for c in range(NCH):
                wp = vop("v", "tensor_tensor", dst_ws[:, c, :], dst_ws[:, c, :],
                         AT_wm[:, :], ALU.subtract, waits=[nbe] if c == 0 else (),
                         inc=SV if c == NCH - 1 else None)
            return wp

        for fo in range(8):
            pp = fo % 2
            wsl = AT_ws[:, pp, :, :]
            wp = prep_128slab(wsl, pp, fo)
            last_e = None
            for tf in range(8):
                bk = tf % 2
                mml = None
                for c in range(NCH):
                    mml = mm(PS[:, bk, :], wsl[:, c, :],
                             ln2full[:, c, tf * 512:(tf + 1) * 512],
                             start=(c == 0), stop=(c == NCH - 1),
                             waits=[wp, d3, b01_rel[bk]] if c == 0 else (),
                             inc=SM if c == NCH - 1 else None)
                if fo < 4:
                    last_e = vop("s", "copy", AT_q[:, fo, tf * 512:(tf + 1) * 512],
                                 PS[:, bk, :], waits=[mml], inc=SS)
                else:
                    last_e = vop("s", "copy", AT_kb[:, pp, tf * 512:(tf + 1) * 512],
                                 PS[:, bk, :],
                                 waits=[mml] + ([kb_rel[pp]] if tf == 0 else []),
                                 inc=SS)
                b01_rel[bk] = last_e
            wsl_rel[pp] = last_e
            qk_last = last_e
            if fo >= 4:
                dkk = dma("y", k_dram[(fo - 4) * 128:(fo - 3) * 128, :], AT_kb[:, pp, :],
                          waits=[last_e], inc=DS)
                kb_rel[pp] = dkk
                k_dma.append(dkk)
        vw_ld = dma("g", VW[:, :, :],
                    wint_in[:, 1024:1536].rearrange("(c p) n -> p c n", p=128),
                    waits=[d3, k_dma[2], kb_rel[0]], inc=DW)
        csl = None
        for c in range(NCH):
            csl = mm(PS[0:1, 6, 0:512], onesB[:, :], VW[:, c, :],
                     start=(c == 0), stop=(c == NCH - 1),
                     waits=[vw_ld] + wm_rel if c == 0 else (),
                     inc=SM if c == NCH - 1 else None)
        nm = vop("v", "tensor_scalar", AT_pr[0:1, 0:512], PS[0:1, 6, 0:512],
                 consts[0:1, 4:5], None, ALU.mult, waits=[csl], inc=SV)
        nbe = None
        for half in range(2):
            nb = mm(PS[:, 6, 0:256], ones1B[0:1, :],
                    AT_pr[0:1, half * 256:half * 256 + 256],
                    start=True, stop=True, waits=[nm, nbe], inc=SM)
            nbe = vop("v", "tensor_copy", nmb[:, half * 256:half * 256 + 256],
                      PS[:, 6, 0:256], waits=[nb], inc=SV)
        vwp = None
        for c in range(NCH):
            vwp = vop("v", "tensor_tensor", VW[:, c, :], VW[:, c, :], nmb[:, 0:512],
                      ALU.subtract, waits=[nbe] if c == 0 else (),
                      inc=SV if c == NCH - 1 else None)
        vones = ms(AT_vb[:, :, :, :, :], 1.0, waits=ln_dma, inc=SG)
        v_dma = []
        b23_rel = [None, None]
        vb_rel = [None, None]
        for tc in range(32):
            pb = (tc // 2) % 2
            sub = tc % 2
            bk = 2 + tc % 2
            mml = None
            for c in range(NCH):
                mml = mm(PS[:, bk, :], ln2full[:, c, tc * 128:(tc + 1) * 128],
                         VW[:, c, :], start=(c == 0), stop=(c == NCH - 1),
                         waits=[vwp, d3, b23_rel[tc % 2]] + ([vones] if tc < 2 else [])
                         if c == 0 else (), inc=SM if c == NCH - 1 else None)
            evv = vop("v", "tensor_copy", AT_vb[:, pb, sub, :, 0:64],
                      PS[:, bk, :].rearrange("p (a b) -> p a b", a=8),
                      waits=[mml] + ([vb_rel[pb]] if sub == 0 else []), inc=SV)
            b23_rel[tc % 2] = evv
            vd = dma("y",
                     v_dram.rearrange("h (a p) f -> p a h f", p=128)[:, tc, :, :],
                     AT_vb[:, pb, sub, :, :], waits=[evv], inc=DS)
            if sub == 1:
                vb_rel[pb] = vd
            v_dma.append(vd)
        x_back = dma("y", xar[:, :, :], x_page[:, :, :],
                     waits=[qk_last, vwp, v_dma[-1], SM.now()], inc=DL)

        dump("atq", AT_q[:, 0, :], [qk_last, SM.now(), SS.now()])
        bar1 = barrier()
        ks_rel = [None, None]
        vs_rel = [None, None]
        exp_g = [None, None]
        av_g = [None, None]
        psO_rel = [None, None]
        bpt_rel = [None, None]
        rta_rel = [None, None]
        ah_rel = {}
        ao_dma = []
        for hp in range(4):
            kss = dma("s", KS[:, :], k_dram[hp * 128:(hp + 1) * 128, :],
                      waits=[k_dma[hp], ks_rel[0]] + (bar1 if hp == 0 else []),
                      inc=DK)
            if hp == 0:
                ksd = dump("ks0", KS[0:64, :], [kss])
                if ksd is not None:
                    ks_rel[0] = [ks_rel[0], ksd] if ks_rel[0] else [ksd]
            vsd = []
            for h01 in range(2):
                vsd.append(dma("s", AT_vs[:, h01, :, :],
                               v_dram[2 * hp + h01, :, :].rearrange("(a p) f -> p a f", p=128),
                               waits=[v_dma[-1], vs_rel[h01]] + (bar1 if hp == 0 else []),
                               inc=DK))
            hp_av = None
            for qc in range(8):
                nk = (qc + 1) * 4
                avv = [None, None]
                pend = None
                for kc in range(nk + 1):
                    mask_w = None
                    if kc < nk:
                        bS = (kc % 2) * 2
                        qkk = None
                        for h01 in range(2):
                            qkk = mm(PS[:, bS + h01, :],
                                     KS[h01 * 64:h01 * 64 + 64,
                                        kc * 128:(kc + 1) * 128],
                                     AT_q[h01 * 64:h01 * 64 + 64, hp,
                                          qc * 512:(qc + 1) * 512],
                                     start=True, stop=True,
                                     waits=[kss, exp_g[kc % 2]] if h01 == 0 else (),
                                     inc=SM if h01 == 1 else None,
                                     tp=(64, 0) if h01 == 1 else None)
                        ex = act(BPt[:, kc % 2, :],
                                 PS[:, bS:bS + 2, :].rearrange("p a b -> p (a b)"),
                                 AF.Exp, bias=consts[:, 2:3], scale=consts[:, 0:1],
                                 waits=[qkk, bpt_rel[kc % 2]], inc=SS)
                        exp_g[kc % 2] = ex
                        mask_w = [ex]
                        if kc >= qc * 4:
                            afl = None
                            for h01 in range(2):
                                afl = vop("g", "affine_select",
                                          BPt[:, kc % 2, h01 * 512:(h01 + 1) * 512],
                                          BPt[:, kc % 2, h01 * 512:(h01 + 1) * 512],
                                          pattern=[[1, 512]], compare_op=ALU.is_ge,
                                          fill=0.0, base=qc * 512 - kc * 128,
                                          channel_multiplier=-1,
                                          waits=[ex] if h01 == 0 else (),
                                          inc=SG if h01 == 1 else None)
                            mask_w = [afl]
                    if pend is not None:
                        pkc, pmask = pend
                        for h01 in range(2):
                            avv[h01] = mm(PS[:, 4 + h01, :],
                                          AT_vs[:, h01, pkc, :],
                                          BPt[:, pkc % 2, h01 * 512:(h01 + 1) * 512],
                                          start=(pkc == 0), stop=(pkc == nk - 1),
                                          waits=pmask + vsd +
                                          ([psO_rel[h01]] if pkc == 0 else []),
                                          inc=SM if h01 == 1 or pkc == nk - 1 else None)
                        bpt_rel[pkc % 2] = avv[1]
                    if kc < nk:
                        pend = (kc, mask_w)
                hp_av = avv[1]
                for h01 in range(2):
                    rrow = RtA[64:128, :] if h01 == 0 else RtA2[64:128, :]
                    lna = act(rrow, PS[64:128, 4 + h01, :], AF.Ln,
                              waits=[avv[h01], rta_rel[h01]], inc=SS)
                    exa = act(rrow, rrow, AF.Exp, scale=-1.0,
                              waits=[lna], inc=SS)
                    dv = vop("v", "tensor_tensor", AT_ah[0:64, h01, :],
                             PS[0:64, 4 + h01, :], rrow, ALU.mult,
                             waits=[exa, ah_rel.get(h01)], inc=SV)
                    psO_rel[h01] = dv
                    rta_rel[h01] = dv
                    half = qc // 4
                    tl0 = qc * 512 - half * NT
                    add = dma("y",
                              attnown_d[half, (2 * hp + h01) * 64:(2 * hp + h01 + 1) * 64,
                                        tl0:tl0 + 512],
                              AT_ah[0:64, h01, :], waits=[dv], inc=DS)
                    ah_rel[h01] = add
                    ao_dma.append(add)
            ks_rel[0] = hp_av
            vs_rel[0] = hp_av
            vs_rel[1] = hp_av
        a2a = sch.op("g", lambda e: e.collective_compute(
            "AllGather", ALU.bypass,
            replica_groups=[[0, 1], [2, 3], [4, 5], [6, 7]],
            ins=[attnown_d.rearrange("a b c -> (a b) c").opt()],
            outs=[attnall_d.rearrange("r a b c -> (r a b) c").opt()]),
            ao_dma, CC)
        dump("attnown", attnown_d[0, 0:64, :], [a2a])
        dump("kdram", k_dram[0:64, 0:2048], [a2a])
        dump("vdram", v_dram[0, 0:2048, :].rearrange("(a b) f -> a (b f)", b=16), [a2a])
        dump("attnall", attnall_d[0, 0, 0:64, :], [a2a])

        bar2 = barrier()
        wout_ld = dma("g", AT_wout[:, :, :],
                      wout_in.rearrange("(c p) n -> p c n", p=128),
                      waits=bar2, inc=DW)
        slab4 = at_slab[:, :, :].rearrange("p (r c) n -> p r c n", r=2)

        def slab_fn(e):
            reg = e.alloc_register("rreg")
            e.load(reg, ridxt[0:1, 0:1])
            for mh in range(2):
                ctx = e.If_eq(reg, 0) if mh == 0 else e.Else()
                with ctx:
                    for r in range(2):
                        e.dma_start(out=slab4[:, r, :, :],
                                    in_=attnall_d[r, mh, :, :].rearrange("(c p) n -> p c n", p=128)
                                    ).then_inc(DL.h, 16)
            return None
        sch.op("y", slab_fn, [a2a, ridx_ld] + bar2, None)
        DL.bump(32)
        sl_ld = DL.now()
        dump("slab", at_slab[:, :, :].rearrange("p a b -> p (a b)"), [sl_ld])
        dump("wout2", AT_wout[:, :, :].rearrange("p a b -> p (a b)"), [wout_ld])
        dump("xrest", xar[:, 0, :], [x_back])
        x2l = None
        b_rel = [bar2[1], bar2[1]]
        for fo in range(NCH):
            for tf in range(4):
                bk = (fo * 4 + tf) % 2
                mml = None
                for c in range(NCH):
                    mml = mm(PS[:, bk, :], AT_wout[:, c, fo * 128:(fo + 1) * 128],
                             at_slab[:, c, tf * 512:(tf + 1) * 512],
                             start=(c == 0), stop=(c == NCH - 1),
                             waits=[wout_ld, sl_ld, x_back, b_rel[bk]] if c == 0 else (),
                             inc=SM if c == NCH - 1 else None)
                x2l = vop("v", "tensor_tensor", xar[:, fo, tf * 512:(tf + 1) * 512],
                          xar[:, fo, tf * 512:(tf + 1) * 512], PS[:, bk, :],
                          ALU.add, waits=[mml], inc=SV)
                b_rel[bk] = x2l
        dump("x2", xar[:, 0, :], [x2l])
        dump("wout3", AT_wout[:, :, :].rearrange("p a b -> p (a b)"), [x2l])
        dump("slab3", at_slab[:, :, :].rearrange("p a b -> p (a b)"), [x2l])
        return x2l

    x1 = emit_ffn(0, x_ld)
    last = x1
    if parts >= 2:
        last = emit_attn(x1)
    if parts >= 3:
        barx = barrier()
        last = emit_ffn(1, barx[1])
    od = dma("y", out_d.rearrange("(c p) n -> p c n", p=128), xar[:, :, :],
             waits=[last], inc=DS)
    sch.op("y", lambda e: e.nop(), [od, DS.now()], None)

    with nc.Block() as block:
        @block.tensor
        def _(e):
            sch.replay("t", e)

        @block.scalar
        def _(e):
            sch.replay("s", e)

        @block.vector
        def _(e):
            sch.replay("v", e)

        @block.gpsimd
        def _(e):
            sch.replay("g", e)

        @block.sync
        def _(e):
            sch.replay("y", e)

    return nc


_NCACHE = {}


def _prep_inputs(inputs):
    f32 = np.float32
    Wq = np.asarray(inputs["L0_ffn_Wq"], f32)
    Wdown = np.asarray(inputs["L0_ffn_Wdown"], f32)
    Wup = np.asarray(inputs["L0_ffn_Wup"], f32)
    Wg = np.asarray(inputs["L0_ffn_Wg"], f32)
    Wo = np.asarray(inputs["L0_ffn_Wo"], f32)
    We = np.asarray(inputs["L0_ffn_We"], f32)
    Win = np.asarray(inputs["L0_attn_in_proj_weight"], f32)
    Wout = np.asarray(inputs["L0_attn_out_proj_weight"], f32)
    x = np.asarray(inputs["L0_attn_query"], f32)

    wqd = np.ascontiguousarray(np.concatenate([Wq, Wdown], axis=1))
    idx = np.arange(2048).reshape(16, 2, 64)
    perm = np.concatenate([idx[:, 0, :].reshape(-1), idx[:, 1, :].reshape(-1)])
    wup = np.ascontiguousarray(Wup[:, perm])
    bup = np.asarray(inputs["L0_ffn_bup"], f32)[perm]
    wintT = np.ascontiguousarray(Win.T)
    woutT = np.ascontiguousarray(Wout.T)

    mr = np.zeros((9, 16 * 128), np.float32)
    sub = np.arange(16 * 128)
    blk = (sub // 16) % 8
    for j in range(8):
        mr[j, blk == j] = SQB
    mr[8, :] = SQB
    mrq = mr.copy()
    mrq[8, :] = -SQB
    com = {
        "mrq": mrq,
        "mrk": mr,
        "wqd": wqd,
        "wup": wup,
        "wg": np.ascontiguousarray(Wg),
        "wo": np.ascontiguousarray(Wo),
        "bo": np.ascontiguousarray(np.asarray(inputs["L0_ffn_bo"], f32).reshape(1, NCH, 128)),
        "we": np.ascontiguousarray(We),
        "wout": woutT,
        "g1c": np.ascontiguousarray(np.asarray(inputs["L0_ffn_gamma1"], f32).reshape(NCH, 128).T),
        "g2c": np.ascontiguousarray(np.asarray(inputs["L0_ffn_gamma2"], f32).reshape(NCH, 128).T),
        "bqc": np.ascontiguousarray(np.asarray(inputs["L0_ffn_bq"], f32).reshape(NCH, 128).T),
        "bdc": np.ascontiguousarray(np.asarray(inputs["L0_ffn_bdown"], f32).reshape(64, 1)),
        "bupc": np.ascontiguousarray(bup.reshape(16, 128).T),
        "bgc": np.ascontiguousarray(np.asarray(inputs["L0_ffn_bg"], f32).reshape(8, 1)),
    }
    in_maps = []
    for c in range(8):
        b, half = c // 2, c % 2
        m = dict(com)
        m["xt"] = np.ascontiguousarray(x[b, half * NT:(half + 1) * NT, :].T)
        cols = np.concatenate([np.arange(half * 512, half * 512 + 512),
                               1024 + np.arange(half * 512, half * 512 + 512),
                               2048 + np.arange(half * 512, half * 512 + 512)])
        m["wint"] = np.ascontiguousarray(wintT[:, cols])
        m["ridx"] = np.array([[half]], np.int32)
        in_maps.append(m)
    return in_maps


def kernel(**inputs):
    in_maps = _prep_inputs(inputs)
    nc = build()
    res = run_bass_kernel_spmd(nc, in_maps, core_ids=list(range(8)))
    out = np.empty((4, T, D), np.float32)
    for c in range(8):
        b, half = c // 2, c % 2
        out[b, half * NT:(half + 1) * NT, :] = res.results[c]["out"].T
    return out
